# version 21
# baseline (speedup 1.0000x reference)
import contextlib

import numpy as np
import ml_dtypes

import concourse.bass as bass
import concourse.mybir as mybir
from concourse.bass_utils import run_bass_kernel_spmd

F32 = mybir.dt.float32
BF16 = mybir.dt.bfloat16
I32 = mybir.dt.int32
I8 = mybir.dt.int8
AF = mybir.ActivationFunctionType
ALU = mybir.AluOpType
AX = mybir.AxisListType

D = 1024
E = 2048
HD = 128
GRP = 256
DH = 4096
NMAIN = 4096
NPRE = 4096
FCH = 256
SL = FCH + 2
INV_SCALE = 1.0 / float(np.sqrt(2048.0))
EPS = 1e-5
WN = D * DH

ENGS = ("sync", "tensor", "scalar", "vector", "gpsimd")


def bcast0(t2d, col0, ncols, reps):
    v = t2d[:, col0:col0 + ncols].rearrange("p (c u) -> p c u", u=1)
    ap = [list(p) for p in v.ap]
    ap[-1] = [0, reps]
    return bass.AP(tensor=v.tensor, offset=v.offset, ap=ap)


class Sched:

    def __init__(self, nc, ctx):
        self.nc = nc
        self.ctx = ctx
        self.q = {e: [] for e in ENGS}
        self.sem = {}
        self.cnt = {}
        self.dma_streams = set()
        for name in ("pe", "ac", "dv", "bar"):
            self._sem(name)
        self.nbar = 0

    def _sem(self, name):
        if name not in self.sem:
            self.sem[name] = self.ctx.enter_context(self.nc.semaphore("sem_" + name))
            self.cnt[name] = 0
        return self.sem[name]

    ENG_SEM = {"tensor": "pe", "scalar": "ac", "vector": "dv"}

    def op(self, eng, fn, wait=(), inc=None, inc_by=1, sw=None):
        if sw is None and eng in ("vector", "scalar"):
            sw = True
        if sw:
            es = self.ENG_SEM[eng]
            wait = tuple(wait) + ((es, self.cnt[es]),)
        w = tuple((self._sem(s), val) for s, val in wait if val > 0)
        if inc is not None:
            self._sem(inc)
            self.cnt[inc] += inc_by
        self.q[eng].append((fn, w, (self.sem[inc], inc_by) if inc else None))
        return self.cnt[inc] if inc else None

    def dma(self, eng, out, in_, stream, wait=(), serial=False):
        self._sem(stream)
        self.dma_streams.add(stream)
        if serial:
            wait = tuple(wait) + ((stream, self.cnt[stream]),)
        return self.op(eng, lambda e, o=out, i=in_: e.dma_start(out=o, in_=i),
                       wait=wait, inc=stream, inc_by=16)

    def n(self, name):
        self._sem(name)
        return self.cnt[name]

    def dbg_fence(self):
        w = (("dbg", self.cnt.get("dbg", 0)),)
        for eng in ("tensor", "scalar", "vector"):
            self.op(eng, lambda e: None, wait=w)

    def barrier(self):
        waits = tuple((s, self.cnt[s]) for s in sorted(self.dma_streams))
        self.op("sync", lambda e: None, wait=waits)
        for eng in ENGS:
            self.op(eng, lambda e: None, inc="bar")
        self.nbar += 5
        for eng in ENGS:
            self.op(eng, lambda e: None, wait=(("bar", self.nbar),))

    def emit(self, block):
        for eng in ENGS:
            items = self.q[eng]

            def body(handle, items=items):
                for fn, waits, inc in items:
                    ws = list(waits)
                    while len(ws) > 2:
                        handle.wait_ge(*ws[0])
                        handle.wait_ge(*ws[1])
                        handle.nop()
                        ws = ws[2:]
                    for sem, val in ws:
                        handle.wait_ge(sem, val)
                    instr = fn(handle)
                    if inc is not None:
                        sem, by = inc
                        if instr is None:
                            handle.sem_inc(sem, by)
                        else:
                            instr.then_inc(sem, by)
                    elif instr is None and ws:
                        handle.nop()

            getattr(block, eng)(body)


def build_nc(n_pre_groups=16, n_main_groups=16, debug=False):
    NGP = n_pre_groups
    NG = n_main_groups
    NTOK = NG * GRP
    NT = NTOK * 0 + NG * 2
    NFC = NTOK // FCH
    nc = bass.Bass("TRN2", target_bir_lowering=False)

    dtn = lambda n, s, d, k: nc.dram_tensor(n, s, d, kind=k).ap()
    xm = dtn("xm", [NTOK, D], F32, "ExternalInput")
    xp = dtn("xp", [NGP * GRP, D], F32, "ExternalInput")
    xhT = dtn("xhT", [128, 16], F32, "ExternalInput")
    xh = dtn("xh", [2, D], F32, "ExternalInput")
    wv = dtn("wv", [D, E], F32, "ExternalInput")
    wg = dtn("wg", [D, E], F32, "ExternalInput")
    wi = dtn("wi", [D, HD], F32, "ExternalInput")
    wo = dtn("wo", [E, D], F32, "ExternalInput")
    bvr = dtn("bvr", [1, E], F32, "ExternalInput")
    bgr = dtn("bgr", [1, E], F32, "ExternalInput")
    bgc = dtn("bgc", [128, 16], F32, "ExternalInput")
    binc = dtn("binc", [128, 1], F32, "ExternalInput")
    boutr = dtn("boutr", [1, D], F32, "ExternalInput")
    affc = dtn("affc", [128, 8], F32, "ExternalInput")
    w1t = dtn("w1t", [D, DH], F32, "ExternalInput")
    w2t = dtn("w2t", [DH, D], F32, "ExternalInput")
    cwc = dtn("cwc", [128, 128], F32, "ExternalInput")
    identf = dtn("identf", [128, 128], F32, "ExternalInput")
    identb = dtn("identb", [128, 128], BF16, "ExternalInput")
    onesb = dtn("onesb", [1, 128], BF16, "ExternalInput")
    onesf = dtn("onesf", [1, 128], F32, "ExternalInput")
    onesm = dtn("onesm", [128, 256], F32, "ExternalInput")
    mask01 = dtn("mask01", [128, 2 * GRP], BF16, "ExternalInput")
    pmask = dtn("pmask", [128, 1], F32, "ExternalInput")
    hmv = dtn("hmv", [2, 1], F32, "ExternalInput")

    out = dtn("out", [NTOK, D], F32, "ExternalOutput")

    w1q_d = dtn("w1q_d", [8, 128, DH], BF16, "Internal")
    w2q_d = dtn("w2q_d", [32, 128, D], BF16, "Internal")
    x1_d = dtn("x1_d", [NT, 128, D], F32, "Internal")
    xqdT_d = dtn("xqdT_d", [128, 8, NTOK], BF16, "Internal")
    hal_d = dtn("hal_d", [2, DH], BF16, "Internal")

    dbg = {}
    if debug:
        dbg["S"] = dtn("dbg_S", [128, E], F32, "ExternalOutput")
        dbg["x1"] = dtn("dbg_x1", [NT, 128, D], F32, "ExternalOutput")
        dbg["xqdT"] = dtn("dbg_xqdT", [128, 8, NTOK], BF16, "ExternalOutput")
        dbg["halxq"] = dtn("dbg_halxq", [128, 16], BF16, "ExternalOutput")
        dbg["at"] = dtn("dbg_at", [128, 2 * GRP], BF16, "ExternalOutput")
        dbg["stat"] = dtn("dbg_stat", [128, 32], F32, "ExternalOutput")
        dbg["h"] = dtn("dbg_h", [128, 32 * SL], BF16, "ExternalOutput")
        dbg["hs"] = dtn("dbg_hs", [128, 32 * SL], BF16, "ExternalOutput")
        dbg["xq2"] = dtn("dbg_xq2", [128, 32 * FCH], BF16, "ExternalOutput")
        dbg["fst"] = dtn("dbg_fst", [128, 32], F32, "ExternalOutput")
        dbg["y2"] = dtn("dbg_y2", [128, D], F32, "ExternalOutput")
        dbg["fst2"] = dtn("dbg_fst2", [128, 32], F32, "ExternalOutput")
        dbg["wsc"] = dtn("dbg_wsc", [128, 4], F32, "ExternalOutput")
        dbg["pre"] = dtn("dbg_pre", [128, 32], F32, "ExternalOutput")
        dbg["pw0"] = dtn("dbg_pw0", [128, DH], F32, "ExternalOutput")
        dbg["y"] = dtn("dbg_y", [128, D], F32, "ExternalOutput")
        dbg["o"] = dtn("dbg_o", [128, 2 * GRP], BF16, "ExternalOutput")
        dbg["vt"] = dtn("dbg_vt", [128, 2 * E], BF16, "ExternalOutput")
        dbg["gate"] = dtn("dbg_gate", [128, 2 * GRP], BF16, "ExternalOutput")

    with contextlib.ExitStack() as root:
        S = Sched(nc, root)
        sb = lambda n, s, d: root.enter_context(nc.sbuf_tensor(n, s, d))

        c_idf = sb("c_idf", [128, 128], F32)
        c_idb = sb("c_idb", [128, 128], BF16)
        c_ones = sb("c_ones", [1, 128], BF16)
        c_onesf = sb("c_onesf", [1, 128], F32)
        c_onesm = sb("c_onesm", [128, 256], F32)
        c_hm = sb("c_hm", [2, 1], F32)
        stat = sb("stat", [128, 32], F32)
        hal_xqdT = sb("hal_xqdT", [128, 16], BF16)
        p_cw = sb("p_cw", [128, 128], F32)
        wsc = sb("wsc", [128, 4], F32)

        B = [root.enter_context(nc.psum_tensor(f"bk{i}", [128, 512], F32))
             for i in range(7)]
        B.append(root.enter_context(nc.psum_tensor("bk7", [128, 1024], BF16)))

        pe, ac, dv = "pe", "ac", "dv"

        env = {}

        with contextlib.ExitStack() as attn_scope:
            asb = lambda n, s, d: attn_scope.enter_context(nc.sbuf_tensor(n, s, d))
            c_m01 = asb("c_m01", [128, 2 * GRP], BF16)
            c_pm = asb("c_pm", [128, 1], F32)
            p_aff = asb("p_aff", [128, 8], F32)

            for dst, srcx in ((c_idf, identf), (c_idb, identb), (c_ones, onesb),
                              (c_onesf, onesf), (c_onesm, onesm), (c_hm, hmv),
                              (c_m01, mask01), (c_pm, pmask), (p_aff, affc),
                              (p_cw, cwc)):
                S.dma("sync", dst[:, :], srcx[:, :], "lc")
            n_const = S.n("lc")

            S.op("vector", lambda e: e.tensor_scalar(
                out=p_aff[:, 0:2], in0=p_aff[:, 0:2], scalar1=INV_SCALE,
                scalar2=None, op0=ALU.mult), wait=(("lc", n_const),), inc=dv)
            if debug:
                S.op("vector", lambda e: e.memset(stat[:, :], 0.0), inc=dv)

            p_bv = asb("p_bv", [1, E], BF16)
            p_bg = asb("p_bg", [1, E], BF16)
            p_bgc = asb("p_bgc", [128, 16], F32)
            p_bout = asb("p_bout", [1, D], BF16)
            w_v = asb("w_v", [128, 8 * E], BF16)
            w_i = asb("w_i", [128, 8 * HD], BF16)
            hal_xT = asb("hal_xT", [128, 16], BF16)
            xbuf = asb("xbuf", [128, 4 * D], F32)
            xT = asb("xT", [128, 2 * 8 * GRP], BF16)
            v_tm = asb("v_tm", [128, 4 * E], BF16)
            x_fm = asb("x_fm", [128, GRP], BF16)
            qq_f = asb("qq_f", [128, GRP], BF16)
            qk_f = asb("qk_f", [128, GRP], BF16)
            lq_f = asb("lq_f", [128, 2 * GRP], BF16)
            lk_f = asb("lk_f", [128, GRP], BF16)
            lk_tm = asb("lk_tm", [128, 2 * 128], BF16)
            S_m = asb("S_m", [128, E], F32)
            S_bf = asb("S_bf", [128, E], BF16)
            Sh_bf = asb("Sh_bf", [128, E], BF16)
            binc_col = asb("binc_col", [128, 1], F32)
            S.dma("sync", p_bgc[:, :], bgc[:, :], "lc", serial=True)
            S.dma("sync", binc_col[:, :], binc[:, :], "lc", serial=True)
            for k in range(8):
                S.dma("gpsimd", w_v[:, k * E:(k + 1) * E], wv[k * 128:(k + 1) * 128, :], "gw")
            for k in range(8):
                S.dma("gpsimd", w_i[:, k * HD:(k + 1) * HD], wi[k * 128:(k + 1) * 128, :], "gw")
            for dstt, srcx in ((p_bv, bvr), (p_bg, bgr), (p_bout, boutr)):
                S.dma("gpsimd", dstt[:, :], srcx[:, :], "gw")
            S.dma("gpsimd", hal_xT[:, :], xhT[:, :], "gw")
            n_wload_pre = S.n("gw")

            pre_scope = contextlib.ExitStack()
            qsb = lambda n, s, d: pre_scope.enter_context(nc.sbuf_tensor(n, s, d))
            pw = [qsb(f"pw{i}", [128, DH], F32) for i in range(4)]
            pi = qsb("pi", [128, DH // 2], I32)
            pb = qsb("pb", [128, DH // 2], BF16)
            pslices = []
            prest = {"dv": 0}

            def ps_load(wsrc, cs, dst_of, wait_dv):
                def f():
                    wl = ((dv, S.n(dv)),) if wait_dv else ()
                    for i, c in enumerate(cs):
                        t, sl = dst_of(c)
                        S.dma("gpsimd", pw[t][:, sl], wsrc[c * 128:(c + 1) * 128, :],
                              "lq", wait=wl if i == 0 else ())
                return f

            def ps_sums(scol0):
                def f():
                    nld = S.n("lq")
                    for t in range(4):
                        S.op("vector", lambda e, t=t, scol0=scol0: e.reduce_sum(
                            stat[:, scol0 + t:scol0 + t + 1], pw[t][:, :], axis=AX.X,
                            apply_absolute_value=True),
                            wait=(("lq", nld), (ac, S.n(ac))) if t == 0 else (),
                            inc=dv, sw=False)
                return f

            def ps_scale(col):
                def f():
                    ndv = S.n(dv)
                    S.op("tensor", lambda e, col=col: e.matmul(
                        B[1][:, 0:8], c_onesm[:, col * 64:col * 64 + 128],
                        stat[:, 16:24], start=True, stop=True),
                        wait=((dv, ndv), ("lc", n_const), (ac, S.n(ac))), inc=pe)
                    npe = S.n(pe)
                    S.op("scalar", lambda e: e.activation(
                        out=stat[:, 16:24], in_=B[1][:, 0:8], func=AF.Copy),
                        wait=((pe, npe),), inc=ac)
                    nac0 = S.n(ac)
                    S.op("vector", lambda e: e.reduce_sum(
                        stat[:, 25:26], stat[:, 16:24], axis=AX.X),
                        wait=((ac, nac0),), inc=dv, sw=True)
                    S.op("vector", lambda e: e.tensor_scalar(
                        out=stat[:, 26:27], in0=stat[:, 25:26],
                        scalar1=1.0 / WN, scalar2=EPS, op0=ALU.mult, op1=ALU.max),
                        inc=dv, sw=True)
                    S.op("vector", lambda e: e.reciprocal(
                        stat[:, 27:28], stat[:, 26:27]), inc=dv, sw=True)
                    S.op("vector", lambda e, col=col: e.tensor_copy(
                        wsc[:, col:col + 1], stat[:, 27:28]), inc=dv, sw=True)
                    S.op("vector", lambda e, col=col: e.tensor_copy(
                        wsc[:, col + 1:col + 2], stat[:, 26:27]), inc=dv, sw=True)
                return f

            def ps_quant(wdst, cs, src_of, col, qn):
                def f():
                    nlq = S.n("lq")
                    first = True
                    for c in cs:
                        t, off, dsl = src_of(c)
                        for h in range((off[1] - off[0]) // qn):
                            a = off[0] + h * qn
                            n_stp = S.n("sw")
                            wl = [("sw", n_stp)]
                            if first:
                                wl.append(("lq", nlq))
                                first = False
                            S.op("vector", lambda e, t=t, a=a, col=col: e.tensor_scalar(
                                out=pi[:, 0:qn], in0=pw[t][:, a:a + qn],
                                scalar1=wsc[:, col:col + 1], scalar2=None,
                                op0=ALU.mult),
                                wait=wl, inc=dv)
                            S.op("vector", lambda e: e.tensor_scalar(
                                out=pb[:, 0:qn], in0=pi[:, 0:qn],
                                scalar1=-1.0, scalar2=1.0, op0=ALU.max, op1=ALU.min),
                                inc=dv)
                            ndv = S.n(dv)
                            S.dma("sync", wdst[c, :, dsl[0] + h * qn: dsl[0] + (h + 1) * qn],
                                  pb[:, 0:qn], "sw", wait=((dv, ndv),))
                    prest["dv"] = S.n(dv)
                return f

            QN = DH // 2
            w1_dst = lambda c: (c % 4, slice(0, DH))
            w1_src = lambda c: (c % 4, (0, DH), (0, DH))
            w2_dst = lambda c: ((c % 16) // 4, slice((c % 4) * D, (c % 4 + 1) * D))
            w2_src = lambda c: ((c % 16) // 4, ((c % 4) * D, (c % 4 + 1) * D), (0, D))
            pslices.append(ps_load(w1t, range(0, 4), w1_dst, False))
            pslices.append(lambda f1=ps_sums(16), f2=ps_load(w1t, range(4, 8), w1_dst, True): (f1(), f2()))
            pslices.append(lambda f1=ps_sums(20), f2=ps_scale(0): (f1(), f2()))
            pslices.append(ps_quant(w1q_d, range(4, 8), w1_src, 0, QN))
            pslices.append(ps_load(w1t, range(0, 4), w1_dst, True))
            pslices.append(ps_quant(w1q_d, range(0, 4), w1_src, 0, QN))
            pslices.append(ps_load(w2t, range(0, 16), w2_dst, True))
            pslices.append(ps_sums(16))
            pslices.append(ps_load(w2t, range(16, 32), w2_dst, True))
            pslices.append(lambda f1=ps_sums(20), f2=ps_scale(2): (f1(), f2()))
            pslices.append(ps_quant(w2q_d, range(16, 24), w2_src, 2, D))
            pslices.append(ps_quant(w2q_d, range(24, 32), w2_src, 2, D))
            pslices.append(ps_load(w2t, range(0, 16), w2_dst, True))
            pslices.append(ps_quant(w2q_d, range(0, 8), w2_src, 2, D))
            pslices.append(lambda f1=ps_quant(w2q_d, range(8, 16), w2_src, 2, D),
                           f2=(lambda: S.op("vector", lambda e: e.tensor_scalar(
                               out=p_cw[:, 0:96], in0=p_cw[:, 0:96],
                               scalar1=wsc[:, 1:2], scalar2=None, op0=ALU.mult),
                               wait=((ac, S.n(ac)),), inc=dv)): (f1(), f2()))

            S.op("vector", lambda e: e.memset(S_m[:, :], 0.0), inc=dv)


            st8 = {}

            def attn_ln1_a(j):
                gi_, ndv_ = st8[f"lnp{j}"]
                sc = j * 8
                xqb = xqd_t if j == 0 else xqd_t2
                S.op("scalar", lambda e, sc=sc, j=j, xqb=xqb: e.activation(
                    out=xqb[:, :], in_=x1_t[:, j * D:(j + 1) * D], func=AF.Square,
                    accum_out=stat[:, sc + 2:sc + 3]),
                    wait=((dv, max(ndv_, st8.get(f"xqd_last{j}", 0))),
                          (pe, st8.get(f"xqdT_pe{j}", 0))), inc=ac)
                nac = S.n(ac)
                S.op("vector", lambda e, sc=sc: e.tensor_add(
                    stat[:, sc + 3:sc + 4], stat[:, sc:sc + 1],
                    stat[:, sc + 1:sc + 2]), wait=((ac, nac),), inc=dv)
                S.op("vector", lambda e, sc=sc: e.tensor_scalar(
                    out=stat[:, sc + 3:sc + 4], in0=stat[:, sc + 3:sc + 4],
                    scalar1=1.0 / D, scalar2=None, op0=ALU.mult), inc=dv)
                S.op("vector", lambda e, sc=sc: e.tensor_scalar(
                    out=stat[:, sc + 4:sc + 5], in0=stat[:, sc + 2:sc + 3],
                    scalar1=1.0 / D, scalar2=None, op0=ALU.mult), inc=dv)
                S.op("vector", lambda e, sc=sc: e.tensor_mul(
                    stat[:, sc + 5:sc + 6], stat[:, sc + 3:sc + 4],
                    stat[:, sc + 3:sc + 4]), inc=dv)
                S.op("vector", lambda e, sc=sc: e.tensor_sub(
                    stat[:, sc + 5:sc + 6], stat[:, sc + 4:sc + 5],
                    stat[:, sc + 5:sc + 6]), inc=dv)
                S.op("vector", lambda e, sc=sc: e.tensor_scalar(
                    out=stat[:, sc + 5:sc + 6], in0=stat[:, sc + 5:sc + 6],
                    scalar1=EPS, scalar2=None, op0=ALU.add), inc=dv)
                ndv = S.n(dv)
                S.op("scalar", lambda e, sc=sc: e.activation(
                    out=stat[:, sc + 6:sc + 7], in_=stat[:, sc + 5:sc + 6],
                    func=AF.Sqrt),
                    wait=((dv, ndv),), inc=ac)
                nac = S.n(ac)
                S.op("vector", lambda e, sc=sc: e.reciprocal(
                    stat[:, sc + 7:sc + 8], stat[:, sc + 6:sc + 7]),
                    wait=((ac, nac),), inc=dv)
                S.op("vector", lambda e, sc=sc, j=j: e.tensor_scalar(
                    out=x1_t[:, j * D:(j + 1) * D], in0=x1_t[:, j * D:(j + 1) * D],
                    scalar1=stat[:, sc + 3:sc + 4], scalar2=stat[:, sc + 7:sc + 8],
                    op0=ALU.subtract, op1=ALU.mult), inc=dv)
                S.op("vector", lambda e, sc=sc, j=j: e.reduce_max(
                    stat[:, sc + 4:sc + 5], x1_t[:, j * D:(j + 1) * D],
                    axis=AX.X, apply_absolute_value=True), inc=dv)
                S.op("vector", lambda e, sc=sc: e.tensor_scalar(
                    out=stat[:, sc + 4:sc + 5], in0=stat[:, sc + 4:sc + 5],
                    scalar1=EPS, scalar2=None, op0=ALU.max), inc=dv)
                S.op("vector", lambda e, sc=sc: e.reciprocal(
                    stat[:, sc + 5:sc + 6], stat[:, sc + 4:sc + 5]), inc=dv)
                S.op("vector", lambda e, sc=sc: e.tensor_scalar(
                    out=stat[:, sc + 5:sc + 6], in0=stat[:, sc + 5:sc + 6],
                    scalar1=127.0, scalar2=None, op0=ALU.mult), inc=dv)
                S.op("vector", lambda e, sc=sc: e.tensor_scalar(
                    out=stat[:, sc + 4:sc + 5], in0=stat[:, sc + 4:sc + 5],
                    scalar1=1.0 / 127.0, scalar2=None, op0=ALU.mult), inc=dv)
                for hf in range(2):
                    S.op("vector", lambda e, sc=sc, j=j, hf=hf: e.tensor_scalar(
                        out=xqi_t[:, :],
                        in0=x1_t[:, j * D + hf * 512: j * D + (hf + 1) * 512],
                        scalar1=stat[:, sc + 5:sc + 6], scalar2=None, op0=ALU.mult),
                        inc=dv)
                    S.op("vector", lambda e, sc=sc, hf=hf, xqb=xqb: e.tensor_scalar(
                        out=xqb[:, hf * 512:(hf + 1) * 512], in0=xqi_t[:, :],
                        scalar1=stat[:, sc + 4:sc + 5], scalar2=None, op0=ALU.mult),
                        inc=dv)
                st8[f"lnq{j}"] = S.n(dv)
                st8[f"xqd_last{j}"] = S.n(dv)

            def attn_ln1_b(j):
                gi_, _ = st8.pop(f"lnp{j}")
                ndv = st8[f"lnq{j}"]
                xqb = xqd_t if j == 0 else xqd_t2
                wl = [(dv, ndv), (ac, st8.get("b7_ac", 0))]
                for c in range(8):
                    S.op("tensor", lambda e, c=c, xqb=xqb: e.transpose(
                        B[7][:, c * 128:(c + 1) * 128],
                        xqb[:, c * 128:(c + 1) * 128], c_idb[:, :]),
                        wait=wl if c == 0 else (), inc=pe)
                npe = S.n(pe)
                st8[f"xqdT_pe{j}"] = npe
                for c in range(8):
                    S.op("scalar", lambda e, c=c, j=j: e.activation(
                        out=xqdT_t[:, c * 128:(c + 1) * 128],
                        in_=B[7][:, c * 128:(c + 1) * 128], func=AF.Copy),
                        wait=((pe, npe), ("sq0", st8.get("xqst0", 0)),
                              ("sq1", st8.get("xqst1", 0))) if c == 0 else (),
                        inc=ac)
                st8["b7_ac"] = S.n(ac)
                nac = S.n(ac)
                ndva = S.n(dv)
                tile = gi_ * 2 + j
                S.dma("sync", x1_d[tile, :, :], x1_t[:, j * D:(j + 1) * D],
                      f"s1{j}", wait=((dv, ndva),))
                if debug:
                    S.dma("sync", dbg["x1"][tile, :, :], x1_t[:, j * D:(j + 1) * D],
                          "dbg", serial=True)
                    S.dbg_fence()
                st8[f"x1st{j}"] = S.n(f"s1{j}")
                S.dma("sync",
                      xqdT_d[:, :, gi_ * GRP + j * 128: gi_ * GRP + j * 128 + 128],
                      xqdT_t[:, 0:D].rearrange("p (c t) -> p c t", c=8),
                      f"sq{j}", wait=((ac, nac),))
                if debug:
                    S.dma("sync",
                          dbg["xqdT"][:, :, gi_ * GRP + j * 128: gi_ * GRP + j * 128 + 128],
                          xqdT_t[:, 0:D].rearrange("p (c t) -> p c t", c=8),
                          "dbg", serial=True)
                    S.dbg_fence()
                st8[f"xqst{j}"] = S.n(f"sq{j}")

            def emit_group(gi, src_ap, full):
                gb = gi % 2
                xs = f"xm{gb}" if full else f"x{gb}"
                pre = st8.pop(f"xpre{gb}", None)
                if pre is None:
                    for j in range(2):
                        S.dma("gpsimd" if full else "sync",
                              xbuf[:, (gb * 2 + j) * D:(gb * 2 + j + 1) * D],
                              src_ap[gi * GRP + j * 128: gi * GRP + (j + 1) * 128, :],
                              xs,
                              wait=((dv, st8.get(f"xres{gb}", 0)),
                                    (pe, st8.get(f"xfree{gb}", 0)),))
                    nx = S.n(xs)
                else:
                    nx = pre
                for b4 in range(4):
                    bank = B[2 + b4 % 2]
                    wl = [(xs, nx), (ac, st8.get(f"vb{b4 % 2}_ac", 0))]
                    for s4 in range(4):
                        idx = b4 * 4 + s4
                        j, c = divmod(idx, 8)
                        S.op("tensor", lambda e, bank=bank, s4=s4, j=j, c=c, gb=gb: e.transpose(
                            bank[:, s4 * 128:(s4 + 1) * 128],
                            xbuf[:, (gb * 2 + j) * D + c * 128:(gb * 2 + j) * D + (c + 1) * 128],
                            c_idf[:, :]),
                            wait=wl if s4 == 0 else (), inc=pe)
                    npe = S.n(pe)
                    for s4 in range(4):
                        idx = b4 * 4 + s4
                        j, c = divmod(idx, 8)
                        S.op("scalar", lambda e, bank=bank, s4=s4, j=j, c=c, gb=gb: e.activation(
                            out=xT[:, gb * 8 * GRP + c * GRP + j * 128: gb * 8 * GRP + c * GRP + j * 128 + 128],
                            in_=bank[:, s4 * 128:(s4 + 1) * 128], func=AF.Copy),
                            wait=((pe, npe),) if s4 == 0 else (), inc=ac)
                    st8[f"vb{b4 % 2}_ac"] = S.n(ac)
                n_xT = S.n(ac)
                xTb = gb * 8 * GRP
                if full:
                    for j in range(2):
                        if st8.get(f"lnp{j}") is not None:
                            attn_ln1_a(j)

                for j in range(2):
                    for e4 in range(4):
                        bk = (j * 4 + e4) % 2
                        bank = B[2 + bk]
                        wl = [(ac, n_xT), (ac, st8.get(f"vb{bk}_ac", 0))]
                        for c in range(8):
                            S.op("tensor", lambda e, bank=bank, j=j, c=c, e4=e4, xTb=xTb: e.matmul(
                                bank[:, :],
                                xT[:, xTb + c * GRP + j * 128: xTb + c * GRP + j * 128 + 128],
                                w_v[:, c * E + e4 * 512: c * E + (e4 + 1) * 512],
                                start=(c == 0), stop=False),
                                wait=wl if c == 0 else (), inc=pe)
                        S.op("tensor", lambda e, bank=bank, e4=e4: e.matmul(
                            bank[:, :], c_ones[0:1, :],
                            p_bv[0:1, e4 * 512:(e4 + 1) * 512], start=False, stop=True), inc=pe)
                        npe = S.n(pe)
                        S.op("scalar", lambda e, bank=bank, j=j, e4=e4, gb=gb: e.activation(
                            out=v_tm[:, gb * 2 * E + j * E + e4 * 512: gb * 2 * E + j * E + (e4 + 1) * 512],
                            in_=bank[:, :], func=AF.Silu),
                            wait=((pe, npe), (pe, st8.get(f"vfree{gb}", 0))),
                            inc=ac)
                        st8[f"vb{bk}_ac"] = S.n(ac)
                n_v = S.n(ac)
                vb = gb * 2 * E
                if full:
                    for j in range(2):
                        if st8.get(f"lnp{j}") is not None:
                            attn_ln1_b(j)

                wl = [(ac, st8.get("vb0_ac", 0))]
                for c in range(8):
                    S.op("tensor", lambda e, c=c, xTb=xTb: e.matmul(
                        B[2][:, 0:GRP],
                        w_i[:, c * HD: c * HD + 128],
                        xT[:, xTb + c * GRP: xTb + (c + 1) * GRP],
                        start=(c == 0), stop=(c == 7)),
                        wait=wl if c == 0 else (), inc=pe)
                npe = S.n(pe)
                S.op("scalar", lambda e: e.activation(
                    out=x_fm[:, :], in_=B[2][:, 0:GRP], func=AF.Silu,
                    bias=binc_col[:, 0:1], scale=1.0),
                    wait=((pe, npe), (dv, st8.get("xfm_dv", 0)),
                          (ac, st8.get("xfm_ac", 0))), inc=ac)
                st8["vb0_ac"] = S.n(ac)
                n_xf = S.n(ac)

                S.op("scalar", lambda e: e.activation(
                    out=lk_f[:, :], in_=x_fm[:, :], func=AF.Identity,
                    bias=p_aff[:, 7:8], scale=p_aff[:, 6:7]),
                    wait=((ac, n_xf),), inc=ac)
                n_lkac = S.n(ac)
                st8["xfm_ac"] = n_lkac
                if full:
                    for ai, (dstt, gcol) in enumerate(((qq_f[:, 0:GRP], 0), (qk_f[:, 0:GRP], 2),
                                       (lq_f[:, gb * GRP:(gb + 1) * GRP], 4))):
                        S.op("vector", lambda e, dstt=dstt, gcol=gcol: e.tensor_scalar(
                            out=dstt, in0=x_fm[:, :],
                            scalar1=p_aff[:, gcol:gcol + 1],
                            scalar2=p_aff[:, gcol + 1:gcol + 2],
                            op0=ALU.mult, op1=ALU.add),
                            wait=((ac, n_xf),) if ai == 0 else (), inc=dv)
                n_aff = S.n(dv)
                st8["xfm_dv"] = n_aff

                wl = [(ac, max(n_lkac, st8.get("b7_ac", 0))),
                      (dv, st8.get("b7_dv", 0))]
                for cc in range(2):
                    S.op("tensor", lambda e, cc=cc: e.transpose(
                        B[7][:, cc * 128:(cc + 1) * 128], lk_f[:, cc * 128:(cc + 1) * 128],
                        c_idb[:, :]), wait=wl if cc == 0 else (), inc=pe)
                npe = S.n(pe)
                for cc in range(2):
                    S.op("scalar", lambda e, cc=cc: e.activation(
                        out=lk_tm[:, cc * 128:(cc + 1) * 128],
                        in_=B[7][:, cc * 128:(cc + 1) * 128], func=AF.Copy),
                        wait=((pe, npe),) if cc == 0 else (), inc=ac)
                st8["b7_ac"] = S.n(ac)
                n_lk = S.n(ac)

                if full:
                    S.op("vector", lambda e: e.tensor_copy(S_bf[:, :], S_m[:, :]),
                         wait=((pe, st8.get("sbf_pe", 0)),), inc=dv)
                n_scast = S.n(dv)

                for e4 in range(4):
                    wl = [(ac, n_v), (ac, n_lk), (dv, st8.get("kv_dv", 0)),
                          (dv, n_scast)]
                    for cc in range(2):
                        S.op("tensor", lambda e, cc=cc, e4=e4, vb=vb: e.matmul(
                            B[3][:, :],
                            lk_tm[:, cc * 128:(cc + 1) * 128],
                            v_tm[:, vb + cc * E + e4 * 512: vb + cc * E + (e4 + 1) * 512],
                            start=(cc == 0), stop=(cc == 1)),
                            wait=wl if cc == 0 else (), inc=pe)
                    npe = S.n(pe)
                    S.op("vector", lambda e, e4=e4: e.tensor_add(
                        S_m[:, e4 * 512:(e4 + 1) * 512],
                        S_m[:, e4 * 512:(e4 + 1) * 512], B[3][:, :]),
                        wait=((pe, npe),), inc=dv)
                    st8["kv_dv"] = S.n(dv)
                n_kv = S.n(dv)

                if not full:
                    st8[f"xfree{gb}"] = S.n(pe)
                    st8[f"xres{gb}"] = 0
                    st8[f"vfree{gb}"] = S.n(pe)
                    return

                wl = [(dv, n_aff), (ac, st8.get("b4_ac", 0)), (dv, st8.get("b4_dv", 0))]
                for mc in range(2):
                    S.op("tensor", lambda e, mc=mc: e.matmul(
                        B[4][:, mc * GRP:(mc + 1) * GRP],
                        qk_f[:, mc * 128:(mc + 1) * 128],
                        qq_f[:, 0:GRP], start=True, stop=True),
                        wait=wl if mc == 0 else (), inc=pe)
                npe = S.n(pe)
                for mc in range(2):
                    S.op("scalar", lambda e, mc=mc: e.activation(
                        out=at_bf[:, mc * GRP:(mc + 1) * GRP],
                        in_=B[4][:, mc * GRP:(mc + 1) * GRP], func=AF.Relu),
                        wait=((pe, npe),) if mc == 0 else (), inc=ac)
                st8["b4_ac"] = S.n(ac)
                n_rl = S.n(ac)
                for mc in range(2):
                    S.op("vector", lambda e, mc=mc: e.scalar_tensor_tensor(
                        out=at_bf[:, mc * GRP:(mc + 1) * GRP],
                        in0=at_bf[:, mc * GRP:(mc + 1) * GRP], scalar=1.0,
                        in1=at_bf[:, mc * GRP:(mc + 1) * GRP],
                        op0=ALU.mult, op1=ALU.mult),
                        wait=((ac, n_rl),) if mc == 0 else (), inc=dv)
                    S.op("vector", lambda e, mc=mc: e.tensor_mul(
                        at_bf[:, mc * GRP:(mc + 1) * GRP],
                        at_bf[:, mc * GRP:(mc + 1) * GRP],
                        c_m01[:, mc * GRP:(mc + 1) * GRP]), inc=dv)
                n_at = S.n(dv)

                for ec in range(16):
                    eb = ec % 2
                    gbk = B[5] if eb == 0 else B[0]
                    obk = B[4] if eb == 0 else B[6]
                    wl = [(ac, n_xT), (ac, st8.get(f"gb{eb}_ac", 0))]
                    if eb == 1 and ec == 1:
                        wl.append((dv, st8.get("resid0_dv", 0)))
                    if eb == 0 and ec == 0:
                        wl.append((dv, st8.get("resid1_dv", 0)))
                    for c in range(8):
                        S.op("tensor", lambda e, gbk=gbk, c=c, ec=ec, xTb=xTb: e.matmul(
                            gbk[:, 0:GRP],
                            w_g[:, c * E + ec * 128: c * E + (ec + 1) * 128],
                            xT[:, xTb + c * GRP: xTb + (c + 1) * GRP],
                            start=(c == 0), stop=(c == 7)),
                            wait=wl if c == 0 else (), inc=pe)
                    npe = S.n(pe)
                    S.op("scalar", lambda e, gbk=gbk, ec=ec, eb=eb: e.activation(
                        out=gate_p[:, eb * GRP:(eb + 1) * GRP],
                        in_=gbk[:, 0:GRP], func=AF.Silu,
                        bias=p_bgc[:, ec:ec + 1], scale=1.0),
                        wait=((pe, npe), (dv, st8.get(f"gate{eb}_dv", 0))), inc=ac)
                    st8[f"gb{eb}_ac"] = S.n(ac)
                    n_gate = S.n(ac)
                    wl = [(dv, n_at), (ac, n_v), (dv, max(st8.get(f"o{eb}_dv", 0),
                          st8.get("resid1_dv", 0) if eb == 1 else 0))]
                    for mc in range(2):
                        S.op("tensor", lambda e, obk=obk, mc=mc, ec=ec, vb=vb: e.matmul(
                            obk[:, 0:GRP],
                            v_tm[:, vb + mc * E + ec * 128: vb + mc * E + (ec + 1) * 128],
                            at_bf[:, mc * GRP:(mc + 1) * GRP],
                            start=(mc == 0), stop=False),
                            wait=wl if mc == 0 else (), inc=pe)
                    S.op("tensor", lambda e, obk=obk, ec=ec, gb=gb: e.matmul(
                        obk[:, 0:GRP], S_bf[:, ec * 128:(ec + 1) * 128],
                        lq_f[:, gb * GRP:(gb + 1) * GRP], start=False, stop=True), inc=pe)
                    npe = S.n(pe)
                    if ec == 15:
                        st8["sbf_pe"] = npe
                    S.op("vector", lambda e, obk=obk, ec=ec, eb=eb: e.scalar_tensor_tensor(
                        out=o_fm[:, ec * GRP:(ec + 1) * GRP],
                        in0=obk[:, 0:GRP], scalar=1.0,
                        in1=gate_p[:, eb * GRP:(eb + 1) * GRP],
                        op0=ALU.mult, op1=ALU.mult),
                        wait=((pe, npe), (ac, n_gate)), inc=dv)
                    st8[f"o{eb}_dv"] = S.n(dv)
                    st8[f"gate{eb}_dv"] = S.n(dv)
                st8["b4_dv"] = st8["o0_dv"]
                st8["b7_dv"] = 0
                n_omul = S.n(dv)

                wbanks = ((B[0], B[1]), (B[5], B[6]))
                for j in range(2):
                    wl0 = [(dv, n_omul), (dv, st8.get(f"resid{j}_dv", 0))]
                    for ec in range(16):
                        for hf in range(2):
                            bank = wbanks[j][hf]
                            S.op("tensor", lambda e, bank=bank, j=j, hf=hf, ec=ec: e.matmul(
                                bank[:, :],
                                o_fm[:, ec * GRP + j * 128: ec * GRP + j * 128 + 128],
                                w_o[:, ec * D + hf * 512: ec * D + (hf + 1) * 512],
                                start=(ec == 0), stop=False),
                                wait=wl0 if (ec == 0 and hf == 0) else (), inc=pe)
                    for hf in range(2):
                        bank = wbanks[j][hf]
                        S.op("tensor", lambda e, bank=bank, hf=hf: e.matmul(
                            bank[:, :], c_ones[0:1, :],
                            p_bout[0:1, hf * 512:(hf + 1) * 512], start=False, stop=True), inc=pe)
                    n_wout = S.n(pe)

                    sc = j * 8
                    for hf in range(2):
                        bank = wbanks[j][hf]
                        S.op("vector", lambda e, bank=bank, j=j, hf=hf, sc=sc, gb=gb: e.scalar_tensor_tensor(
                            out=x1_t[:, j * D + hf * 512: j * D + (hf + 1) * 512],
                            in0=bank[:, :], scalar=1.0,
                            in1=xbuf[:, (gb * 2 + j) * D + hf * 512:(gb * 2 + j) * D + (hf + 1) * 512],
                            op0=ALU.mult, op1=ALU.add,
                            accum_out=stat[:, sc + hf:sc + hf + 1]),
                            wait=((pe, n_wout), (f"s1{j}", st8.get(f"x1st{j}", 0))) if hf == 0 else (),
                            inc=dv)
                    st8[f"resid{j}_dv"] = S.n(dv)
                    if j == 1:
                        st8[f"xres{gb}"] = S.n(dv)
                    st8[f"lnp{j}"] = (gi, S.n(dv))
                    pass
                st8[f"xfree{gb}"] = S.n(pe)
                st8[f"vfree{gb}"] = S.n(pe)


            env.update(dict(
                c_ones=c_ones, c_onesf=c_onesf, c_idb=c_idb, c_idf=c_idf,
                stat=stat, hal_xqdT=hal_xqdT, hal_xT=hal_xT, p_aff=p_aff,
                p_bv=p_bv, p_bg=p_bg, p_bout=p_bout, binc_col=binc_col,
                w_v=w_v, w_i=w_i, x_fm=x_fm, qk_f=qk_f,
                v_tm=v_tm, Sh_bf=Sh_bf, S_bf=S_bf, xh=xh))

            S.op("tensor", lambda e: None, wait=(("gw", n_wload_pre), ("lc", S.n("lc"))))
            for gi in range(NGP):
                if gi == NGP - 1:
                    S.op("vector", lambda e: e.tensor_copy(Sh_bf[:, :], S_m[:, :]),
                         inc=dv)
                emit_group(gi, xp, full=False)
                if gi + 1 < NGP:
                    gb2 = (gi + 1) % 2
                    for j in range(2):
                        S.dma("sync", xbuf[:, (gb2 * 2 + j) * D:(gb2 * 2 + j + 1) * D],
                              xp[(gi + 1) * GRP + j * 128:(gi + 1) * GRP + (j + 1) * 128, :],
                              f"x{gb2}",
                              wait=((dv, st8.get(f"xres{gb2}", 0)),
                                    (pe, st8.get(f"xfree{gb2}", 0)),))
                    st8[f"xpre{gb2}"] = S.n(f"x{gb2}")
                for _ in range(2):
                    if pslices:
                        pslices.pop(0)()
            S.op("vector", lambda e: e.tensor_scalar(
                out=S_m[:, :], in0=S_m[:, :], scalar1=c_pm[:, 0:1],
                scalar2=None, op0=ALU.mult), inc=dv)
            while pslices:
                pslices.pop(0)()
            pre_scope.close()
            w_g = asb("w_g", [128, 8 * E], BF16)
            w_o = asb("w_o", [128, 16 * D], BF16)
            at_bf = asb("at_bf", [128, 2 * GRP], BF16)
            gate_p = asb("gate_p", [128, 2 * GRP], BF16)
            o_fm = asb("o_fm", [128, 16 * GRP], BF16)
            x1_t = asb("x1_t", [128, 2 * D], F32)
            xqi_t = asb("xqi_t", [128, 512], I32)
            xqd_t = asb("xqd_t", [128, D], BF16)
            xqd_t2 = asb("xqd_t2", [128, D], BF16)
            xqdT_t = asb("xqdT_t", [128, D], BF16)
            for k in range(8):
                S.dma("gpsimd", w_g[:, k * E:(k + 1) * E], wg[k * 128:(k + 1) * 128, :],
                      "gw2", wait=((dv, prest["dv"]),) if k == 0 else ())
            for k in range(16):
                S.dma("gpsimd", w_o[:, k * D:(k + 1) * D], wo[k * 128:(k + 1) * 128, :], "gw2")
            n_wload = S.n("gw2")
            env.update(locals())
            S.op("tensor", lambda e: None, wait=(("gw2", n_wload),))
            env["pre_last_gb"] = (NGP - 1) % 2
            emit_halo(nc, S, B, env, which=0)
            st8["xqdT_pe0"] = S.n(pe)
            st8["xqdT_pe1"] = S.n(pe)
            S.barrier()

            for gi in range(NG):
                emit_group(gi, xm, full=True)
            for j in range(2):
                if st8.get(f"lnp{j}") is not None:
                    attn_ln1_a(j)
            for j in range(2):
                if st8.get(f"lnp{j}") is not None:
                    attn_ln1_b(j)
            S.op("vector", lambda e: e.tensor_copy(S_bf[:, :], S_m[:, :]), inc=dv)
            emit_halo(nc, S, B, env, which=1)

            if debug:
                ndv = S.n(dv)
                S.dma("sync", dbg["S"][:, :], S_m[:, :], "dbg", wait=((dv, ndv),), serial=True)
                S.dma("sync", dbg["halxq"][:, :], hal_xqdT[:, :], "dbg",
                      wait=((ac, S.n(ac)),), serial=True)
                S.dbg_fence()
            S.barrier()

        with contextlib.ExitStack() as ffn_scope:
            fsb = lambda n, s, d: ffn_scope.enter_context(nc.sbuf_tensor(n, s, d))
            w1q = fsb("w1q", [128, 8 * DH], BF16)
            w2q = fsb("w2q", [128, 32 * D], BF16)
            hsl = fsb("hsl", [128, 2 * 32 * SL], BF16)
            cv_h = fsb("cv_h", [128, 8 * FCH], BF16)
            q8_h = fsb("q8_h", [128, 16 * FCH], I8)
            xq2 = fsb("xq2", [128, 32 * FCH], BF16)
            x1f = fsb("x1f", [128, 2 * D], F32)
            xqTf = fsb("xqTf", [128, 8 * FCH], BF16)
            xqTf2 = fsb("xqTf2", [128, 8 * FCH], BF16)

            s2r = fsb("s2r", [1, 256], F32)
            s2bc = fsb("s2bc", [128, FCH], F32)
            hal_cols = fsb("hal_cols", [128, 64], BF16)
            hal_hrow = fsb("hal_hrow", [2, 512], BF16)
            scr4 = fsb("scr4", [128, 512], BF16)

            for c in range(8):
                S.dma("sync", w1q[:, c * DH:(c + 1) * DH], w1q_d[c, :, :], "lwq",
                      wait=(("sw", S.n("sw")),) if c == 0 else ())
            for c in range(32):
                S.dma("sync", w2q[:, c * D:(c + 1) * D], w2q_d[c, :, :], "lwq")
            n_wq = S.n("lwq")

            for nsl in range(8):
                bank = B[nsl % 2]
                wl = [("lwq", n_wq), (ac, 0)]
                for c in range(8):
                    S.op("tensor", lambda e, bank=bank, nsl=nsl, c=c: e.matmul(
                        bank[0:2, 0:512], hal_xqdT[:, c * 2:(c + 1) * 2],
                        w1q[:, c * DH + nsl * 512: c * DH + (nsl + 1) * 512],
                        start=(c == 0), stop=(c == 7)),
                        wait=wl if c == 0 else (), inc=pe)
                npe = S.n(pe)
                S.op("scalar", lambda e, bank=bank, nsl=nsl: e.activation(
                    out=hal_hrow[0:2, :],
                    in_=bank[0:2, 0:512], func=AF.Copy,
                    scale=c_hm[0:2, 0:1]),
                    wait=((pe, npe),), inc=ac)
                nac = S.n(ac)
                for j in range(4):
                    S.op("tensor", lambda e, j=j: e.transpose(
                        B[7][:, j * 2:(j + 1) * 2], hal_hrow[0:2, j * 128:(j + 1) * 128],
                        c_idb[0:2, 0:2]),
                        wait=((ac, nac),) if j == 0 else (), inc=pe)
                npe = S.n(pe)
                for j in range(4):
                    cc = nsl * 4 + j
                    S.op("scalar", lambda e, j=j, cc=cc: e.activation(
                        out=hal_cols[:, cc * 2:(cc + 1) * 2], in_=B[7][:, j * 2:(j + 1) * 2],
                        func=AF.Copy), wait=((pe, npe),) if j == 0 else (), inc=ac)
            n_halc_ac = S.n(ac)
            fst_init_halh = True
            n_halc = 0

            fst = {"halh_ac": n_halc_ac, "psil": [0] * 8, "tpend": [], "fpend": [], "fchain": [0, 0], "b2pend": []}
            if debug:
                S.op("vector", lambda e: e.memset(hsl[:, :], 0.0), inc=dv)
                fst["memset_dv"] = S.n(dv)
            hview = lambda fb: hsl[:, fb * 32 * SL:(fb + 1) * 32 * SL].rearrange(
                "p (c s) -> p c s", c=32)

            halv = hal_cols[:, :].rearrange("p (c u) -> p c u", c=32)
            mx_t = xq2[:, 0:2 * FCH]
            mn_t = xq2[:, 2 * FCH:4 * FCH]

            def slvt(tfc, hc, a, b):
                bs = (tfc % 2) * 32 * SL
                return hsl[:, bs + hc * SL + a: bs + hc * SL + b]

            def emit_sf(tfc, ent):
                _, g0, cntC = ent
                psil = fst["psil"]
                nsil = []
                for i, k2 in enumerate(range(g0, g0 + 4)):
                    S.op("scalar", lambda e, tfc=tfc, k2=k2: e.activation(
                        out=slvt(tfc, k2, 1, 1 + FCH),
                        in_=cv_h[:, (k2 % 8) * FCH:((k2 % 8) + 1) * FCH],
                        func=AF.Silu),
                        wait=((dv, cntC[i]),), inc=ac, sw=False)
                    psil[k2 % 8] = S.n(ac)
                    nsil.append(S.n(ac))
                fst["fpend"].append((max(g0 + 5, 18), g0, nsil))

            def emit_fold(tfc, ent):
                _, g0, nsil = ent
                ch = fst["fchain"]
                for i, k2 in enumerate(range(g0, g0 + 4)):
                    for ci, (buf, opx) in enumerate(((mx_t, ALU.max), (mn_t, ALU.min))):
                        if k2 == 0:
                            wl = [(ac, nsil[i])]
                            if ci == 0:
                                wl.append((pe, fst.get("w2done_pe", 0)))
                            S.op("vector", lambda e, tfc=tfc, buf=buf, k2=k2: e.tensor_copy(
                                buf[:, 0:FCH], slvt(tfc, k2, 1, 1 + FCH)),
                                wait=wl, inc=dv, sw=False)
                        else:
                            S.op("vector", lambda e, tfc=tfc, buf=buf, opx=opx, k2=k2: e.tensor_tensor(
                                buf[:, 0:FCH], slvt(tfc, k2, 1, 1 + FCH),
                                buf[:, 0:FCH], op=opx),
                                wait=((ac, nsil[i]), (dv, ch[ci])), inc=dv, sw=False)
                        ch[ci] = S.n(dv)

            def tail_piece(tfc, hc, evacs):
                while fst["tpend"] and fst["tpend"][0][0] <= hc:
                    emit_sf(tfc, fst["tpend"].pop(0))
                while fst["fpend"] and fst["fpend"][0][0] <= hc:
                    emit_fold(tfc, fst["fpend"].pop(0))
                for _ in range(2):
                    if fst["b2pend"]:
                        fst["b2pend"].pop(0)()
                final = (tfc == NFC - 1)
                fb = tfc % 2
                fbn = (tfc + 1) % 2
                k = hc % 8
                hs = hc // 16
                psil = fst["psil"]
                if hc == 0 and tfc == 0:
                    S.op("vector", lambda e: e.tensor_copy(
                        hview(0)[:, :, 0:1], halv[:, :, 0:1]),
                        wait=((ac, n_halc_ac),), inc=dv, sw=False)
                if final:
                    if hc == 0:
                        S.op("vector", lambda e, fb=fb: e.tensor_copy(
                            hview(fb)[:, :, SL - 1:SL], halv[:, :, 1:2]),
                            wait=((ac, n_halc_ac),), inc=dv, sw=False)
                else:
                    if hc == 0:
                        S.op("vector", lambda e, fb=fb, fbn=fbn: e.tensor_copy(
                            hview(fbn)[:, :, 0:1], hview(fb)[:, :, FCH:FCH + 1]),
                            wait=((ac, evacs[0]),), inc=dv, sw=False)
                    S.op("vector", lambda e, hc=hc, fb=fb, fbn=fbn: e.tensor_copy(
                        hview(fb)[:, hc:hc + 1, SL - 1:SL],
                        hview(fbn)[:, hc:hc + 1, 1:2]),
                        wait=((ac, evacs[hc]),), inc=dv, sw=False)
                S.op("vector", lambda e, tfc=tfc, hc=hc, k=k: e.tensor_scalar(
                    out=cv_h[:, k * FCH:(k + 1) * FCH],
                    in0=slvt(tfc, hc, 1, 1 + FCH),
                    scalar1=p_cw[:, 32 + hc:33 + hc],
                    scalar2=p_cw[:, 96 + hc:97 + hc],
                    op0=ALU.mult, op1=ALU.add),
                    wait=((ac, max(psil[k], evacs[hc] if final else 0)),),
                    inc=dv, sw=False)
                if hc % 4 != 3:
                    return
                g0 = hc - 3
                nA = S.n(dv)
                for k2 in range(g0, hc + 1):
                    S.op("vector", lambda e, tfc=tfc, k2=k2: e.scalar_tensor_tensor(
                        out=cv_h[:, (k2 % 8) * FCH:((k2 % 8) + 1) * FCH],
                        in0=slvt(tfc, k2, 0, FCH),
                        scalar=p_cw[:, k2:k2 + 1],
                        in1=cv_h[:, (k2 % 8) * FCH:((k2 % 8) + 1) * FCH],
                        op0=ALU.mult, op1=ALU.add),
                        wait=((dv, nA),) if k2 == g0 else (), inc=dv, sw=False)
                nB = S.n(dv)
                cntC = []
                for k2 in range(g0, hc + 1):
                    S.op("vector", lambda e, tfc=tfc, k2=k2: e.scalar_tensor_tensor(
                        out=cv_h[:, (k2 % 8) * FCH:((k2 % 8) + 1) * FCH],
                        in0=slvt(tfc, k2, 2, 2 + FCH),
                        scalar=p_cw[:, 64 + k2:65 + k2],
                        in1=cv_h[:, (k2 % 8) * FCH:((k2 % 8) + 1) * FCH],
                        op0=ALU.mult, op1=ALU.add),
                        wait=((dv, nB),) if k2 == g0 else (), inc=dv, sw=False)
                    cntC.append(S.n(dv))
                fst["tpend"].append((g0 + 5, g0, cntC))
                if hc == 31:
                    while fst["tpend"]:
                        emit_sf(tfc, fst["tpend"].pop(0))
                    while fst["fpend"]:
                        emit_fold(tfc, fst["fpend"].pop(0))
                    nw0 = S.n(dv)
                    S.op("vector", lambda e: e.tensor_scalar(
                        out=mn_t[:, 0:FCH], in0=mn_t[:, 0:FCH], scalar1=-1.0,
                        scalar2=None, op0=ALU.mult),
                        wait=((dv, nw0),), inc=dv, sw=False)
                    S.op("vector", lambda e: e.tensor_max(
                        mx_t[:, 0:FCH], mx_t[:, 0:FCH], mn_t[:, 0:FCH]),
                        inc=dv, sw=True)
                    fst["cvfold_dv"] = S.n(dv)

            def tail_rest(fc):
                fb = fc % 2
                base = fb * 32 * SL
                ndv = fst["cvfold_dv"]
                for tc in range(2):
                    S.op("tensor", lambda e, tc=tc: e.transpose(
                        B[7][:, tc * 128:(tc + 1) * 128],
                        mx_t[:, tc * 128:(tc + 1) * 128], c_idb[:, :]),
                        wait=((dv, ndv), (dv, fst.get("s2_dv", 0)),
                              (ac, fst.get("ln2sq_ac", 0))) if tc == 0 else (),
                        inc=pe)
                npe = S.n(pe)
                for tc in range(2):
                    S.op("vector", lambda e, tc=tc: e.reduce_max(
                        stat[:, 20 + tc:21 + tc], B[7][:, tc * 128:(tc + 1) * 128],
                        axis=AX.X), wait=((pe, npe),) if tc == 0 else (), inc=dv)
                S.op("vector", lambda e: e.tensor_scalar(
                    out=stat[:, 20:22], in0=stat[:, 20:22], scalar1=EPS,
                    scalar2=None, op0=ALU.max), inc=dv, sw=True)
                S.op("vector", lambda e: e.reciprocal(stat[:, 22:23], stat[:, 20:21]), inc=dv, sw=True)
                S.op("vector", lambda e: e.reciprocal(stat[:, 23:24], stat[:, 21:22]), inc=dv, sw=True)
                S.op("vector", lambda e: e.tensor_scalar(
                    out=stat[:, 22:24], in0=stat[:, 22:24], scalar1=127.0,
                    scalar2=None, op0=ALU.mult), inc=dv, sw=True)
                S.op("vector", lambda e: e.tensor_scalar(
                    out=stat[:, 24:26], in0=stat[:, 20:22], scalar1=1.0 / 127.0,
                    scalar2=wsc[:, 3:4], op0=ALU.mult, op1=ALU.mult), inc=dv, sw=True)
                ndv = S.n(dv)
                for tc in range(2):
                    S.op("tensor", lambda e, tc=tc: e.transpose(
                        B[4][0:1, 256 + tc * 128:384 + tc * 128],
                        stat[:, 22 + tc:23 + tc], c_idf[:, :]),
                        wait=((dv, ndv), (ac, fst.get("ln2sq_ac", 0))) if tc == 0 else (),
                        inc=pe)
                npe = S.n(pe)
                for tc in range(2):
                    S.op("scalar", lambda e, tc=tc: e.activation(
                        out=s2r[0:1, tc * 128:(tc + 1) * 128],
                        in_=B[4][0:1, 256 + tc * 128:384 + tc * 128], func=AF.Copy),
                        wait=((pe, npe),) if tc == 0 else (), inc=ac)
                nac = S.n(ac)
                for tc in range(2):
                    S.op("tensor", lambda e, tc=tc: e.matmul(
                        B[4][:, tc * 128:(tc + 1) * 128], c_onesf[0:1, :],
                        s2r[0:1, tc * 128:(tc + 1) * 128], start=True, stop=True),
                        wait=((ac, nac),) if tc == 0 else (), inc=pe)
                npe = S.n(pe)
                S.op("scalar", lambda e: e.activation(
                    out=s2bc[:, :], in_=B[4][:, 0:256], func=AF.Copy),
                    wait=((pe, npe),), inc=ac)
                n_s2 = S.n(ac)
                fst["s2bc_ac"] = n_s2

                s2v = s2bc[:, :]
                s2ap = [list(p) for p in s2v.ap]
                s2ap = [s2ap[0], [0, 16], s2ap[1]]
                s2b3 = bass.AP(tensor=s2v.tensor, offset=s2v.offset, ap=s2ap)
                n_q2h = [0, 0]
                for hs in range(2):
                    hv = hsl[:, base + hs * 16 * SL: base + (hs + 1) * 16 * SL].rearrange(
                        "p (c s) -> p c s", c=16)
                    S.op("vector", lambda e, hv=hv: e.tensor_mul(
                        q8_h[:, :].rearrange("p (c t) -> p c t", c=16),
                        hv[:, :, 1:1 + FCH], s2b3),
                        wait=((ac, n_s2), (dv, fst.get("q8_dv", 0))), inc=dv, sw=False)
                    S.op("vector", lambda e, hs=hs: e.tensor_copy(
                        xq2[:, hs * 16 * FCH:(hs + 1) * 16 * FCH], q8_h[:, :]),
                        inc=dv, sw=True)
                    fst["q8_dv"] = S.n(dv)
                    n_q2h[hs] = S.n(dv)
                n_q2 = S.n(dv)
                fst["s2_dv"] = n_q2
                fst[f"slabfree{fb}0"] = n_q2h[0]
                fst[f"slabfree{fb}1"] = n_q2h[1]
                fst["n_q2h"] = n_q2h

            def tail_rest_b(fc):
                fb = fc % 2
                n_q2h = fst.pop("n_q2h")
                banksets = ((B[5], B[6]), (B[0], B[1]))
                npe_t = [0, 0]
                for tt in range(2):
                    tile = fc * 2 + tt
                    xb = tt
                    S.dma("sync", x1f[:, xb * D:(xb + 1) * D], x1_d[tile, :, :],
                          f"lx1{xb}",
                          wait=((dv, fst.get(f"x1used{xb}", 0)),
                                (f"so{xb}", fst.get(f"o2st{xb}", 0))))
                    fst[f"nld_t{tt}"] = S.n(f"lx1{xb}")
                    for hf in range(2):
                        bank = banksets[tt][hf]
                        for hc in range(32):
                            wl = []
                            if hc == 0:
                                wl = [(dv, max(n_q2h[0], fst.get(f"w2o{tt}{hf}_dv", 0)))]
                                if tt == 1:
                                    wl.append((ac, fst.get("halh_ac", 0)))
                            elif hc == 16:
                                wl = [(dv, n_q2h[1])]
                            S.op("tensor", lambda e, bank=bank, tt=tt, hf=hf, hc=hc: e.matmul(
                                bank[:, :],
                                xq2[:, hc * FCH + tt * 128: hc * FCH + tt * 128 + 128],
                                w2q[:, hc * D + hf * 512: hc * D + (hf + 1) * 512],
                                start=(hc == 0), stop=(hc == 31)),
                                wait=wl, inc=pe)
                    npe_t[tt] = S.n(pe)
                fst["npe_t"] = npe_t
                fst["w2done_pe"] = npe_t[1]

            def tail_rest_b2(fc):
                fb = fc % 2
                npe_t = fst.pop("npe_t")
                banksets = ((B[5], B[6]), (B[0], B[1]))
                th = fst["b2pend"]

                def ep_stt(tt, hf):
                    xb = tt
                    scc = 24 + tt
                    bank = banksets[tt][hf]
                    wl = ((pe, npe_t[tt]), (f"lx1{xb}", fst[f"nld_t{tt}"])) if hf == 0 else ()
                    S.op("vector", lambda e, bank=bank, xb=xb, hf=hf, scc=scc, tt=tt: e.scalar_tensor_tensor(
                        out=x1f[:, xb * D + hf * 512: xb * D + (hf + 1) * 512],
                        in0=bank[:, :], scalar=stat[:, scc:scc + 1],
                        in1=x1f[:, xb * D + hf * 512: xb * D + (hf + 1) * 512],
                        op0=ALU.mult, op1=ALU.add,
                        accum_out=stat[:, 28 + hf * 2 + tt:29 + hf * 2 + tt]),
                        wait=wl, inc=dv, sw=False)
                    fst[f"w2o{tt}{hf}_dv"] = S.n(dv)
                    fst[f"b2stt{tt}"] = S.n(dv)

                def ep_sq(tt, hf):
                    xb = tt
                    wl = [(ac, S.n(ac))]
                    if hf == 0:
                        wl.append((dv, fst[f"b2stt{tt}"]))
                    S.op("scalar", lambda e, xb=xb, hf=hf, tt=tt: e.activation(
                        out=scr4[:, :],
                        in_=x1f[:, xb * D + hf * 512: xb * D + (hf + 1) * 512],
                        func=AF.Square,
                        accum_out=stat[:, 20 + hf * 2 + tt:21 + hf * 2 + tt]),
                        wait=wl, inc=ac, sw=False)
                    fst["ln2sq_ac"] = S.n(ac)

                def chain(i):
                    if i == 0:
                        S.op("vector", lambda e: e.tensor_add(
                            stat[:, 28:30], stat[:, 28:30], stat[:, 30:32]),
                            wait=((ac, fst["ln2sq_ac"]),), inc=dv, sw=False)
                    elif i == 1:
                        S.op("vector", lambda e: e.tensor_scalar(
                            out=stat[:, 28:30], in0=stat[:, 28:30],
                            scalar1=1.0 / D, scalar2=None, op0=ALU.mult),
                            inc=dv, sw=True)
                    elif i == 2:
                        S.op("vector", lambda e: e.tensor_add(
                            stat[:, 20:22], stat[:, 20:22], stat[:, 22:24]),
                            inc=dv, sw=True)
                    elif i == 3:
                        S.op("vector", lambda e: e.tensor_scalar(
                            out=stat[:, 20:22], in0=stat[:, 20:22],
                            scalar1=1.0 / D, scalar2=None, op0=ALU.mult),
                            inc=dv, sw=True)
                    elif i == 4:
                        S.op("vector", lambda e: e.tensor_mul(
                            stat[:, 22:24], stat[:, 28:30], stat[:, 28:30]),
                            inc=dv, sw=True)
                    elif i == 5:
                        S.op("vector", lambda e: e.tensor_sub(
                            stat[:, 22:24], stat[:, 20:22], stat[:, 22:24]),
                            inc=dv, sw=True)
                    elif i == 6:
                        S.op("vector", lambda e: e.tensor_scalar(
                            out=stat[:, 22:24], in0=stat[:, 22:24], scalar1=EPS,
                            scalar2=None, op0=ALU.add), inc=dv, sw=True)
                    elif i == 7:
                        ndv = S.n(dv)
                        S.op("scalar", lambda e: e.activation(
                            out=stat[:, 20:22], in_=stat[:, 22:24], func=AF.Sqrt),
                            wait=((dv, ndv),), inc=ac, sw=False)
                    else:
                        nac = S.n(ac)
                        S.op("vector", lambda e: e.reciprocal(
                            stat[:, 22:24], stat[:, 20:22]),
                            wait=((ac, nac),), inc=dv, sw=False)

                def fin(tt):
                    xb = tt
                    S.op("vector", lambda e, xb=xb, tt=tt: e.tensor_scalar(
                        out=x1f[:, xb * D:(xb + 1) * D], in0=x1f[:, xb * D:(xb + 1) * D],
                        scalar1=stat[:, 28 + tt:29 + tt], scalar2=stat[:, 22 + tt:23 + tt],
                        op0=ALU.subtract, op1=ALU.mult),
                        inc=dv, sw=True)
                    fst[f"x1used{xb}"] = S.n(dv)
                    ndv = S.n(dv)
                    tile = fc * 2 + tt
                    S.dma("sync", out[tile * 128:(tile + 1) * 128, :],
                          x1f[:, xb * D:(xb + 1) * D], f"so{xb}", wait=((dv, ndv),))
                    fst[f"o2st{xb}"] = S.n(f"so{xb}")

                for tt in range(2):
                    for hf in range(2):
                        th.append(lambda tt=tt, hf=hf: ep_stt(tt, hf))
                for tt in range(2):
                    for hf in range(2):
                        th.append(lambda tt=tt, hf=hf: ep_sq(tt, hf))
                for i in range(9):
                    th.append(lambda i=i: chain(i))
                th.append(lambda: fin(0))
                th.append(lambda: fin(1))

            def ffn_chunk(fc):
                fb = fc % 2
                xqT = xqTf if fb == 0 else xqTf2
                S.dma("gpsimd", xqT[:, :].rearrange("p (c t) -> p c t", c=8),
                      xqdT_d[:, :, fc * FCH:(fc + 1) * FCH], f"lxq{fb}",
                      wait=((pe, fst.get(f"xqTfree{fb}", 0)),))
                nld = S.n(f"lxq{fb}")
                tp = fst.pop("tailpend", None)
                evac_ac = []
                for hc in range(32):
                    bank = (B[2], B[3], B[4])[hc % 3]
                    wl = [(f"lxq{fb}", nld), ("lwq", n_wq),
                          (ac, fst.get(f"h{hc % 3}_ac", 0))]
                    if hc == 2:
                        wl.append((ac, fst.get("s2bc_ac", 0)))
                    for c in range(8):
                        S.op("tensor", lambda e, bank=bank, xqT=xqT, hc=hc, c=c: e.matmul(
                            bank[:, 0:FCH],
                            w1q[:, c * DH + hc * 128: c * DH + (hc + 1) * 128],
                            xqT[:, c * FCH:(c + 1) * FCH],
                            start=(c == 0), stop=(c == 7)),
                            wait=wl if c == 0 else (), inc=pe)
                    npe = S.n(pe)
                    S.op("scalar", lambda e, bank=bank, hc=hc, fb=fb: e.activation(
                        out=hsl[:, fb * 32 * SL + hc * SL + 1: fb * 32 * SL + hc * SL + 1 + FCH],
                        in_=bank[:, 0:FCH], func=AF.Copy),
                        wait=((pe, npe), (dv, max(fst.get(f"slabfree{fb}{hc // 16}", 0),
                                                  fst.get("memset_dv", 0)))),
                        inc=ac)
                    fst[f"h{hc % 3}_ac"] = S.n(ac)
                    evac_ac.append(S.n(ac))
                    if tp is not None:
                        tail_piece(tp, hc, evac_ac)
                    if hc == 6 and fst.get("restpend") is not None:
                        tail_rest_b(fst["restpend"])
                    if hc == 15 and fst.get("restpend") is not None:
                        tail_rest_b2(fst.pop("restpend"))
                n_h = S.n(ac)
                if debug and fc == 0:
                    S.dma("sync", dbg["h"][:, :], hsl[:, 0:32 * SL], "dbg",
                          wait=((ac, n_h),), serial=True)
                    S.dbg_fence()
                fst[f"xqTfree{fb}"] = S.n(pe)
                while fst["b2pend"]:
                    fst["b2pend"].pop(0)()
                if tp is not None:
                    tail_rest(tp)
                    fst["restpend"] = tp
                fst["tailpend"] = fc
                if fc == NFC - 1:
                    fst.pop("tailpend")
                    rp = fst.pop("restpend", None)
                    if rp is not None:
                        tail_rest_b(rp)
                        tail_rest_b2(rp)
                    for hc in range(32):
                        tail_piece(fc, hc, evac_ac)
                    tail_rest(fc)
                    tail_rest_b(fc)
                    tail_rest_b2(fc)
                    while fst["b2pend"]:
                        fst["b2pend"].pop(0)()

            for fc in range(NFC):
                ffn_chunk(fc)

            S.op("sync", lambda e: None,
                 wait=(("so0", S.n("so0")), ("so1", S.n("so1"))))

        with nc.Block() as block:
            S.emit(block)
    return nc


def emit_halo(nc, S, B, env, which):
    pe, ac, dv = "pe", "ac", "dv"
    c_ones, c_onesf = env["c_ones"], env["c_onesf"]
    c_idb = env["c_idb"]
    stat = env["stat"]
    aff = env["p_aff"]
    x1_t, xqi_t, xqd_t = env["x1_t"], env["xqi_t"], env["xqd_t"]
    h_y = x1_t[0:1, 0:D]
    hx = x1_t[0:1, D:2 * D]
    h_xq = xqd_t[0:1, 0:D]
    h_q = xqi_t[0:1, 0:512]
    vb15 = env.get("pre_last_gb", 1) * 2 * E

    with contextlib.ExitStack() as hs:
        hsb = lambda n, s, d: hs.enter_context(nc.sbuf_tensor(n, s, d))
        h_ve = hsb(f"h_ve{which}", [1, 512], BF16)
        h_ga = hsb(f"h_ga{which}", [1, 512], BF16)
        h_ob = hsb(f"h_ob{which}", [1, 512], BF16)
        h_xf = hsb(f"h_xf{which}", [128, 4], F32)
        hq_b = hsb(f"hq_b{which}", [128, 4], BF16)
        h_at = hsb(f"h_at{which}", [128, 2], BF16)
        h_oT = hsb(f"h_oT{which}", [128, 16], BF16)

        xcol = lambda c: env["hal_xT"][:, c * 2 + which: c * 2 + which + 1]
        n0dv, n0ac = S.n(dv), S.n(ac)
        w_drain = (("s10", S.n("s10")), ("s11", S.n("s11")),
                   ("sq0", S.n("sq0")), ("sq1", S.n("sq1")))
        for engx in ("vector", "scalar"):
            S.op(engx, lambda e: None, wait=w_drain[:2])
            S.op(engx, lambda e: None, wait=w_drain[2:])

        for c in range(8):
            S.op("tensor", lambda e, c=c: e.matmul(
                B[5][:, 0:1], env["w_i"][:, c * HD: c * HD + 128], xcol(c),
                start=(c == 0), stop=(c == 7)),
                wait=((dv, n0dv), (ac, n0ac)) if c == 0 else (), inc=pe)
        npe = S.n(pe)
        S.op("scalar", lambda e: e.activation(
            out=h_xf[:, 0:1], in_=B[5][:, 0:1], func=AF.Silu,
            bias=env["binc_col"][:, 0:1], scale=1.0), wait=((pe, npe),), inc=ac)
        nac = S.n(ac)
        for col, gcol in ((1, 0), (2, 2), (3, 4)):
            S.op("vector", lambda e, col=col, gcol=gcol: e.tensor_scalar(
                out=h_xf[:, col:col + 1], in0=h_xf[:, 0:1],
                scalar1=aff[:, gcol:gcol + 1], scalar2=aff[:, gcol + 1:gcol + 2],
                op0=ALU.mult, op1=ALU.add),
                wait=((ac, nac),) if col == 1 else (), inc=dv)
        S.op("vector", lambda e: e.tensor_copy(hq_b[:, 0:4], h_xf[:, 0:4]), inc=dv)
        ndv = S.n(dv)

        if which == 0:
            S.op("vector", lambda e: e.tensor_scalar(
                out=env["qk_f"][:, 0:GRP], in0=env["x_fm"][:, :],
                scalar1=aff[:, 2:3], scalar2=aff[:, 3:4],
                op0=ALU.mult, op1=ALU.add), inc=dv)
            ndv = S.n(dv)
            for mc in range(2):
                S.op("tensor", lambda e, mc=mc: e.matmul(
                    B[6][:, mc: mc + 1],
                    env["qk_f"][:, mc * 128:(mc + 1) * 128], hq_b[:, 1:2],
                    start=True, stop=True),
                    wait=((dv, ndv),) if mc == 0 else (), inc=pe)
            npe = S.n(pe)
            S.op("scalar", lambda e: e.activation(
                out=h_at[:, 0:2], in_=B[6][:, 0:2], func=AF.Relu),
                wait=((pe, npe),), inc=ac)
            nac = S.n(ac)
            S.op("vector", lambda e: e.scalar_tensor_tensor(
                out=h_at[:, 0:2], in0=h_at[:, 0:2], scalar=1.0,
                in1=h_at[:, 0:2], op0=ALU.mult, op1=ALU.mult),
                wait=((ac, nac),), inc=dv)
        else:
            S.op("tensor", lambda e: e.matmul(
                B[6][0:1, 0:1], hq_b[:, 1:2], hq_b[:, 2:3], start=True, stop=True),
                wait=((dv, ndv),), inc=pe)
            npe = S.n(pe)
            S.op("scalar", lambda e: e.activation(
                out=stat[0:1, 19:20], in_=B[6][0:1, 0:1], func=AF.Relu),
                wait=((pe, npe),), inc=ac)
            nac = S.n(ac)
            S.op("vector", lambda e: e.scalar_tensor_tensor(
                out=stat[0:1, 19:20], in0=stat[0:1, 19:20], scalar=1.0,
                in1=stat[0:1, 19:20], op0=ALU.mult, op1=ALU.mult),
                wait=((ac, nac),), inc=dv)
        n_att = S.n(dv)

        nprev_ac = S.n(ac)
        for e4 in range(4):
            wl = [(dv, n_att), (ac, nprev_ac)]
            for c in range(8):
                S.op("tensor", lambda e, c=c, e4=e4: e.matmul(
                    B[5][0:1, :], xcol(c),
                    env["w_g"][:, c * E + e4 * 512: c * E + (e4 + 1) * 512],
                    start=(c == 0), stop=False),
                    wait=wl if c == 0 else (), inc=pe)
            S.op("tensor", lambda e, e4=e4: e.matmul(
                B[5][0:1, :], c_ones[0:1, 0:1],
                env["p_bg"][0:1, e4 * 512:(e4 + 1) * 512],
                start=False, stop=True), inc=pe)
            npe = S.n(pe)
            S.op("scalar", lambda e: e.activation(
                out=h_ga[0:1, :], in_=B[5][0:1, :], func=AF.Silu),
                wait=((pe, npe),), inc=ac)
            nga = S.n(ac)
            if which == 1:
                for c in range(8):
                    S.op("tensor", lambda e, c=c, e4=e4: e.matmul(
                        B[5][0:1, :], xcol(c),
                        env["w_v"][:, c * E + e4 * 512: c * E + (e4 + 1) * 512],
                        start=(c == 0), stop=False),
                        wait=((ac, nga),) if c == 0 else (), inc=pe)
                S.op("tensor", lambda e, e4=e4: e.matmul(
                    B[5][0:1, :], c_ones[0:1, 0:1],
                    env["p_bv"][0:1, e4 * 512:(e4 + 1) * 512],
                    start=False, stop=True), inc=pe)
                npe = S.n(pe)
                S.op("scalar", lambda e: e.activation(
                    out=h_ve[0:1, :], in_=B[5][0:1, :], func=AF.Silu),
                    wait=((pe, npe),), inc=ac)
            n_ve = S.n(ac)
            if which == 0:
                for mc in range(2):
                    S.op("tensor", lambda e, mc=mc, e4=e4: e.matmul(
                        B[6][0:1, :], h_at[:, mc:mc + 1],
                        env["v_tm"][:, vb15 + mc * E + e4 * 512: vb15 + mc * E + (e4 + 1) * 512],
                        start=(mc == 0), stop=False),
                        wait=((dv, n_att),) if mc == 0 else (), inc=pe)
                S.op("tensor", lambda e, e4=e4: e.matmul(
                    B[6][0:1, :], hq_b[:, 3:4],
                    env["Sh_bf"][:, e4 * 512:(e4 + 1) * 512],
                    start=False, stop=True), inc=pe)
            else:
                S.op("tensor", lambda e, e4=e4: e.matmul(
                    B[6][0:1, :], hq_b[:, 3:4],
                    env["S_bf"][:, e4 * 512:(e4 + 1) * 512], start=True, stop=True),
                    wait=((dv, n_att),), inc=pe)
            npe = S.n(pe)
            S.op("scalar", lambda e: e.activation(
                out=h_ob[0:1, :], in_=B[6][0:1, :], func=AF.Copy),
                wait=((pe, npe),), inc=ac)
            nac = S.n(ac)
            if which == 1:
                S.op("vector", lambda e: e.scalar_tensor_tensor(
                    out=h_ob[0:1, :], in0=h_ve[0:1, :], scalar=stat[0:1, 19:20],
                    in1=h_ob[0:1, :], op0=ALU.mult, op1=ALU.add),
                    wait=((ac, max(nac, n_ve)),), inc=dv)
            S.op("vector", lambda e: e.tensor_mul(
                h_ob[0:1, :], h_ob[0:1, :], h_ga[0:1, :]),
                wait=((ac, max(nac, nga)),), inc=dv)
            ndv = S.n(dv)
            for s4 in range(4):
                S.op("tensor", lambda e, s4=s4: e.transpose(
                    B[7][:, s4 * 128:s4 * 128 + 1],
                    h_ob[0:1, s4 * 128:(s4 + 1) * 128], c_idb[0:1, 0:1]),
                    wait=((dv, ndv),) if s4 == 0 else (), inc=pe)
            npe = S.n(pe)
            for s4 in range(4):
                cix = e4 * 4 + s4
                S.op("scalar", lambda e, s4=s4, cix=cix: e.activation(
                    out=h_oT[:, cix:cix + 1], in_=B[7][:, s4 * 128:s4 * 128 + 1],
                    func=AF.Copy), wait=((pe, npe),) if s4 == 0 else (), inc=ac)
            nprev_ac = S.n(ac)
        nac = S.n(ac)

        nprev = nac
        for hf in range(2):
            for c in range(16):
                S.op("tensor", lambda e, hf=hf, c=c: e.matmul(
                    B[5][0:1, :], h_oT[:, c:c + 1],
                    env["w_o"][:, c * D + hf * 512: c * D + (hf + 1) * 512],
                    start=(c == 0), stop=False),
                    wait=((ac, nprev),) if c == 0 else (), inc=pe)
            S.op("tensor", lambda e, hf=hf: e.matmul(
                B[5][0:1, :], c_ones[0:1, 0:1], env["p_bout"][0:1, hf * 512:(hf + 1) * 512],
                start=False, stop=True), inc=pe)
            npe = S.n(pe)
            S.op("scalar", lambda e, hf=hf: e.activation(
                out=h_y[0:1, hf * 512:(hf + 1) * 512], in_=B[5][0:1, :],
                func=AF.Copy), wait=((pe, npe),), inc=ac)
            nprev = S.n(ac)
        nac = S.n(ac)
        S.dma("sync", hx[0:1, :], env["xh"][which:which + 1, :], "lhx", serial=True,
              wait=(("s10", S.n("s10")), ("s11", S.n("s11"))))
        nld = S.n("lhx")
        S.op("vector", lambda e: e.tensor_add(h_y[0:1, :], h_y[0:1, :], hx[0:1, :]),
             wait=((ac, nac), ("lhx", nld)), inc=dv)
        S.op("vector", lambda e: e.reduce_sum(stat[0:1, 16:17], h_y[0:1, :], axis=AX.X),
             inc=dv)
        ndv = S.n(dv)
        S.op("scalar", lambda e: e.activation(
            out=h_xq[0:1, :], in_=h_y[0:1, :], func=AF.Square,
            accum_out=stat[0:1, 17:18]), wait=((dv, ndv),), inc=ac)
        nac = S.n(ac)
        S.op("vector", lambda e: e.tensor_scalar(
            out=stat[0:1, 16:18], in0=stat[0:1, 16:18], scalar1=1.0 / D,
            scalar2=None, op0=ALU.mult), wait=((ac, nac),), inc=dv)
        S.op("vector", lambda e: e.tensor_mul(
            stat[0:1, 18:19], stat[0:1, 16:17], stat[0:1, 16:17]), inc=dv)
        S.op("vector", lambda e: e.tensor_sub(
            stat[0:1, 18:19], stat[0:1, 17:18], stat[0:1, 18:19]), inc=dv)
        S.op("vector", lambda e: e.tensor_scalar(
            out=stat[0:1, 18:19], in0=stat[0:1, 18:19], scalar1=EPS,
            scalar2=None, op0=ALU.add), inc=dv)
        ndv = S.n(dv)
        S.op("scalar", lambda e: e.activation(
            out=stat[0:1, 17:18], in_=stat[0:1, 18:19], func=AF.Sqrt),
            wait=((dv, ndv),), inc=ac)
        nac = S.n(ac)
        S.op("vector", lambda e: e.reciprocal(stat[0:1, 18:19], stat[0:1, 17:18]),
             wait=((ac, nac),), inc=dv)
        S.op("vector", lambda e: e.tensor_scalar(
            out=h_y[0:1, :], in0=h_y[0:1, :], scalar1=stat[0:1, 16:17],
            scalar2=stat[0:1, 18:19], op0=ALU.subtract, op1=ALU.mult), inc=dv)
        S.op("vector", lambda e: e.reduce_max(
            stat[0:1, 16:17], h_y[0:1, :], axis=AX.X, apply_absolute_value=True),
            inc=dv)
        S.op("vector", lambda e: e.tensor_scalar(
            out=stat[0:1, 16:17], in0=stat[0:1, 16:17], scalar1=EPS, scalar2=None,
            op0=ALU.max), inc=dv)
        S.op("vector", lambda e: e.reciprocal(stat[0:1, 17:18], stat[0:1, 16:17]),
             inc=dv)
        S.op("vector", lambda e: e.tensor_scalar(
            out=stat[0:1, 17:18], in0=stat[0:1, 17:18], scalar1=127.0, scalar2=None,
            op0=ALU.mult), inc=dv)
        S.op("vector", lambda e: e.tensor_scalar(
            out=stat[0:1, 16:17], in0=stat[0:1, 16:17], scalar1=1.0 / 127.0,
            scalar2=None, op0=ALU.mult), inc=dv)
        for hf in range(2):
            S.op("vector", lambda e, hf=hf: e.tensor_scalar(
                out=h_q[0:1, :], in0=h_y[0:1, hf * 512:(hf + 1) * 512],
                scalar1=stat[0:1, 17:18], scalar2=None, op0=ALU.mult), inc=dv)
            S.op("vector", lambda e, hf=hf: e.tensor_scalar(
                out=h_xq[0:1, hf * 512:(hf + 1) * 512], in0=h_q[0:1, :],
                scalar1=stat[0:1, 16:17], scalar2=None, op0=ALU.mult), inc=dv)
        ndv = S.n(dv)
        nprev_ac = S.n(ac)
        for b4 in range(2):
            wl = [(dv, ndv), (ac, nprev_ac)]
            for s4 in range(4):
                c = b4 * 4 + s4
                S.op("tensor", lambda e, s4=s4, c=c: e.transpose(
                    B[7][:, s4 * 128:s4 * 128 + 1],
                    h_xq[0:1, c * 128:(c + 1) * 128], c_idb[0:1, 0:1]),
                    wait=wl if s4 == 0 else (), inc=pe)
            npe = S.n(pe)
            for s4 in range(4):
                c = b4 * 4 + s4
                S.op("scalar", lambda e, s4=s4, c=c: e.activation(
                    out=env["hal_xqdT"][:, c * 2 + which: c * 2 + which + 1],
                    in_=B[7][:, s4 * 128:s4 * 128 + 1], func=AF.Copy),
                    wait=((pe, npe),) if s4 == 0 else (), inc=ac)
            nprev_ac = S.n(ac)


def _prep_inputs(inp):
    bf = ml_dtypes.bfloat16
    x = np.ascontiguousarray(np.asarray(inp["L0_attn_x"], np.float32))
    f32 = lambda a: np.ascontiguousarray(np.asarray(a, np.float32))
    col = lambda a: np.ascontiguousarray(f32(a).reshape(128, 1))
    aff = np.concatenate([f32(inp[k]).reshape(128, 1) for k in (
        "L0_attn_g_qq", "L0_attn_b_qq", "L0_attn_g_qk", "L0_attn_b_qk",
        "L0_attn_g_lq", "L0_attn_b_lq", "L0_attn_g_lk", "L0_attn_b_lk")], axis=1)
    cw = f32(inp["L0_ffn_conv_w"])
    cwc = np.concatenate(
        [np.ascontiguousarray(cw[:, 0, k].reshape(32, 128).T) for k in range(3)]
        + [np.ascontiguousarray(f32(inp["L0_ffn_conv_b"]).reshape(32, 128).T)], axis=1)
    m = np.arange(128)[:, None]
    n = np.arange(GRP)[None, :]
    mask01 = np.concatenate([(n >= m).astype(bf), (n >= m + 128).astype(bf)], axis=1)
    shared = {
        "wv": f32(inp["L0_attn_Wv"]), "wg": f32(inp["L0_attn_Wg"]),
        "wi": f32(inp["L0_attn_Win"]), "wo": f32(inp["L0_attn_Wout"]),
        "bvr": f32(inp["L0_attn_bv"]).reshape(1, E),
        "bgr": f32(inp["L0_attn_bg"]).reshape(1, E),
        "bgc": np.ascontiguousarray(f32(inp["L0_attn_bg"]).reshape(16, 128).T),
        "binc": col(inp["L0_attn_bin_"]),
        "boutr": f32(inp["L0_attn_bout"]).reshape(1, D),
        "affc": np.ascontiguousarray(aff),
        "w1t": np.ascontiguousarray(f32(inp["L0_ffn_w1"]).T),
        "w2t": np.ascontiguousarray(f32(inp["L0_ffn_w2"]).T),
        "cwc": np.ascontiguousarray(cwc),
        "identf": np.eye(128, dtype=np.float32),
        "identb": np.eye(128, dtype=bf),
        "onesb": np.ones((1, 128), dtype=bf),
        "onesf": np.ones((1, 128), dtype=np.float32),
        "onesm": np.ones((128, 256), dtype=np.float32),
        "mask01": np.ascontiguousarray(mask01),
    }
    in_maps = []
    for core in range(8):
        b, half = divmod(core, 2)
        t0 = half * NMAIN
        hL = x[b, t0 - 1] if t0 > 0 else np.zeros(D, np.float32)
        hR = x[b, t0 + NMAIN] if t0 + NMAIN < 8192 else np.zeros(D, np.float32)
        xhv = np.stack([hL, hR]).astype(np.float32)
        xhT = np.ascontiguousarray(
            xhv.T.reshape(8, 128, 2).transpose(1, 0, 2).reshape(128, 16))
        im = dict(shared)
        im.update({
            "xm": np.ascontiguousarray(x[b, t0:t0 + NMAIN]),
            "xp": np.ascontiguousarray(x[b, 0:NPRE]),
            "xh": np.ascontiguousarray(xhv), "xhT": xhT,
            "pmask": np.full((128, 1), 1.0 if half == 1 else 0.0, np.float32),
            "hmv": np.array([[1.0 if half == 1 else 0.0],
                             [1.0 if half == 0 else 0.0]], np.float32),
        })
        in_maps.append(im)
    return in_maps


_NC_CACHE = {}


def kernel(**inputs):
    if "nc" not in _NC_CACHE:
        _NC_CACHE["nc"] = build_nc()
    nc = _NC_CACHE["nc"]
    in_maps = _prep_inputs(inputs)
    res = run_bass_kernel_spmd(nc, in_maps, core_ids=list(range(8)))
    outv = np.empty((4, 8192, D), np.float32)
    for core in range(8):
        b, half = divmod(core, 2)
        outv[b, half * NMAIN:(half + 1) * NMAIN] = res.results[core]["out"]
    return outv


# version 23
# speedup vs baseline: 1.0017x; 1.0017x over previous
import contextlib

import numpy as np
import ml_dtypes

import concourse.bass as bass
import concourse.mybir as mybir
from concourse.bass_utils import run_bass_kernel_spmd

F32 = mybir.dt.float32
BF16 = mybir.dt.bfloat16
I32 = mybir.dt.int32
I8 = mybir.dt.int8
AF = mybir.ActivationFunctionType
ALU = mybir.AluOpType
AX = mybir.AxisListType

D = 1024
E = 2048
HD = 128
GRP = 256
DH = 4096
NMAIN = 4096
NPRE = 4096
FCH = 256
SL = FCH + 2
INV_SCALE = 1.0 / float(np.sqrt(2048.0))
EPS = 1e-5
WN = D * DH

ENGS = ("sync", "tensor", "scalar", "vector", "gpsimd")


def bcast0(t2d, col0, ncols, reps):
    v = t2d[:, col0:col0 + ncols].rearrange("p (c u) -> p c u", u=1)
    ap = [list(p) for p in v.ap]
    ap[-1] = [0, reps]
    return bass.AP(tensor=v.tensor, offset=v.offset, ap=ap)


class Sched:

    def __init__(self, nc, ctx):
        self.nc = nc
        self.ctx = ctx
        self.q = {e: [] for e in ENGS}
        self.sem = {}
        self.cnt = {}
        self.dma_streams = set()
        for name in ("pe", "ac", "dv", "bar"):
            self._sem(name)
        self.nbar = 0

    def _sem(self, name):
        if name not in self.sem:
            self.sem[name] = self.ctx.enter_context(self.nc.semaphore("sem_" + name))
            self.cnt[name] = 0
        return self.sem[name]

    ENG_SEM = {"tensor": "pe", "scalar": "ac", "vector": "dv"}

    def op(self, eng, fn, wait=(), inc=None, inc_by=1, sw=None):
        if sw is None and eng in ("vector", "scalar"):
            sw = True
        if sw:
            es = self.ENG_SEM[eng]
            wait = tuple(wait) + ((es, self.cnt[es]),)
        w = tuple((self._sem(s), val) for s, val in wait if val > 0)
        if inc is not None:
            self._sem(inc)
            self.cnt[inc] += inc_by
        self.q[eng].append((fn, w, (self.sem[inc], inc_by) if inc else None))
        return self.cnt[inc] if inc else None

    def dma(self, eng, out, in_, stream, wait=(), serial=False):
        self._sem(stream)
        self.dma_streams.add(stream)
        if serial:
            wait = tuple(wait) + ((stream, self.cnt[stream]),)
        return self.op(eng, lambda e, o=out, i=in_: e.dma_start(out=o, in_=i),
                       wait=wait, inc=stream, inc_by=16)

    def n(self, name):
        self._sem(name)
        return self.cnt[name]

    def dbg_fence(self):
        w = (("dbg", self.cnt.get("dbg", 0)),)
        for eng in ("tensor", "scalar", "vector"):
            self.op(eng, lambda e: None, wait=w)

    def barrier(self):
        waits = tuple((s, self.cnt[s]) for s in sorted(self.dma_streams))
        self.op("sync", lambda e: None, wait=waits)
        for eng in ENGS:
            self.op(eng, lambda e: None, inc="bar")
        self.nbar += 5
        for eng in ENGS:
            self.op(eng, lambda e: None, wait=(("bar", self.nbar),))

    def emit(self, block):
        for eng in ENGS:
            items = self.q[eng]

            def body(handle, items=items):
                for fn, waits, inc in items:
                    ws = list(waits)
                    while len(ws) > 2:
                        handle.wait_ge(*ws[0])
                        handle.wait_ge(*ws[1])
                        handle.nop()
                        ws = ws[2:]
                    for sem, val in ws:
                        handle.wait_ge(sem, val)
                    instr = fn(handle)
                    if inc is not None:
                        sem, by = inc
                        if instr is None:
                            handle.sem_inc(sem, by)
                        else:
                            instr.then_inc(sem, by)
                    elif instr is None and ws:
                        handle.nop()

            getattr(block, eng)(body)


def build_nc(n_pre_groups=16, n_main_groups=16, debug=False):
    NGP = n_pre_groups
    NG = n_main_groups
    NTOK = NG * GRP
    NT = NTOK * 0 + NG * 2
    NFC = NTOK // FCH
    nc = bass.Bass("TRN2", target_bir_lowering=False)

    dtn = lambda n, s, d, k: nc.dram_tensor(n, s, d, kind=k).ap()
    xm = dtn("xm", [NTOK, D], F32, "ExternalInput")
    xp = dtn("xp", [NGP * GRP, D], F32, "ExternalInput")
    xhT = dtn("xhT", [128, 16], F32, "ExternalInput")
    xh = dtn("xh", [2, D], F32, "ExternalInput")
    wv = dtn("wv", [D, E], F32, "ExternalInput")
    wg = dtn("wg", [D, E], F32, "ExternalInput")
    wi = dtn("wi", [D, HD], F32, "ExternalInput")
    wo = dtn("wo", [E, D], F32, "ExternalInput")
    bvr = dtn("bvr", [1, E], F32, "ExternalInput")
    bgr = dtn("bgr", [1, E], F32, "ExternalInput")
    bgc = dtn("bgc", [128, 16], F32, "ExternalInput")
    binc = dtn("binc", [128, 1], F32, "ExternalInput")
    boutr = dtn("boutr", [1, D], F32, "ExternalInput")
    affc = dtn("affc", [128, 8], F32, "ExternalInput")
    w1t = dtn("w1t", [D, DH], F32, "ExternalInput")
    w2t = dtn("w2t", [DH, D], F32, "ExternalInput")
    cwc = dtn("cwc", [128, 128], F32, "ExternalInput")
    identf = dtn("identf", [128, 128], F32, "ExternalInput")
    identb = dtn("identb", [128, 128], BF16, "ExternalInput")
    onesb = dtn("onesb", [1, 128], BF16, "ExternalInput")
    onesf = dtn("onesf", [1, 128], F32, "ExternalInput")
    onesm = dtn("onesm", [128, 256], F32, "ExternalInput")
    mask01 = dtn("mask01", [128, 2 * GRP], BF16, "ExternalInput")
    pmask = dtn("pmask", [128, 1], F32, "ExternalInput")
    hmv = dtn("hmv", [2, 1], F32, "ExternalInput")

    out = dtn("out", [NTOK, D], F32, "ExternalOutput")

    w1q_d = dtn("w1q_d", [8, 128, DH], BF16, "Internal")
    w2q_d = dtn("w2q_d", [32, 128, D], BF16, "Internal")
    x1_d = dtn("x1_d", [NT, 128, D], F32, "Internal")
    xqdT_d = dtn("xqdT_d", [128, 8, NTOK], BF16, "Internal")
    hal_d = dtn("hal_d", [2, DH], BF16, "Internal")

    dbg = {}
    if debug:
        dbg["S"] = dtn("dbg_S", [128, E], F32, "ExternalOutput")
        dbg["x1"] = dtn("dbg_x1", [NT, 128, D], F32, "ExternalOutput")
        dbg["xqdT"] = dtn("dbg_xqdT", [128, 8, NTOK], BF16, "ExternalOutput")
        dbg["halxq"] = dtn("dbg_halxq", [128, 16], BF16, "ExternalOutput")
        dbg["at"] = dtn("dbg_at", [128, 2 * GRP], BF16, "ExternalOutput")
        dbg["stat"] = dtn("dbg_stat", [128, 32], F32, "ExternalOutput")
        dbg["h"] = dtn("dbg_h", [128, 32 * SL], BF16, "ExternalOutput")
        dbg["hs"] = dtn("dbg_hs", [128, 32 * SL], BF16, "ExternalOutput")
        dbg["xq2"] = dtn("dbg_xq2", [128, 32 * FCH], BF16, "ExternalOutput")
        dbg["fst"] = dtn("dbg_fst", [128, 32], F32, "ExternalOutput")
        dbg["y2"] = dtn("dbg_y2", [128, D], F32, "ExternalOutput")
        dbg["fst2"] = dtn("dbg_fst2", [128, 32], F32, "ExternalOutput")
        dbg["wsc"] = dtn("dbg_wsc", [128, 4], F32, "ExternalOutput")
        dbg["pre"] = dtn("dbg_pre", [128, 32], F32, "ExternalOutput")
        dbg["pw0"] = dtn("dbg_pw0", [128, DH], F32, "ExternalOutput")
        dbg["y"] = dtn("dbg_y", [128, D], F32, "ExternalOutput")
        dbg["o"] = dtn("dbg_o", [128, 2 * GRP], BF16, "ExternalOutput")
        dbg["vt"] = dtn("dbg_vt", [128, 2 * E], BF16, "ExternalOutput")
        dbg["gate"] = dtn("dbg_gate", [128, 2 * GRP], BF16, "ExternalOutput")

    with contextlib.ExitStack() as root:
        S = Sched(nc, root)
        sb = lambda n, s, d: root.enter_context(nc.sbuf_tensor(n, s, d))

        c_idf = sb("c_idf", [128, 128], F32)
        c_idb = sb("c_idb", [128, 128], BF16)
        c_ones = sb("c_ones", [1, 128], BF16)
        c_onesf = sb("c_onesf", [1, 128], F32)
        c_onesm = sb("c_onesm", [128, 256], F32)
        c_hm = sb("c_hm", [2, 1], F32)
        stat = sb("stat", [128, 32], F32)
        hal_xqdT = sb("hal_xqdT", [128, 16], BF16)
        p_cw = sb("p_cw", [128, 128], F32)
        wsc = sb("wsc", [128, 4], F32)

        B = [root.enter_context(nc.psum_tensor(f"bk{i}", [128, 512], F32))
             for i in range(7)]
        B.append(root.enter_context(nc.psum_tensor("bk7", [128, 1024], BF16)))

        pe, ac, dv = "pe", "ac", "dv"

        env = {}

        with contextlib.ExitStack() as attn_scope:
            asb = lambda n, s, d: attn_scope.enter_context(nc.sbuf_tensor(n, s, d))
            c_m01 = asb("c_m01", [128, 2 * GRP], BF16)
            c_pm = asb("c_pm", [128, 1], F32)
            p_aff = asb("p_aff", [128, 8], F32)

            for dst, srcx in ((c_idf, identf), (c_idb, identb), (c_ones, onesb),
                              (c_onesf, onesf), (c_onesm, onesm), (c_hm, hmv),
                              (c_m01, mask01), (c_pm, pmask), (p_aff, affc),
                              (p_cw, cwc)):
                S.dma("sync", dst[:, :], srcx[:, :], "lc")
            n_const = S.n("lc")

            S.op("vector", lambda e: e.tensor_scalar(
                out=p_aff[:, 0:2], in0=p_aff[:, 0:2], scalar1=INV_SCALE,
                scalar2=None, op0=ALU.mult), wait=(("lc", n_const),), inc=dv)
            if debug:
                S.op("vector", lambda e: e.memset(stat[:, :], 0.0), inc=dv)

            p_bv = asb("p_bv", [1, E], BF16)
            p_bg = asb("p_bg", [1, E], BF16)
            p_bgc = asb("p_bgc", [128, 16], F32)
            p_bout = asb("p_bout", [1, D], BF16)
            w_v = asb("w_v", [128, 8 * E], BF16)
            w_i = asb("w_i", [128, 8 * HD], BF16)
            hal_xT = asb("hal_xT", [128, 16], BF16)
            xbuf = asb("xbuf", [128, 4 * D], F32)
            xT = asb("xT", [128, 2 * 8 * GRP], BF16)
            v_tm = asb("v_tm", [128, 4 * E], BF16)
            x_fm = asb("x_fm", [128, GRP], BF16)
            qq_f = asb("qq_f", [128, GRP], BF16)
            qk_f = asb("qk_f", [128, GRP], BF16)
            lq_f = asb("lq_f", [128, 2 * GRP], BF16)
            lk_f = asb("lk_f", [128, GRP], BF16)
            lk_tm = asb("lk_tm", [128, 2 * 128], BF16)
            S_m = asb("S_m", [128, E], F32)
            S_bf = asb("S_bf", [128, E], BF16)
            Sh_bf = asb("Sh_bf", [128, E], BF16)
            binc_col = asb("binc_col", [128, 1], F32)
            S.dma("sync", p_bgc[:, :], bgc[:, :], "lc", serial=True)
            S.dma("sync", binc_col[:, :], binc[:, :], "lc", serial=True)
            for k in range(8):
                S.dma("gpsimd", w_v[:, k * E:(k + 1) * E], wv[k * 128:(k + 1) * 128, :], "gw")
            for k in range(8):
                S.dma("gpsimd", w_i[:, k * HD:(k + 1) * HD], wi[k * 128:(k + 1) * 128, :], "gw")
            for dstt, srcx in ((p_bv, bvr), (p_bg, bgr), (p_bout, boutr)):
                S.dma("gpsimd", dstt[:, :], srcx[:, :], "gw")
            S.dma("gpsimd", hal_xT[:, :], xhT[:, :], "gw")
            n_wload_pre = S.n("gw")

            pre_scope = contextlib.ExitStack()
            qsb = lambda n, s, d: pre_scope.enter_context(nc.sbuf_tensor(n, s, d))
            pw = [qsb(f"pw{i}", [128, DH], F32) for i in range(4)]
            pi = qsb("pi", [128, DH // 2], I32)
            pb = qsb("pb", [128, DH // 2], BF16)
            pslices = []
            prest = {"dv": 0}

            def ps_load(wsrc, cs, dst_of, wait_dv):
                def f():
                    wl = ((dv, S.n(dv)),) if wait_dv else ()
                    for i, c in enumerate(cs):
                        t, sl = dst_of(c)
                        S.dma("gpsimd", pw[t][:, sl], wsrc[c * 128:(c + 1) * 128, :],
                              "lq", wait=wl if i == 0 else ())
                return f

            def ps_sums(scol0):
                def f():
                    nld = S.n("lq")
                    for t in range(4):
                        S.op("vector", lambda e, t=t, scol0=scol0: e.reduce_sum(
                            stat[:, scol0 + t:scol0 + t + 1], pw[t][:, :], axis=AX.X,
                            apply_absolute_value=True),
                            wait=(("lq", nld), (ac, S.n(ac))) if t == 0 else (),
                            inc=dv, sw=False)
                return f

            def ps_scale(col):
                def f():
                    ndv = S.n(dv)
                    S.op("tensor", lambda e, col=col: e.matmul(
                        B[1][:, 0:8], c_onesm[:, col * 64:col * 64 + 128],
                        stat[:, 16:24], start=True, stop=True),
                        wait=((dv, ndv), ("lc", n_const), (ac, S.n(ac))), inc=pe)
                    npe = S.n(pe)
                    S.op("scalar", lambda e: e.activation(
                        out=stat[:, 16:24], in_=B[1][:, 0:8], func=AF.Copy),
                        wait=((pe, npe),), inc=ac)
                    nac0 = S.n(ac)
                    S.op("vector", lambda e: e.reduce_sum(
                        stat[:, 25:26], stat[:, 16:24], axis=AX.X),
                        wait=((ac, nac0),), inc=dv, sw=True)
                    S.op("vector", lambda e: e.tensor_scalar(
                        out=stat[:, 26:27], in0=stat[:, 25:26],
                        scalar1=1.0 / WN, scalar2=EPS, op0=ALU.mult, op1=ALU.max),
                        inc=dv, sw=True)
                    S.op("vector", lambda e: e.reciprocal(
                        stat[:, 27:28], stat[:, 26:27]), inc=dv, sw=True)
                    S.op("vector", lambda e, col=col: e.tensor_copy(
                        wsc[:, col:col + 1], stat[:, 27:28]), inc=dv, sw=True)
                    S.op("vector", lambda e, col=col: e.tensor_copy(
                        wsc[:, col + 1:col + 2], stat[:, 26:27]), inc=dv, sw=True)
                return f

            def ps_quant(wdst, cs, src_of, col, qn):
                def f():
                    nlq = S.n("lq")
                    first = True
                    for c in cs:
                        t, off, dsl = src_of(c)
                        for h in range((off[1] - off[0]) // qn):
                            a = off[0] + h * qn
                            n_stp = S.n("sw")
                            wl = [("sw", n_stp)]
                            if first:
                                wl.append(("lq", nlq))
                                first = False
                            S.op("vector", lambda e, t=t, a=a, col=col: e.tensor_scalar(
                                out=pi[:, 0:qn], in0=pw[t][:, a:a + qn],
                                scalar1=wsc[:, col:col + 1], scalar2=None,
                                op0=ALU.mult),
                                wait=wl, inc=dv)
                            S.op("vector", lambda e: e.tensor_scalar(
                                out=pb[:, 0:qn], in0=pi[:, 0:qn],
                                scalar1=-1.0, scalar2=1.0, op0=ALU.max, op1=ALU.min),
                                inc=dv)
                            ndv = S.n(dv)
                            S.dma("sync", wdst[c, :, dsl[0] + h * qn: dsl[0] + (h + 1) * qn],
                                  pb[:, 0:qn], "sw", wait=((dv, ndv),))
                    prest["dv"] = S.n(dv)
                return f

            QN = DH // 2
            w1_dst = lambda c: (c % 4, slice(0, DH))
            w1_src = lambda c: (c % 4, (0, DH), (0, DH))
            w2_dst = lambda c: ((c % 16) // 4, slice((c % 4) * D, (c % 4 + 1) * D))
            w2_src = lambda c: ((c % 16) // 4, ((c % 4) * D, (c % 4 + 1) * D), (0, D))
            pslices.append(ps_load(w1t, range(0, 4), w1_dst, False))
            pslices.append(lambda f1=ps_sums(16), f2=ps_load(w1t, range(4, 8), w1_dst, True): (f1(), f2()))
            pslices.append(lambda f1=ps_sums(20), f2=ps_scale(0): (f1(), f2()))
            pslices.append(ps_quant(w1q_d, range(4, 8), w1_src, 0, QN))
            pslices.append(ps_load(w1t, range(0, 4), w1_dst, True))
            pslices.append(ps_quant(w1q_d, range(0, 4), w1_src, 0, QN))
            pslices.append(ps_load(w2t, range(0, 16), w2_dst, True))
            pslices.append(ps_sums(16))
            pslices.append(ps_load(w2t, range(16, 32), w2_dst, True))
            pslices.append(lambda f1=ps_sums(20), f2=ps_scale(2): (f1(), f2()))
            pslices.append(ps_quant(w2q_d, range(16, 24), w2_src, 2, D))
            pslices.append(ps_quant(w2q_d, range(24, 32), w2_src, 2, D))
            pslices.append(ps_load(w2t, range(0, 16), w2_dst, True))
            pslices.append(ps_quant(w2q_d, range(0, 8), w2_src, 2, D))
            pslices.append(lambda f1=ps_quant(w2q_d, range(8, 16), w2_src, 2, D),
                           f2=(lambda: S.op("vector", lambda e: e.tensor_scalar(
                               out=p_cw[:, 0:96], in0=p_cw[:, 0:96],
                               scalar1=wsc[:, 1:2], scalar2=None, op0=ALU.mult),
                               wait=((ac, S.n(ac)),), inc=dv)): (f1(), f2()))

            S.op("vector", lambda e: e.memset(S_m[:, :], 0.0), inc=dv)


            st8 = {}

            def attn_ln1_a(j):
                gi_, ndv_ = st8[f"lnp{j}"]
                sc = j * 8
                xqb = xqd_t if j == 0 else xqd_t2
                S.op("scalar", lambda e, sc=sc, j=j, xqb=xqb: e.activation(
                    out=xqb[:, :], in_=x1_t[:, j * D:(j + 1) * D], func=AF.Square,
                    accum_out=stat[:, sc + 2:sc + 3]),
                    wait=((dv, max(ndv_, st8.get(f"xqd_last{j}", 0))),
                          (pe, st8.get(f"xqdT_pe{j}", 0))), inc=ac)
                nac = S.n(ac)
                S.op("vector", lambda e, sc=sc: e.tensor_add(
                    stat[:, sc + 3:sc + 4], stat[:, sc:sc + 1],
                    stat[:, sc + 1:sc + 2]), wait=((ac, nac),), inc=dv)
                S.op("vector", lambda e, sc=sc: e.tensor_scalar(
                    out=stat[:, sc + 3:sc + 4], in0=stat[:, sc + 3:sc + 4],
                    scalar1=1.0 / D, scalar2=None, op0=ALU.mult), inc=dv)
                S.op("vector", lambda e, sc=sc: e.tensor_scalar(
                    out=stat[:, sc + 4:sc + 5], in0=stat[:, sc + 2:sc + 3],
                    scalar1=1.0 / D, scalar2=None, op0=ALU.mult), inc=dv)
                S.op("vector", lambda e, sc=sc: e.tensor_mul(
                    stat[:, sc + 5:sc + 6], stat[:, sc + 3:sc + 4],
                    stat[:, sc + 3:sc + 4]), inc=dv)
                S.op("vector", lambda e, sc=sc: e.tensor_sub(
                    stat[:, sc + 5:sc + 6], stat[:, sc + 4:sc + 5],
                    stat[:, sc + 5:sc + 6]), inc=dv)
                S.op("vector", lambda e, sc=sc: e.tensor_scalar(
                    out=stat[:, sc + 5:sc + 6], in0=stat[:, sc + 5:sc + 6],
                    scalar1=EPS, scalar2=None, op0=ALU.add), inc=dv)
                ndv = S.n(dv)
                S.op("scalar", lambda e, sc=sc: e.activation(
                    out=stat[:, sc + 6:sc + 7], in_=stat[:, sc + 5:sc + 6],
                    func=AF.Sqrt),
                    wait=((dv, ndv),), inc=ac)
                nac = S.n(ac)
                S.op("vector", lambda e, sc=sc: e.reciprocal(
                    stat[:, sc + 7:sc + 8], stat[:, sc + 6:sc + 7]),
                    wait=((ac, nac),), inc=dv)
                S.op("vector", lambda e, sc=sc, j=j: e.tensor_scalar(
                    out=x1_t[:, j * D:(j + 1) * D], in0=x1_t[:, j * D:(j + 1) * D],
                    scalar1=stat[:, sc + 3:sc + 4], scalar2=stat[:, sc + 7:sc + 8],
                    op0=ALU.subtract, op1=ALU.mult), inc=dv)
                S.op("vector", lambda e, sc=sc, j=j: e.reduce_max(
                    stat[:, sc + 4:sc + 5], x1_t[:, j * D:(j + 1) * D],
                    axis=AX.X, apply_absolute_value=True), inc=dv)
                S.op("vector", lambda e, sc=sc: e.tensor_scalar(
                    out=stat[:, sc + 4:sc + 5], in0=stat[:, sc + 4:sc + 5],
                    scalar1=EPS, scalar2=None, op0=ALU.max), inc=dv)
                S.op("vector", lambda e, sc=sc: e.reciprocal(
                    stat[:, sc + 5:sc + 6], stat[:, sc + 4:sc + 5]), inc=dv)
                S.op("vector", lambda e, sc=sc: e.tensor_scalar(
                    out=stat[:, sc + 5:sc + 6], in0=stat[:, sc + 5:sc + 6],
                    scalar1=127.0, scalar2=None, op0=ALU.mult), inc=dv)
                S.op("vector", lambda e, sc=sc: e.tensor_scalar(
                    out=stat[:, sc + 4:sc + 5], in0=stat[:, sc + 4:sc + 5],
                    scalar1=1.0 / 127.0, scalar2=None, op0=ALU.mult), inc=dv)
                for hf in range(2):
                    S.op("vector", lambda e, sc=sc, j=j, hf=hf: e.tensor_scalar(
                        out=xqi_t[:, :],
                        in0=x1_t[:, j * D + hf * 512: j * D + (hf + 1) * 512],
                        scalar1=stat[:, sc + 5:sc + 6], scalar2=None, op0=ALU.mult),
                        inc=dv)
                    S.op("vector", lambda e, sc=sc, hf=hf, xqb=xqb: e.tensor_scalar(
                        out=xqb[:, hf * 512:(hf + 1) * 512], in0=xqi_t[:, :],
                        scalar1=stat[:, sc + 4:sc + 5], scalar2=None, op0=ALU.mult),
                        inc=dv)
                st8[f"lnq{j}"] = S.n(dv)
                st8[f"xqd_last{j}"] = S.n(dv)

            def attn_ln1_b(j):
                gi_, _ = st8.pop(f"lnp{j}")
                ndv = st8[f"lnq{j}"]
                xqb = xqd_t if j == 0 else xqd_t2
                wl = [(dv, ndv), (ac, st8.get("b7_ac", 0))]
                for c in range(8):
                    S.op("tensor", lambda e, c=c, xqb=xqb: e.transpose(
                        B[7][:, c * 128:(c + 1) * 128],
                        xqb[:, c * 128:(c + 1) * 128], c_idb[:, :]),
                        wait=wl if c == 0 else (), inc=pe)
                npe = S.n(pe)
                st8[f"xqdT_pe{j}"] = npe
                for c in range(8):
                    S.op("scalar", lambda e, c=c, j=j: e.activation(
                        out=xqdT_t[:, c * 128:(c + 1) * 128],
                        in_=B[7][:, c * 128:(c + 1) * 128], func=AF.Copy),
                        wait=((pe, npe), ("sq0", st8.get("xqst0", 0)),
                              ("sq1", st8.get("xqst1", 0))) if c == 0 else (),
                        inc=ac)
                st8["b7_ac"] = S.n(ac)
                nac = S.n(ac)
                ndva = S.n(dv)
                tile = gi_ * 2 + j
                S.dma("sync", x1_d[tile, :, :], x1_t[:, j * D:(j + 1) * D],
                      f"s1{j}", wait=((dv, ndva),))
                if debug:
                    S.dma("sync", dbg["x1"][tile, :, :], x1_t[:, j * D:(j + 1) * D],
                          "dbg", serial=True)
                    S.dbg_fence()
                st8[f"x1st{j}"] = S.n(f"s1{j}")
                S.dma("sync",
                      xqdT_d[:, :, gi_ * GRP + j * 128: gi_ * GRP + j * 128 + 128],
                      xqdT_t[:, 0:D].rearrange("p (c t) -> p c t", c=8),
                      f"sq{j}", wait=((ac, nac),))
                if debug:
                    S.dma("sync",
                          dbg["xqdT"][:, :, gi_ * GRP + j * 128: gi_ * GRP + j * 128 + 128],
                          xqdT_t[:, 0:D].rearrange("p (c t) -> p c t", c=8),
                          "dbg", serial=True)
                    S.dbg_fence()
                st8[f"xqst{j}"] = S.n(f"sq{j}")

            def emit_group(gi, src_ap, full):
                gb = gi % 2
                xs = f"xm{gb}" if full else f"x{gb}"
                pre = st8.pop(f"xpre{gb}", None)
                if pre is None:
                    for j in range(2):
                        S.dma("gpsimd" if full else "sync",
                              xbuf[:, (gb * 2 + j) * D:(gb * 2 + j + 1) * D],
                              src_ap[gi * GRP + j * 128: gi * GRP + (j + 1) * 128, :],
                              xs,
                              wait=((dv, st8.get(f"xres{gb}", 0)),
                                    (pe, st8.get(f"xfree{gb}", 0)),))
                    nx = S.n(xs)
                else:
                    nx = pre
                for b4 in range(4):
                    bank = B[2 + b4 % 2]
                    wl = [(xs, nx), (ac, st8.get(f"vb{b4 % 2}_ac", 0))]
                    for s4 in range(4):
                        idx = b4 * 4 + s4
                        j, c = divmod(idx, 8)
                        S.op("tensor", lambda e, bank=bank, s4=s4, j=j, c=c, gb=gb: e.transpose(
                            bank[:, s4 * 128:(s4 + 1) * 128],
                            xbuf[:, (gb * 2 + j) * D + c * 128:(gb * 2 + j) * D + (c + 1) * 128],
                            c_idf[:, :]),
                            wait=wl if s4 == 0 else (), inc=pe)
                    npe = S.n(pe)
                    for s4 in range(4):
                        idx = b4 * 4 + s4
                        j, c = divmod(idx, 8)
                        S.op("scalar", lambda e, bank=bank, s4=s4, j=j, c=c, gb=gb: e.activation(
                            out=xT[:, gb * 8 * GRP + c * GRP + j * 128: gb * 8 * GRP + c * GRP + j * 128 + 128],
                            in_=bank[:, s4 * 128:(s4 + 1) * 128], func=AF.Copy),
                            wait=((pe, npe),) if s4 == 0 else (), inc=ac)
                    st8[f"vb{b4 % 2}_ac"] = S.n(ac)
                n_xT = S.n(ac)
                xTb = gb * 8 * GRP
                if full:
                    for j in range(2):
                        if st8.get(f"lnp{j}") is not None:
                            attn_ln1_a(j)

                for j in range(2):
                    for e4 in range(4):
                        bk = (j * 4 + e4) % 2
                        bank = B[2 + bk]
                        wl = [(ac, n_xT), (ac, st8.get(f"vb{bk}_ac", 0))]
                        for c in range(8):
                            S.op("tensor", lambda e, bank=bank, j=j, c=c, e4=e4, xTb=xTb: e.matmul(
                                bank[:, :],
                                xT[:, xTb + c * GRP + j * 128: xTb + c * GRP + j * 128 + 128],
                                w_v[:, c * E + e4 * 512: c * E + (e4 + 1) * 512],
                                start=(c == 0), stop=False),
                                wait=wl if c == 0 else (), inc=pe)
                        S.op("tensor", lambda e, bank=bank, e4=e4: e.matmul(
                            bank[:, :], c_ones[0:1, :],
                            p_bv[0:1, e4 * 512:(e4 + 1) * 512], start=False, stop=True), inc=pe)
                        npe = S.n(pe)
                        S.op("scalar", lambda e, bank=bank, j=j, e4=e4, gb=gb: e.activation(
                            out=v_tm[:, gb * 2 * E + j * E + e4 * 512: gb * 2 * E + j * E + (e4 + 1) * 512],
                            in_=bank[:, :], func=AF.Silu),
                            wait=((pe, npe), (pe, st8.get(f"vfree{gb}", 0))),
                            inc=ac)
                        st8[f"vb{bk}_ac"] = S.n(ac)
                n_v = S.n(ac)
                vb = gb * 2 * E
                if full:
                    for j in range(2):
                        if st8.get(f"lnp{j}") is not None:
                            attn_ln1_b(j)

                wl = [(ac, st8.get("vb0_ac", 0))]
                for c in range(8):
                    S.op("tensor", lambda e, c=c, xTb=xTb: e.matmul(
                        B[2][:, 0:GRP],
                        w_i[:, c * HD: c * HD + 128],
                        xT[:, xTb + c * GRP: xTb + (c + 1) * GRP],
                        start=(c == 0), stop=(c == 7)),
                        wait=wl if c == 0 else (), inc=pe)
                npe = S.n(pe)
                S.op("scalar", lambda e: e.activation(
                    out=x_fm[:, :], in_=B[2][:, 0:GRP], func=AF.Silu,
                    bias=binc_col[:, 0:1], scale=1.0),
                    wait=((pe, npe), (dv, st8.get("xfm_dv", 0)),
                          (ac, st8.get("xfm_ac", 0))), inc=ac)
                st8["vb0_ac"] = S.n(ac)
                n_xf = S.n(ac)

                S.op("scalar", lambda e: e.activation(
                    out=lk_f[:, :], in_=x_fm[:, :], func=AF.Identity,
                    bias=p_aff[:, 7:8], scale=p_aff[:, 6:7]),
                    wait=((ac, n_xf),), inc=ac)
                n_lkac = S.n(ac)
                st8["xfm_ac"] = n_lkac
                if full:
                    for ai, (dstt, gcol) in enumerate(((qq_f[:, 0:GRP], 0), (qk_f[:, 0:GRP], 2),
                                       (lq_f[:, gb * GRP:(gb + 1) * GRP], 4))):
                        S.op("vector", lambda e, dstt=dstt, gcol=gcol: e.tensor_scalar(
                            out=dstt, in0=x_fm[:, :],
                            scalar1=p_aff[:, gcol:gcol + 1],
                            scalar2=p_aff[:, gcol + 1:gcol + 2],
                            op0=ALU.mult, op1=ALU.add),
                            wait=((ac, n_xf),) if ai == 0 else (), inc=dv)
                n_aff = S.n(dv)
                st8["xfm_dv"] = n_aff

                wl = [(ac, max(n_lkac, st8.get("b7_ac", 0))),
                      (dv, st8.get("b7_dv", 0))]
                for cc in range(2):
                    S.op("tensor", lambda e, cc=cc: e.transpose(
                        B[7][:, cc * 128:(cc + 1) * 128], lk_f[:, cc * 128:(cc + 1) * 128],
                        c_idb[:, :]), wait=wl if cc == 0 else (), inc=pe)
                npe = S.n(pe)
                for cc in range(2):
                    S.op("scalar", lambda e, cc=cc: e.activation(
                        out=lk_tm[:, cc * 128:(cc + 1) * 128],
                        in_=B[7][:, cc * 128:(cc + 1) * 128], func=AF.Copy),
                        wait=((pe, npe),) if cc == 0 else (), inc=ac)
                st8["b7_ac"] = S.n(ac)
                n_lk = S.n(ac)

                if full:
                    S.op("vector", lambda e: e.tensor_copy(S_bf[:, :], S_m[:, :]),
                         wait=((pe, st8.get("sbf_pe", 0)),), inc=dv)
                n_scast = S.n(dv)

                for e4 in range(4):
                    wl = [(ac, n_v), (ac, n_lk), (dv, st8.get("kv_dv", 0)),
                          (dv, n_scast)]
                    for cc in range(2):
                        S.op("tensor", lambda e, cc=cc, e4=e4, vb=vb: e.matmul(
                            B[3][:, :],
                            lk_tm[:, cc * 128:(cc + 1) * 128],
                            v_tm[:, vb + cc * E + e4 * 512: vb + cc * E + (e4 + 1) * 512],
                            start=(cc == 0), stop=(cc == 1)),
                            wait=wl if cc == 0 else (), inc=pe)
                    npe = S.n(pe)
                    S.op("vector", lambda e, e4=e4: e.tensor_add(
                        S_m[:, e4 * 512:(e4 + 1) * 512],
                        S_m[:, e4 * 512:(e4 + 1) * 512], B[3][:, :]),
                        wait=((pe, npe),), inc=dv)
                    st8["kv_dv"] = S.n(dv)
                n_kv = S.n(dv)

                if not full:
                    st8[f"xfree{gb}"] = S.n(pe)
                    st8[f"xres{gb}"] = 0
                    st8[f"vfree{gb}"] = S.n(pe)
                    return

                wl = [(dv, n_aff), (ac, st8.get("b4_ac", 0)), (dv, st8.get("b4_dv", 0))]
                for mc in range(2):
                    S.op("tensor", lambda e, mc=mc: e.matmul(
                        B[4][:, mc * GRP:(mc + 1) * GRP],
                        qk_f[:, mc * 128:(mc + 1) * 128],
                        qq_f[:, 0:GRP], start=True, stop=True),
                        wait=wl if mc == 0 else (), inc=pe)
                npe = S.n(pe)
                for mc in range(2):
                    S.op("scalar", lambda e, mc=mc: e.activation(
                        out=at_bf[:, mc * GRP:(mc + 1) * GRP],
                        in_=B[4][:, mc * GRP:(mc + 1) * GRP], func=AF.Relu),
                        wait=((pe, npe),) if mc == 0 else (), inc=ac)
                st8["b4_ac"] = S.n(ac)
                n_rl = S.n(ac)
                for mc in range(2):
                    S.op("vector", lambda e, mc=mc: e.scalar_tensor_tensor(
                        out=at_bf[:, mc * GRP:(mc + 1) * GRP],
                        in0=at_bf[:, mc * GRP:(mc + 1) * GRP], scalar=1.0,
                        in1=at_bf[:, mc * GRP:(mc + 1) * GRP],
                        op0=ALU.mult, op1=ALU.mult),
                        wait=((ac, n_rl),) if mc == 0 else (), inc=dv)
                    S.op("vector", lambda e, mc=mc: e.tensor_mul(
                        at_bf[:, mc * GRP:(mc + 1) * GRP],
                        at_bf[:, mc * GRP:(mc + 1) * GRP],
                        c_m01[:, mc * GRP:(mc + 1) * GRP]), inc=dv)
                n_at = S.n(dv)

                for ec in range(16):
                    eb = ec % 2
                    gbk = B[5] if eb == 0 else B[0]
                    obk = B[4] if eb == 0 else B[6]
                    wl = [(ac, n_xT), (ac, st8.get(f"gb{eb}_ac", 0))]
                    if eb == 1 and ec == 1:
                        wl.append((dv, st8.get("resid0_dv", 0)))
                    if eb == 0 and ec == 0:
                        wl.append((dv, st8.get("resid1_dv", 0)))
                    for c in range(8):
                        S.op("tensor", lambda e, gbk=gbk, c=c, ec=ec, xTb=xTb: e.matmul(
                            gbk[:, 0:GRP],
                            w_g[:, c * E + ec * 128: c * E + (ec + 1) * 128],
                            xT[:, xTb + c * GRP: xTb + (c + 1) * GRP],
                            start=(c == 0), stop=(c == 7)),
                            wait=wl if c == 0 else (), inc=pe)
                    npe = S.n(pe)
                    S.op("scalar", lambda e, gbk=gbk, ec=ec, eb=eb: e.activation(
                        out=gate_p[:, eb * GRP:(eb + 1) * GRP],
                        in_=gbk[:, 0:GRP], func=AF.Silu,
                        bias=p_bgc[:, ec:ec + 1], scale=1.0),
                        wait=((pe, npe), (dv, st8.get(f"gate{eb}_dv", 0))), inc=ac)
                    st8[f"gb{eb}_ac"] = S.n(ac)
                    n_gate = S.n(ac)
                    wl = [(dv, n_at), (ac, n_v), (dv, max(st8.get(f"o{eb}_dv", 0),
                          st8.get("resid1_dv", 0) if eb == 1 else 0))]
                    for mc in range(2):
                        S.op("tensor", lambda e, obk=obk, mc=mc, ec=ec, vb=vb: e.matmul(
                            obk[:, 0:GRP],
                            v_tm[:, vb + mc * E + ec * 128: vb + mc * E + (ec + 1) * 128],
                            at_bf[:, mc * GRP:(mc + 1) * GRP],
                            start=(mc == 0), stop=False),
                            wait=wl if mc == 0 else (), inc=pe)
                    S.op("tensor", lambda e, obk=obk, ec=ec, gb=gb: e.matmul(
                        obk[:, 0:GRP], S_bf[:, ec * 128:(ec + 1) * 128],
                        lq_f[:, gb * GRP:(gb + 1) * GRP], start=False, stop=True), inc=pe)
                    npe = S.n(pe)
                    if ec == 15:
                        st8["sbf_pe"] = npe
                    S.op("vector", lambda e, obk=obk, ec=ec, eb=eb: e.scalar_tensor_tensor(
                        out=o_fm[:, ec * GRP:(ec + 1) * GRP],
                        in0=obk[:, 0:GRP], scalar=1.0,
                        in1=gate_p[:, eb * GRP:(eb + 1) * GRP],
                        op0=ALU.mult, op1=ALU.mult),
                        wait=((pe, npe), (ac, n_gate)), inc=dv)
                    st8[f"o{eb}_dv"] = S.n(dv)
                    st8[f"gate{eb}_dv"] = S.n(dv)
                st8["b4_dv"] = st8["o0_dv"]
                st8["b7_dv"] = 0
                n_omul = S.n(dv)

                wbanks = ((B[0], B[1]), (B[5], B[6]))
                for j in range(2):
                    wl0 = [(dv, n_omul), (dv, st8.get(f"resid{j}_dv", 0))]
                    for ec in range(16):
                        for hf in range(2):
                            bank = wbanks[j][hf]
                            S.op("tensor", lambda e, bank=bank, j=j, hf=hf, ec=ec: e.matmul(
                                bank[:, :],
                                o_fm[:, ec * GRP + j * 128: ec * GRP + j * 128 + 128],
                                w_o[:, ec * D + hf * 512: ec * D + (hf + 1) * 512],
                                start=(ec == 0), stop=False),
                                wait=wl0 if (ec == 0 and hf == 0) else (), inc=pe)
                    for hf in range(2):
                        bank = wbanks[j][hf]
                        S.op("tensor", lambda e, bank=bank, hf=hf: e.matmul(
                            bank[:, :], c_ones[0:1, :],
                            p_bout[0:1, hf * 512:(hf + 1) * 512], start=False, stop=True), inc=pe)
                    n_wout = S.n(pe)

                    sc = j * 8
                    for hf in range(2):
                        bank = wbanks[j][hf]
                        S.op("vector", lambda e, bank=bank, j=j, hf=hf, sc=sc, gb=gb: e.scalar_tensor_tensor(
                            out=x1_t[:, j * D + hf * 512: j * D + (hf + 1) * 512],
                            in0=bank[:, :], scalar=1.0,
                            in1=xbuf[:, (gb * 2 + j) * D + hf * 512:(gb * 2 + j) * D + (hf + 1) * 512],
                            op0=ALU.mult, op1=ALU.add,
                            accum_out=stat[:, sc + hf:sc + hf + 1]),
                            wait=((pe, n_wout), (f"s1{j}", st8.get(f"x1st{j}", 0))) if hf == 0 else (),
                            inc=dv)
                    st8[f"resid{j}_dv"] = S.n(dv)
                    if j == 1:
                        st8[f"xres{gb}"] = S.n(dv)
                    st8[f"lnp{j}"] = (gi, S.n(dv))
                    pass
                st8[f"xfree{gb}"] = S.n(pe)
                st8[f"vfree{gb}"] = S.n(pe)


            env.update(dict(
                c_ones=c_ones, c_onesf=c_onesf, c_idb=c_idb, c_idf=c_idf,
                stat=stat, hal_xqdT=hal_xqdT, hal_xT=hal_xT, p_aff=p_aff,
                p_bv=p_bv, p_bg=p_bg, p_bout=p_bout, binc_col=binc_col,
                w_v=w_v, w_i=w_i, x_fm=x_fm, qk_f=qk_f,
                v_tm=v_tm, Sh_bf=Sh_bf, S_bf=S_bf, xh=xh))

            S.op("tensor", lambda e: None, wait=(("gw", n_wload_pre), ("lc", S.n("lc"))))
            for gi in range(NGP):
                if gi == NGP - 1:
                    S.op("vector", lambda e: e.tensor_copy(Sh_bf[:, :], S_m[:, :]),
                         inc=dv)
                emit_group(gi, xp, full=False)
                if gi + 1 < NGP:
                    gb2 = (gi + 1) % 2
                    for j in range(2):
                        S.dma("sync", xbuf[:, (gb2 * 2 + j) * D:(gb2 * 2 + j + 1) * D],
                              xp[(gi + 1) * GRP + j * 128:(gi + 1) * GRP + (j + 1) * 128, :],
                              f"x{gb2}",
                              wait=((dv, st8.get(f"xres{gb2}", 0)),
                                    (pe, st8.get(f"xfree{gb2}", 0)),))
                    st8[f"xpre{gb2}"] = S.n(f"x{gb2}")
                if pslices:
                    pslices.pop(0)()
            S.op("vector", lambda e: e.tensor_scalar(
                out=S_m[:, :], in0=S_m[:, :], scalar1=c_pm[:, 0:1],
                scalar2=None, op0=ALU.mult), inc=dv)
            while pslices:
                pslices.pop(0)()
            pre_scope.close()
            w_g = asb("w_g", [128, 8 * E], BF16)
            w_o = asb("w_o", [128, 16 * D], BF16)
            at_bf = asb("at_bf", [128, 2 * GRP], BF16)
            gate_p = asb("gate_p", [128, 2 * GRP], BF16)
            o_fm = asb("o_fm", [128, 16 * GRP], BF16)
            x1_t = asb("x1_t", [128, 2 * D], F32)
            xqi_t = asb("xqi_t", [128, 512], I32)
            xqd_t = asb("xqd_t", [128, D], BF16)
            xqd_t2 = asb("xqd_t2", [128, D], BF16)
            xqdT_t = asb("xqdT_t", [128, D], BF16)
            for k in range(8):
                S.dma("gpsimd", w_g[:, k * E:(k + 1) * E], wg[k * 128:(k + 1) * 128, :],
                      "gw2", wait=((dv, prest["dv"]),) if k == 0 else ())
            for k in range(16):
                S.dma("gpsimd", w_o[:, k * D:(k + 1) * D], wo[k * 128:(k + 1) * 128, :], "gw2")
            n_wload = S.n("gw2")
            env.update(locals())
            S.op("tensor", lambda e: None, wait=(("gw2", n_wload),))
            env["pre_last_gb"] = (NGP - 1) % 2
            emit_halo(nc, S, B, env, which=0)
            st8["xqdT_pe0"] = S.n(pe)
            st8["xqdT_pe1"] = S.n(pe)
            S.barrier()

            for gi in range(NG):
                emit_group(gi, xm, full=True)
            for j in range(2):
                if st8.get(f"lnp{j}") is not None:
                    attn_ln1_a(j)
            for j in range(2):
                if st8.get(f"lnp{j}") is not None:
                    attn_ln1_b(j)
            S.op("vector", lambda e: e.tensor_copy(S_bf[:, :], S_m[:, :]), inc=dv)
            emit_halo(nc, S, B, env, which=1)

            if debug:
                ndv = S.n(dv)
                S.dma("sync", dbg["S"][:, :], S_m[:, :], "dbg", wait=((dv, ndv),), serial=True)
                S.dma("sync", dbg["halxq"][:, :], hal_xqdT[:, :], "dbg",
                      wait=((ac, S.n(ac)),), serial=True)
                S.dbg_fence()
            S.barrier()

        with contextlib.ExitStack() as ffn_scope:
            fsb = lambda n, s, d: ffn_scope.enter_context(nc.sbuf_tensor(n, s, d))
            w1q = fsb("w1q", [128, 8 * DH], BF16)
            w2q = fsb("w2q", [128, 32 * D], BF16)
            hsl = fsb("hsl", [128, 2 * 32 * SL], BF16)
            cv_h = fsb("cv_h", [128, 8 * FCH], BF16)
            q8_h = fsb("q8_h", [128, 16 * FCH], I8)
            xq2 = fsb("xq2", [128, 32 * FCH], BF16)
            x1f = fsb("x1f", [128, 2 * D], F32)
            xqTf = fsb("xqTf", [128, 8 * FCH], BF16)
            xqTf2 = fsb("xqTf2", [128, 8 * FCH], BF16)

            s2r = fsb("s2r", [1, 256], F32)
            s2bc = fsb("s2bc", [128, FCH], F32)
            hal_cols = fsb("hal_cols", [128, 64], BF16)
            hal_hrow = fsb("hal_hrow", [2, 512], BF16)
            scr4 = fsb("scr4", [128, 512], BF16)

            for c in range(8):
                S.dma("sync", w1q[:, c * DH:(c + 1) * DH], w1q_d[c, :, :], "lwq",
                      wait=(("sw", S.n("sw")),) if c == 0 else ())
            for c in range(32):
                S.dma("sync", w2q[:, c * D:(c + 1) * D], w2q_d[c, :, :], "lwq")
            n_wq = S.n("lwq")

            for nsl in range(8):
                bank = B[nsl % 2]
                wl = [("lwq", n_wq), (ac, 0)]
                for c in range(8):
                    S.op("tensor", lambda e, bank=bank, nsl=nsl, c=c: e.matmul(
                        bank[0:2, 0:512], hal_xqdT[:, c * 2:(c + 1) * 2],
                        w1q[:, c * DH + nsl * 512: c * DH + (nsl + 1) * 512],
                        start=(c == 0), stop=(c == 7)),
                        wait=wl if c == 0 else (), inc=pe)
                npe = S.n(pe)
                S.op("scalar", lambda e, bank=bank, nsl=nsl: e.activation(
                    out=hal_hrow[0:2, :],
                    in_=bank[0:2, 0:512], func=AF.Copy,
                    scale=c_hm[0:2, 0:1]),
                    wait=((pe, npe),), inc=ac)
                nac = S.n(ac)
                for j in range(4):
                    S.op("tensor", lambda e, j=j: e.transpose(
                        B[7][:, j * 2:(j + 1) * 2], hal_hrow[0:2, j * 128:(j + 1) * 128],
                        c_idb[0:2, 0:2]),
                        wait=((ac, nac),) if j == 0 else (), inc=pe)
                npe = S.n(pe)
                for j in range(4):
                    cc = nsl * 4 + j
                    S.op("scalar", lambda e, j=j, cc=cc: e.activation(
                        out=hal_cols[:, cc * 2:(cc + 1) * 2], in_=B[7][:, j * 2:(j + 1) * 2],
                        func=AF.Copy), wait=((pe, npe),) if j == 0 else (), inc=ac)
            n_halc_ac = S.n(ac)
            fst_init_halh = True
            n_halc = 0

            fst = {"halh_ac": n_halc_ac, "psil": [0] * 8, "tpend": [], "fpend": [], "fchain": [0, 0], "b2pend": []}
            if debug:
                S.op("vector", lambda e: e.memset(hsl[:, :], 0.0), inc=dv)
                fst["memset_dv"] = S.n(dv)
            hview = lambda fb: hsl[:, fb * 32 * SL:(fb + 1) * 32 * SL].rearrange(
                "p (c s) -> p c s", c=32)

            halv = hal_cols[:, :].rearrange("p (c u) -> p c u", c=32)
            mx_t = xq2[:, 0:2 * FCH]
            mn_t = xq2[:, 2 * FCH:4 * FCH]

            def slvt(tfc, hc, a, b):
                bs = (tfc % 2) * 32 * SL
                return hsl[:, bs + hc * SL + a: bs + hc * SL + b]

            def emit_sf(tfc, ent):
                _, g0, cntC = ent
                psil = fst["psil"]
                nsil = []
                for i, k2 in enumerate(range(g0, g0 + 4)):
                    S.op("scalar", lambda e, tfc=tfc, k2=k2: e.activation(
                        out=slvt(tfc, k2, 1, 1 + FCH),
                        in_=cv_h[:, (k2 % 8) * FCH:((k2 % 8) + 1) * FCH],
                        func=AF.Silu),
                        wait=((dv, cntC[i]),), inc=ac, sw=False)
                    psil[k2 % 8] = S.n(ac)
                    nsil.append(S.n(ac))
                fst["fpend"].append((max(g0 + 5, 18), g0, nsil))

            def emit_fold(tfc, ent):
                _, g0, nsil = ent
                ch = fst["fchain"]
                for i, k2 in enumerate(range(g0, g0 + 4)):
                    for ci, (buf, opx) in enumerate(((mx_t, ALU.max), (mn_t, ALU.min))):
                        if k2 == 0:
                            wl = [(ac, nsil[i])]
                            if ci == 0:
                                wl.append((pe, fst.get("w2done_pe", 0)))
                            S.op("vector", lambda e, tfc=tfc, buf=buf, k2=k2: e.tensor_copy(
                                buf[:, 0:FCH], slvt(tfc, k2, 1, 1 + FCH)),
                                wait=wl, inc=dv, sw=False)
                        else:
                            S.op("vector", lambda e, tfc=tfc, buf=buf, opx=opx, k2=k2: e.tensor_tensor(
                                buf[:, 0:FCH], slvt(tfc, k2, 1, 1 + FCH),
                                buf[:, 0:FCH], op=opx),
                                wait=((ac, nsil[i]), (dv, ch[ci])), inc=dv, sw=False)
                        ch[ci] = S.n(dv)

            def tail_piece(tfc, hc, evacs):
                while fst["tpend"] and fst["tpend"][0][0] <= hc:
                    emit_sf(tfc, fst["tpend"].pop(0))
                while fst["fpend"] and fst["fpend"][0][0] <= hc:
                    emit_fold(tfc, fst["fpend"].pop(0))
                for _ in range(2):
                    if fst["b2pend"]:
                        fst["b2pend"].pop(0)()
                final = (tfc == NFC - 1)
                fb = tfc % 2
                fbn = (tfc + 1) % 2
                k = hc % 8
                hs = hc // 16
                psil = fst["psil"]
                if hc == 0 and tfc == 0:
                    S.op("vector", lambda e: e.tensor_copy(
                        hview(0)[:, :, 0:1], halv[:, :, 0:1]),
                        wait=((ac, n_halc_ac),), inc=dv, sw=False)
                if final:
                    if hc == 0:
                        S.op("vector", lambda e, fb=fb: e.tensor_copy(
                            hview(fb)[:, :, SL - 1:SL], halv[:, :, 1:2]),
                            wait=((ac, n_halc_ac),), inc=dv, sw=False)
                else:
                    if hc == 0:
                        S.op("vector", lambda e, fb=fb, fbn=fbn: e.tensor_copy(
                            hview(fbn)[:, :, 0:1], hview(fb)[:, :, FCH:FCH + 1]),
                            wait=((ac, evacs[0]),), inc=dv, sw=False)
                    S.op("vector", lambda e, hc=hc, fb=fb, fbn=fbn: e.tensor_copy(
                        hview(fb)[:, hc:hc + 1, SL - 1:SL],
                        hview(fbn)[:, hc:hc + 1, 1:2]),
                        wait=((ac, evacs[hc]),), inc=dv, sw=False)
                S.op("vector", lambda e, tfc=tfc, hc=hc, k=k: e.tensor_scalar(
                    out=cv_h[:, k * FCH:(k + 1) * FCH],
                    in0=slvt(tfc, hc, 1, 1 + FCH),
                    scalar1=p_cw[:, 32 + hc:33 + hc],
                    scalar2=p_cw[:, 96 + hc:97 + hc],
                    op0=ALU.mult, op1=ALU.add),
                    wait=((ac, max(psil[k], evacs[hc] if final else 0)),),
                    inc=dv, sw=False)
                if hc % 4 != 3:
                    return
                g0 = hc - 3
                nA = S.n(dv)
                for k2 in range(g0, hc + 1):
                    S.op("vector", lambda e, tfc=tfc, k2=k2: e.scalar_tensor_tensor(
                        out=cv_h[:, (k2 % 8) * FCH:((k2 % 8) + 1) * FCH],
                        in0=slvt(tfc, k2, 0, FCH),
                        scalar=p_cw[:, k2:k2 + 1],
                        in1=cv_h[:, (k2 % 8) * FCH:((k2 % 8) + 1) * FCH],
                        op0=ALU.mult, op1=ALU.add),
                        wait=((dv, nA),) if k2 == g0 else (), inc=dv, sw=False)
                nB = S.n(dv)
                cntC = []
                for k2 in range(g0, hc + 1):
                    S.op("vector", lambda e, tfc=tfc, k2=k2: e.scalar_tensor_tensor(
                        out=cv_h[:, (k2 % 8) * FCH:((k2 % 8) + 1) * FCH],
                        in0=slvt(tfc, k2, 2, 2 + FCH),
                        scalar=p_cw[:, 64 + k2:65 + k2],
                        in1=cv_h[:, (k2 % 8) * FCH:((k2 % 8) + 1) * FCH],
                        op0=ALU.mult, op1=ALU.add),
                        wait=((dv, nB),) if k2 == g0 else (), inc=dv, sw=False)
                    cntC.append(S.n(dv))
                fst["tpend"].append((g0 + 5, g0, cntC))
                if hc == 31:
                    while fst["tpend"]:
                        emit_sf(tfc, fst["tpend"].pop(0))
                    while fst["fpend"]:
                        emit_fold(tfc, fst["fpend"].pop(0))
                    nw0 = S.n(dv)
                    S.op("vector", lambda e: e.tensor_scalar(
                        out=mn_t[:, 0:FCH], in0=mn_t[:, 0:FCH], scalar1=-1.0,
                        scalar2=None, op0=ALU.mult),
                        wait=((dv, nw0),), inc=dv, sw=False)
                    S.op("vector", lambda e: e.tensor_max(
                        mx_t[:, 0:FCH], mx_t[:, 0:FCH], mn_t[:, 0:FCH]),
                        inc=dv, sw=True)
                    fst["cvfold_dv"] = S.n(dv)

            def tail_rest(fc):
                fb = fc % 2
                base = fb * 32 * SL
                ndv = fst["cvfold_dv"]
                for tc in range(2):
                    S.op("tensor", lambda e, tc=tc: e.transpose(
                        B[7][:, tc * 128:(tc + 1) * 128],
                        mx_t[:, tc * 128:(tc + 1) * 128], c_idb[:, :]),
                        wait=((dv, ndv), (dv, fst.get("s2_dv", 0)),
                              (ac, fst.get("ln2sq_ac", 0))) if tc == 0 else (),
                        inc=pe)
                npe = S.n(pe)
                for tc in range(2):
                    S.op("vector", lambda e, tc=tc: e.reduce_max(
                        stat[:, 20 + tc:21 + tc], B[7][:, tc * 128:(tc + 1) * 128],
                        axis=AX.X), wait=((pe, npe),) if tc == 0 else (), inc=dv)
                S.op("vector", lambda e: e.tensor_scalar(
                    out=stat[:, 20:22], in0=stat[:, 20:22], scalar1=EPS,
                    scalar2=None, op0=ALU.max), inc=dv, sw=True)
                S.op("vector", lambda e: e.reciprocal(stat[:, 22:23], stat[:, 20:21]), inc=dv, sw=True)
                S.op("vector", lambda e: e.reciprocal(stat[:, 23:24], stat[:, 21:22]), inc=dv, sw=True)
                S.op("vector", lambda e: e.tensor_scalar(
                    out=stat[:, 22:24], in0=stat[:, 22:24], scalar1=127.0,
                    scalar2=None, op0=ALU.mult), inc=dv, sw=True)
                S.op("vector", lambda e: e.tensor_scalar(
                    out=stat[:, 24:26], in0=stat[:, 20:22], scalar1=1.0 / 127.0,
                    scalar2=wsc[:, 3:4], op0=ALU.mult, op1=ALU.mult), inc=dv, sw=True)
                ndv = S.n(dv)
                for tc in range(2):
                    S.op("tensor", lambda e, tc=tc: e.transpose(
                        B[4][0:1, 256 + tc * 128:384 + tc * 128],
                        stat[:, 22 + tc:23 + tc], c_idf[:, :]),
                        wait=((dv, ndv), (ac, fst.get("ln2sq_ac", 0))) if tc == 0 else (),
                        inc=pe)
                npe = S.n(pe)
                for tc in range(2):
                    S.op("scalar", lambda e, tc=tc: e.activation(
                        out=s2r[0:1, tc * 128:(tc + 1) * 128],
                        in_=B[4][0:1, 256 + tc * 128:384 + tc * 128], func=AF.Copy),
                        wait=((pe, npe),) if tc == 0 else (), inc=ac)
                nac = S.n(ac)
                for tc in range(2):
                    S.op("tensor", lambda e, tc=tc: e.matmul(
                        B[4][:, tc * 128:(tc + 1) * 128], c_onesf[0:1, :],
                        s2r[0:1, tc * 128:(tc + 1) * 128], start=True, stop=True),
                        wait=((ac, nac),) if tc == 0 else (), inc=pe)
                npe = S.n(pe)
                S.op("scalar", lambda e: e.activation(
                    out=s2bc[:, :], in_=B[4][:, 0:256], func=AF.Copy),
                    wait=((pe, npe),), inc=ac)
                n_s2 = S.n(ac)
                fst["s2bc_ac"] = n_s2

                s2v = s2bc[:, :]
                s2ap = [list(p) for p in s2v.ap]
                s2ap = [s2ap[0], [0, 16], s2ap[1]]
                s2b3 = bass.AP(tensor=s2v.tensor, offset=s2v.offset, ap=s2ap)
                n_q2h = [0, 0]
                for hs in range(2):
                    hv = hsl[:, base + hs * 16 * SL: base + (hs + 1) * 16 * SL].rearrange(
                        "p (c s) -> p c s", c=16)
                    S.op("vector", lambda e, hv=hv: e.tensor_mul(
                        q8_h[:, :].rearrange("p (c t) -> p c t", c=16),
                        hv[:, :, 1:1 + FCH], s2b3),
                        wait=((ac, n_s2), (dv, fst.get("q8_dv", 0))), inc=dv, sw=False)
                    S.op("vector", lambda e, hs=hs: e.tensor_copy(
                        xq2[:, hs * 16 * FCH:(hs + 1) * 16 * FCH], q8_h[:, :]),
                        inc=dv, sw=True)
                    fst["q8_dv"] = S.n(dv)
                    n_q2h[hs] = S.n(dv)
                n_q2 = S.n(dv)
                fst["s2_dv"] = n_q2
                fst[f"slabfree{fb}0"] = n_q2h[0]
                fst[f"slabfree{fb}1"] = n_q2h[1]
                fst["n_q2h"] = n_q2h

            def tail_rest_b(fc):
                fb = fc % 2
                n_q2h = fst.pop("n_q2h")
                banksets = ((B[5], B[6]), (B[0], B[1]))
                npe_t = [0, 0]
                for tt in range(2):
                    tile = fc * 2 + tt
                    xb = tt
                    S.dma("sync", x1f[:, xb * D:(xb + 1) * D], x1_d[tile, :, :],
                          f"lx1{xb}",
                          wait=((dv, fst.get(f"x1used{xb}", 0)),
                                (f"so{xb}", fst.get(f"o2st{xb}", 0))))
                    fst[f"nld_t{tt}"] = S.n(f"lx1{xb}")
                    for hf in range(2):
                        bank = banksets[tt][hf]
                        for hc in range(32):
                            wl = []
                            if hc == 0:
                                wl = [(dv, max(n_q2h[0], fst.get(f"w2o{tt}{hf}_dv", 0)))]
                                if tt == 1:
                                    wl.append((ac, fst.get("halh_ac", 0)))
                            elif hc == 16:
                                wl = [(dv, n_q2h[1])]
                            S.op("tensor", lambda e, bank=bank, tt=tt, hf=hf, hc=hc: e.matmul(
                                bank[:, :],
                                xq2[:, hc * FCH + tt * 128: hc * FCH + tt * 128 + 128],
                                w2q[:, hc * D + hf * 512: hc * D + (hf + 1) * 512],
                                start=(hc == 0), stop=(hc == 31)),
                                wait=wl, inc=pe)
                    npe_t[tt] = S.n(pe)
                fst["npe_t"] = npe_t
                fst["w2done_pe"] = npe_t[1]

            def tail_rest_b2(fc):
                fb = fc % 2
                npe_t = fst.pop("npe_t")
                banksets = ((B[5], B[6]), (B[0], B[1]))
                th = fst["b2pend"]

                def ep_stt(tt, hf):
                    xb = tt
                    scc = 24 + tt
                    bank = banksets[tt][hf]
                    wl = ((pe, npe_t[tt]), (f"lx1{xb}", fst[f"nld_t{tt}"])) if hf == 0 else ()
                    S.op("vector", lambda e, bank=bank, xb=xb, hf=hf, scc=scc, tt=tt: e.scalar_tensor_tensor(
                        out=x1f[:, xb * D + hf * 512: xb * D + (hf + 1) * 512],
                        in0=bank[:, :], scalar=stat[:, scc:scc + 1],
                        in1=x1f[:, xb * D + hf * 512: xb * D + (hf + 1) * 512],
                        op0=ALU.mult, op1=ALU.add,
                        accum_out=stat[:, 28 + hf * 2 + tt:29 + hf * 2 + tt]),
                        wait=wl, inc=dv, sw=False)
                    fst[f"w2o{tt}{hf}_dv"] = S.n(dv)
                    fst[f"b2stt{tt}"] = S.n(dv)

                def ep_sq(tt, hf):
                    xb = tt
                    wl = [(ac, S.n(ac))]
                    if hf == 0:
                        wl.append((dv, fst[f"b2stt{tt}"]))
                    S.op("scalar", lambda e, xb=xb, hf=hf, tt=tt: e.activation(
                        out=scr4[:, :],
                        in_=x1f[:, xb * D + hf * 512: xb * D + (hf + 1) * 512],
                        func=AF.Square,
                        accum_out=stat[:, 20 + hf * 2 + tt:21 + hf * 2 + tt]),
                        wait=wl, inc=ac, sw=False)
                    fst["ln2sq_ac"] = S.n(ac)

                def chain(i):
                    if i == 0:
                        S.op("vector", lambda e: e.tensor_add(
                            stat[:, 28:30], stat[:, 28:30], stat[:, 30:32]),
                            wait=((ac, fst["ln2sq_ac"]),), inc=dv, sw=False)
                    elif i == 1:
                        S.op("vector", lambda e: e.tensor_scalar(
                            out=stat[:, 28:30], in0=stat[:, 28:30],
                            scalar1=1.0 / D, scalar2=None, op0=ALU.mult),
                            inc=dv, sw=True)
                    elif i == 2:
                        S.op("vector", lambda e: e.tensor_add(
                            stat[:, 20:22], stat[:, 20:22], stat[:, 22:24]),
                            inc=dv, sw=True)
                    elif i == 3:
                        S.op("vector", lambda e: e.tensor_scalar(
                            out=stat[:, 20:22], in0=stat[:, 20:22],
                            scalar1=1.0 / D, scalar2=None, op0=ALU.mult),
                            inc=dv, sw=True)
                    elif i == 4:
                        S.op("vector", lambda e: e.tensor_mul(
                            stat[:, 22:24], stat[:, 28:30], stat[:, 28:30]),
                            inc=dv, sw=True)
                    elif i == 5:
                        S.op("vector", lambda e: e.tensor_sub(
                            stat[:, 22:24], stat[:, 20:22], stat[:, 22:24]),
                            inc=dv, sw=True)
                    elif i == 6:
                        S.op("vector", lambda e: e.tensor_scalar(
                            out=stat[:, 22:24], in0=stat[:, 22:24], scalar1=EPS,
                            scalar2=None, op0=ALU.add), inc=dv, sw=True)
                    elif i == 7:
                        ndv = S.n(dv)
                        S.op("scalar", lambda e: e.activation(
                            out=stat[:, 20:22], in_=stat[:, 22:24], func=AF.Sqrt),
                            wait=((dv, ndv),), inc=ac, sw=False)
                    else:
                        nac = S.n(ac)
                        S.op("vector", lambda e: e.reciprocal(
                            stat[:, 22:24], stat[:, 20:22]),
                            wait=((ac, nac),), inc=dv, sw=False)

                def fin(tt):
                    xb = tt
                    S.op("vector", lambda e, xb=xb, tt=tt: e.tensor_scalar(
                        out=x1f[:, xb * D:(xb + 1) * D], in0=x1f[:, xb * D:(xb + 1) * D],
                        scalar1=stat[:, 28 + tt:29 + tt], scalar2=stat[:, 22 + tt:23 + tt],
                        op0=ALU.subtract, op1=ALU.mult),
                        inc=dv, sw=True)
                    fst[f"x1used{xb}"] = S.n(dv)
                    ndv = S.n(dv)
                    tile = fc * 2 + tt
                    S.dma("sync", out[tile * 128:(tile + 1) * 128, :],
                          x1f[:, xb * D:(xb + 1) * D], f"so{xb}", wait=((dv, ndv),))
                    fst[f"o2st{xb}"] = S.n(f"so{xb}")

                for tt in range(2):
                    for hf in range(2):
                        th.append(lambda tt=tt, hf=hf: ep_stt(tt, hf))
                for tt in range(2):
                    for hf in range(2):
                        th.append(lambda tt=tt, hf=hf: ep_sq(tt, hf))
                for i in range(9):
                    th.append(lambda i=i: chain(i))
                th.append(lambda: fin(0))
                th.append(lambda: fin(1))

            def ffn_chunk(fc):
                fb = fc % 2
                xqT = xqTf if fb == 0 else xqTf2
                S.dma("gpsimd", xqT[:, :].rearrange("p (c t) -> p c t", c=8),
                      xqdT_d[:, :, fc * FCH:(fc + 1) * FCH], f"lxq{fb}",
                      wait=((pe, fst.get(f"xqTfree{fb}", 0)),))
                nld = S.n(f"lxq{fb}")
                tp = fst.pop("tailpend", None)
                evac_ac = []
                for hc in range(32):
                    bank = (B[2], B[3], B[4])[hc % 3]
                    wl = [(f"lxq{fb}", nld), ("lwq", n_wq),
                          (ac, fst.get(f"h{hc % 3}_ac", 0))]
                    if hc == 2:
                        wl.append((ac, fst.get("s2bc_ac", 0)))
                    for c in range(8):
                        S.op("tensor", lambda e, bank=bank, xqT=xqT, hc=hc, c=c: e.matmul(
                            bank[:, 0:FCH],
                            w1q[:, c * DH + hc * 128: c * DH + (hc + 1) * 128],
                            xqT[:, c * FCH:(c + 1) * FCH],
                            start=(c == 0), stop=(c == 7)),
                            wait=wl if c == 0 else (), inc=pe)
                    npe = S.n(pe)
                    S.op("scalar", lambda e, bank=bank, hc=hc, fb=fb: e.activation(
                        out=hsl[:, fb * 32 * SL + hc * SL + 1: fb * 32 * SL + hc * SL + 1 + FCH],
                        in_=bank[:, 0:FCH], func=AF.Copy),
                        wait=((pe, npe), (dv, max(fst.get(f"slabfree{fb}{hc // 16}", 0),
                                                  fst.get("memset_dv", 0)))),
                        inc=ac)
                    fst[f"h{hc % 3}_ac"] = S.n(ac)
                    evac_ac.append(S.n(ac))
                    if tp is not None:
                        tail_piece(tp, hc, evac_ac)
                    if hc == 4 and fst.get("restpend") is not None:
                        tail_rest_b(fst["restpend"])
                    if hc == 12 and fst.get("restpend") is not None:
                        tail_rest_b2(fst.pop("restpend"))
                n_h = S.n(ac)
                if debug and fc == 0:
                    S.dma("sync", dbg["h"][:, :], hsl[:, 0:32 * SL], "dbg",
                          wait=((ac, n_h),), serial=True)
                    S.dbg_fence()
                fst[f"xqTfree{fb}"] = S.n(pe)
                while fst["b2pend"]:
                    fst["b2pend"].pop(0)()
                if tp is not None:
                    tail_rest(tp)
                    fst["restpend"] = tp
                fst["tailpend"] = fc
                if fc == NFC - 1:
                    fst.pop("tailpend")
                    rp = fst.pop("restpend", None)
                    if rp is not None:
                        tail_rest_b(rp)
                        tail_rest_b2(rp)
                    for hc in range(32):
                        tail_piece(fc, hc, evac_ac)
                    tail_rest(fc)
                    tail_rest_b(fc)
                    tail_rest_b2(fc)
                    while fst["b2pend"]:
                        fst["b2pend"].pop(0)()

            for fc in range(NFC):
                ffn_chunk(fc)

            S.op("sync", lambda e: None,
                 wait=(("so0", S.n("so0")), ("so1", S.n("so1"))))

        with nc.Block() as block:
            S.emit(block)
    return nc


def emit_halo(nc, S, B, env, which):
    pe, ac, dv = "pe", "ac", "dv"
    c_ones, c_onesf = env["c_ones"], env["c_onesf"]
    c_idb = env["c_idb"]
    stat = env["stat"]
    aff = env["p_aff"]
    x1_t, xqi_t, xqd_t = env["x1_t"], env["xqi_t"], env["xqd_t"]
    h_y = x1_t[0:1, 0:D]
    hx = x1_t[0:1, D:2 * D]
    h_xq = xqd_t[0:1, 0:D]
    h_q = xqi_t[0:1, 0:512]
    vb15 = env.get("pre_last_gb", 1) * 2 * E

    with contextlib.ExitStack() as hs:
        hsb = lambda n, s, d: hs.enter_context(nc.sbuf_tensor(n, s, d))
        h_ve = hsb(f"h_ve{which}", [1, 512], BF16)
        h_ga = hsb(f"h_ga{which}", [1, 512], BF16)
        h_ob = hsb(f"h_ob{which}", [1, 512], BF16)
        h_xf = hsb(f"h_xf{which}", [128, 4], F32)
        hq_b = hsb(f"hq_b{which}", [128, 4], BF16)
        h_at = hsb(f"h_at{which}", [128, 2], BF16)
        h_oT = hsb(f"h_oT{which}", [128, 16], BF16)

        xcol = lambda c: env["hal_xT"][:, c * 2 + which: c * 2 + which + 1]
        n0dv, n0ac = S.n(dv), S.n(ac)
        w_drain = (("s10", S.n("s10")), ("s11", S.n("s11")),
                   ("sq0", S.n("sq0")), ("sq1", S.n("sq1")))
        for engx in ("vector", "scalar"):
            S.op(engx, lambda e: None, wait=w_drain[:2])
            S.op(engx, lambda e: None, wait=w_drain[2:])

        for c in range(8):
            S.op("tensor", lambda e, c=c: e.matmul(
                B[5][:, 0:1], env["w_i"][:, c * HD: c * HD + 128], xcol(c),
                start=(c == 0), stop=(c == 7)),
                wait=((dv, n0dv), (ac, n0ac)) if c == 0 else (), inc=pe)
        npe = S.n(pe)
        S.op("scalar", lambda e: e.activation(
            out=h_xf[:, 0:1], in_=B[5][:, 0:1], func=AF.Silu,
            bias=env["binc_col"][:, 0:1], scale=1.0), wait=((pe, npe),), inc=ac)
        nac = S.n(ac)
        for col, gcol in ((1, 0), (2, 2), (3, 4)):
            S.op("vector", lambda e, col=col, gcol=gcol: e.tensor_scalar(
                out=h_xf[:, col:col + 1], in0=h_xf[:, 0:1],
                scalar1=aff[:, gcol:gcol + 1], scalar2=aff[:, gcol + 1:gcol + 2],
                op0=ALU.mult, op1=ALU.add),
                wait=((ac, nac),) if col == 1 else (), inc=dv)
        S.op("vector", lambda e: e.tensor_copy(hq_b[:, 0:4], h_xf[:, 0:4]), inc=dv)
        ndv = S.n(dv)

        if which == 0:
            S.op("vector", lambda e: e.tensor_scalar(
                out=env["qk_f"][:, 0:GRP], in0=env["x_fm"][:, :],
                scalar1=aff[:, 2:3], scalar2=aff[:, 3:4],
                op0=ALU.mult, op1=ALU.add), inc=dv)
            ndv = S.n(dv)
            for mc in range(2):
                S.op("tensor", lambda e, mc=mc: e.matmul(
                    B[6][:, mc: mc + 1],
                    env["qk_f"][:, mc * 128:(mc + 1) * 128], hq_b[:, 1:2],
                    start=True, stop=True),
                    wait=((dv, ndv),) if mc == 0 else (), inc=pe)
            npe = S.n(pe)
            S.op("scalar", lambda e: e.activation(
                out=h_at[:, 0:2], in_=B[6][:, 0:2], func=AF.Relu),
                wait=((pe, npe),), inc=ac)
            nac = S.n(ac)
            S.op("vector", lambda e: e.scalar_tensor_tensor(
                out=h_at[:, 0:2], in0=h_at[:, 0:2], scalar=1.0,
                in1=h_at[:, 0:2], op0=ALU.mult, op1=ALU.mult),
                wait=((ac, nac),), inc=dv)
        else:
            S.op("tensor", lambda e: e.matmul(
                B[6][0:1, 0:1], hq_b[:, 1:2], hq_b[:, 2:3], start=True, stop=True),
                wait=((dv, ndv),), inc=pe)
            npe = S.n(pe)
            S.op("scalar", lambda e: e.activation(
                out=stat[0:1, 19:20], in_=B[6][0:1, 0:1], func=AF.Relu),
                wait=((pe, npe),), inc=ac)
            nac = S.n(ac)
            S.op("vector", lambda e: e.scalar_tensor_tensor(
                out=stat[0:1, 19:20], in0=stat[0:1, 19:20], scalar=1.0,
                in1=stat[0:1, 19:20], op0=ALU.mult, op1=ALU.mult),
                wait=((ac, nac),), inc=dv)
        n_att = S.n(dv)

        nprev_ac = S.n(ac)
        for e4 in range(4):
            wl = [(dv, n_att), (ac, nprev_ac)]
            for c in range(8):
                S.op("tensor", lambda e, c=c, e4=e4: e.matmul(
                    B[5][0:1, :], xcol(c),
                    env["w_g"][:, c * E + e4 * 512: c * E + (e4 + 1) * 512],
                    start=(c == 0), stop=False),
                    wait=wl if c == 0 else (), inc=pe)
            S.op("tensor", lambda e, e4=e4: e.matmul(
                B[5][0:1, :], c_ones[0:1, 0:1],
                env["p_bg"][0:1, e4 * 512:(e4 + 1) * 512],
                start=False, stop=True), inc=pe)
            npe = S.n(pe)
            S.op("scalar", lambda e: e.activation(
                out=h_ga[0:1, :], in_=B[5][0:1, :], func=AF.Silu),
                wait=((pe, npe),), inc=ac)
            nga = S.n(ac)
            if which == 1:
                for c in range(8):
                    S.op("tensor", lambda e, c=c, e4=e4: e.matmul(
                        B[5][0:1, :], xcol(c),
                        env["w_v"][:, c * E + e4 * 512: c * E + (e4 + 1) * 512],
                        start=(c == 0), stop=False),
                        wait=((ac, nga),) if c == 0 else (), inc=pe)
                S.op("tensor", lambda e, e4=e4: e.matmul(
                    B[5][0:1, :], c_ones[0:1, 0:1],
                    env["p_bv"][0:1, e4 * 512:(e4 + 1) * 512],
                    start=False, stop=True), inc=pe)
                npe = S.n(pe)
                S.op("scalar", lambda e: e.activation(
                    out=h_ve[0:1, :], in_=B[5][0:1, :], func=AF.Silu),
                    wait=((pe, npe),), inc=ac)
            n_ve = S.n(ac)
            if which == 0:
                for mc in range(2):
                    S.op("tensor", lambda e, mc=mc, e4=e4: e.matmul(
                        B[6][0:1, :], h_at[:, mc:mc + 1],
                        env["v_tm"][:, vb15 + mc * E + e4 * 512: vb15 + mc * E + (e4 + 1) * 512],
                        start=(mc == 0), stop=False),
                        wait=((dv, n_att),) if mc == 0 else (), inc=pe)
                S.op("tensor", lambda e, e4=e4: e.matmul(
                    B[6][0:1, :], hq_b[:, 3:4],
                    env["Sh_bf"][:, e4 * 512:(e4 + 1) * 512],
                    start=False, stop=True), inc=pe)
            else:
                S.op("tensor", lambda e, e4=e4: e.matmul(
                    B[6][0:1, :], hq_b[:, 3:4],
                    env["S_bf"][:, e4 * 512:(e4 + 1) * 512], start=True, stop=True),
                    wait=((dv, n_att),), inc=pe)
            npe = S.n(pe)
            S.op("scalar", lambda e: e.activation(
                out=h_ob[0:1, :], in_=B[6][0:1, :], func=AF.Copy),
                wait=((pe, npe),), inc=ac)
            nac = S.n(ac)
            if which == 1:
                S.op("vector", lambda e: e.scalar_tensor_tensor(
                    out=h_ob[0:1, :], in0=h_ve[0:1, :], scalar=stat[0:1, 19:20],
                    in1=h_ob[0:1, :], op0=ALU.mult, op1=ALU.add),
                    wait=((ac, max(nac, n_ve)),), inc=dv)
            S.op("vector", lambda e: e.tensor_mul(
                h_ob[0:1, :], h_ob[0:1, :], h_ga[0:1, :]),
                wait=((ac, max(nac, nga)),), inc=dv)
            ndv = S.n(dv)
            for s4 in range(4):
                S.op("tensor", lambda e, s4=s4: e.transpose(
                    B[7][:, s4 * 128:s4 * 128 + 1],
                    h_ob[0:1, s4 * 128:(s4 + 1) * 128], c_idb[0:1, 0:1]),
                    wait=((dv, ndv),) if s4 == 0 else (), inc=pe)
            npe = S.n(pe)
            for s4 in range(4):
                cix = e4 * 4 + s4
                S.op("scalar", lambda e, s4=s4, cix=cix: e.activation(
                    out=h_oT[:, cix:cix + 1], in_=B[7][:, s4 * 128:s4 * 128 + 1],
                    func=AF.Copy), wait=((pe, npe),) if s4 == 0 else (), inc=ac)
            nprev_ac = S.n(ac)
        nac = S.n(ac)

        nprev = nac
        for hf in range(2):
            for c in range(16):
                S.op("tensor", lambda e, hf=hf, c=c: e.matmul(
                    B[5][0:1, :], h_oT[:, c:c + 1],
                    env["w_o"][:, c * D + hf * 512: c * D + (hf + 1) * 512],
                    start=(c == 0), stop=False),
                    wait=((ac, nprev),) if c == 0 else (), inc=pe)
            S.op("tensor", lambda e, hf=hf: e.matmul(
                B[5][0:1, :], c_ones[0:1, 0:1], env["p_bout"][0:1, hf * 512:(hf + 1) * 512],
                start=False, stop=True), inc=pe)
            npe = S.n(pe)
            S.op("scalar", lambda e, hf=hf: e.activation(
                out=h_y[0:1, hf * 512:(hf + 1) * 512], in_=B[5][0:1, :],
                func=AF.Copy), wait=((pe, npe),), inc=ac)
            nprev = S.n(ac)
        nac = S.n(ac)
        S.dma("sync", hx[0:1, :], env["xh"][which:which + 1, :], "lhx", serial=True,
              wait=(("s10", S.n("s10")), ("s11", S.n("s11"))))
        nld = S.n("lhx")
        S.op("vector", lambda e: e.tensor_add(h_y[0:1, :], h_y[0:1, :], hx[0:1, :]),
             wait=((ac, nac), ("lhx", nld)), inc=dv)
        S.op("vector", lambda e: e.reduce_sum(stat[0:1, 16:17], h_y[0:1, :], axis=AX.X),
             inc=dv)
        ndv = S.n(dv)
        S.op("scalar", lambda e: e.activation(
            out=h_xq[0:1, :], in_=h_y[0:1, :], func=AF.Square,
            accum_out=stat[0:1, 17:18]), wait=((dv, ndv),), inc=ac)
        nac = S.n(ac)
        S.op("vector", lambda e: e.tensor_scalar(
            out=stat[0:1, 16:18], in0=stat[0:1, 16:18], scalar1=1.0 / D,
            scalar2=None, op0=ALU.mult), wait=((ac, nac),), inc=dv)
        S.op("vector", lambda e: e.tensor_mul(
            stat[0:1, 18:19], stat[0:1, 16:17], stat[0:1, 16:17]), inc=dv)
        S.op("vector", lambda e: e.tensor_sub(
            stat[0:1, 18:19], stat[0:1, 17:18], stat[0:1, 18:19]), inc=dv)
        S.op("vector", lambda e: e.tensor_scalar(
            out=stat[0:1, 18:19], in0=stat[0:1, 18:19], scalar1=EPS,
            scalar2=None, op0=ALU.add), inc=dv)
        ndv = S.n(dv)
        S.op("scalar", lambda e: e.activation(
            out=stat[0:1, 17:18], in_=stat[0:1, 18:19], func=AF.Sqrt),
            wait=((dv, ndv),), inc=ac)
        nac = S.n(ac)
        S.op("vector", lambda e: e.reciprocal(stat[0:1, 18:19], stat[0:1, 17:18]),
             wait=((ac, nac),), inc=dv)
        S.op("vector", lambda e: e.tensor_scalar(
            out=h_y[0:1, :], in0=h_y[0:1, :], scalar1=stat[0:1, 16:17],
            scalar2=stat[0:1, 18:19], op0=ALU.subtract, op1=ALU.mult), inc=dv)
        S.op("vector", lambda e: e.reduce_max(
            stat[0:1, 16:17], h_y[0:1, :], axis=AX.X, apply_absolute_value=True),
            inc=dv)
        S.op("vector", lambda e: e.tensor_scalar(
            out=stat[0:1, 16:17], in0=stat[0:1, 16:17], scalar1=EPS, scalar2=None,
            op0=ALU.max), inc=dv)
        S.op("vector", lambda e: e.reciprocal(stat[0:1, 17:18], stat[0:1, 16:17]),
             inc=dv)
        S.op("vector", lambda e: e.tensor_scalar(
            out=stat[0:1, 17:18], in0=stat[0:1, 17:18], scalar1=127.0, scalar2=None,
            op0=ALU.mult), inc=dv)
        S.op("vector", lambda e: e.tensor_scalar(
            out=stat[0:1, 16:17], in0=stat[0:1, 16:17], scalar1=1.0 / 127.0,
            scalar2=None, op0=ALU.mult), inc=dv)
        for hf in range(2):
            S.op("vector", lambda e, hf=hf: e.tensor_scalar(
                out=h_q[0:1, :], in0=h_y[0:1, hf * 512:(hf + 1) * 512],
                scalar1=stat[0:1, 17:18], scalar2=None, op0=ALU.mult), inc=dv)
            S.op("vector", lambda e, hf=hf: e.tensor_scalar(
                out=h_xq[0:1, hf * 512:(hf + 1) * 512], in0=h_q[0:1, :],
                scalar1=stat[0:1, 16:17], scalar2=None, op0=ALU.mult), inc=dv)
        ndv = S.n(dv)
        nprev_ac = S.n(ac)
        for b4 in range(2):
            wl = [(dv, ndv), (ac, nprev_ac)]
            for s4 in range(4):
                c = b4 * 4 + s4
                S.op("tensor", lambda e, s4=s4, c=c: e.transpose(
                    B[7][:, s4 * 128:s4 * 128 + 1],
                    h_xq[0:1, c * 128:(c + 1) * 128], c_idb[0:1, 0:1]),
                    wait=wl if s4 == 0 else (), inc=pe)
            npe = S.n(pe)
            for s4 in range(4):
                c = b4 * 4 + s4
                S.op("scalar", lambda e, s4=s4, c=c: e.activation(
                    out=env["hal_xqdT"][:, c * 2 + which: c * 2 + which + 1],
                    in_=B[7][:, s4 * 128:s4 * 128 + 1], func=AF.Copy),
                    wait=((pe, npe),) if s4 == 0 else (), inc=ac)
            nprev_ac = S.n(ac)


def _prep_inputs(inp):
    bf = ml_dtypes.bfloat16
    x = np.ascontiguousarray(np.asarray(inp["L0_attn_x"], np.float32))
    f32 = lambda a: np.ascontiguousarray(np.asarray(a, np.float32))
    col = lambda a: np.ascontiguousarray(f32(a).reshape(128, 1))
    aff = np.concatenate([f32(inp[k]).reshape(128, 1) for k in (
        "L0_attn_g_qq", "L0_attn_b_qq", "L0_attn_g_qk", "L0_attn_b_qk",
        "L0_attn_g_lq", "L0_attn_b_lq", "L0_attn_g_lk", "L0_attn_b_lk")], axis=1)
    cw = f32(inp["L0_ffn_conv_w"])
    cwc = np.concatenate(
        [np.ascontiguousarray(cw[:, 0, k].reshape(32, 128).T) for k in range(3)]
        + [np.ascontiguousarray(f32(inp["L0_ffn_conv_b"]).reshape(32, 128).T)], axis=1)
    m = np.arange(128)[:, None]
    n = np.arange(GRP)[None, :]
    mask01 = np.concatenate([(n >= m).astype(bf), (n >= m + 128).astype(bf)], axis=1)
    shared = {
        "wv": f32(inp["L0_attn_Wv"]), "wg": f32(inp["L0_attn_Wg"]),
        "wi": f32(inp["L0_attn_Win"]), "wo": f32(inp["L0_attn_Wout"]),
        "bvr": f32(inp["L0_attn_bv"]).reshape(1, E),
        "bgr": f32(inp["L0_attn_bg"]).reshape(1, E),
        "bgc": np.ascontiguousarray(f32(inp["L0_attn_bg"]).reshape(16, 128).T),
        "binc": col(inp["L0_attn_bin_"]),
        "boutr": f32(inp["L0_attn_bout"]).reshape(1, D),
        "affc": np.ascontiguousarray(aff),
        "w1t": np.ascontiguousarray(f32(inp["L0_ffn_w1"]).T),
        "w2t": np.ascontiguousarray(f32(inp["L0_ffn_w2"]).T),
        "cwc": np.ascontiguousarray(cwc),
        "identf": np.eye(128, dtype=np.float32),
        "identb": np.eye(128, dtype=bf),
        "onesb": np.ones((1, 128), dtype=bf),
        "onesf": np.ones((1, 128), dtype=np.float32),
        "onesm": np.ones((128, 256), dtype=np.float32),
        "mask01": np.ascontiguousarray(mask01),
    }
    in_maps = []
    for core in range(8):
        b, half = divmod(core, 2)
        t0 = half * NMAIN
        hL = x[b, t0 - 1] if t0 > 0 else np.zeros(D, np.float32)
        hR = x[b, t0 + NMAIN] if t0 + NMAIN < 8192 else np.zeros(D, np.float32)
        xhv = np.stack([hL, hR]).astype(np.float32)
        xhT = np.ascontiguousarray(
            xhv.T.reshape(8, 128, 2).transpose(1, 0, 2).reshape(128, 16))
        im = dict(shared)
        im.update({
            "xm": np.ascontiguousarray(x[b, t0:t0 + NMAIN]),
            "xp": np.ascontiguousarray(x[b, 0:NPRE]),
            "xh": np.ascontiguousarray(xhv), "xhT": xhT,
            "pmask": np.full((128, 1), 1.0 if half == 1 else 0.0, np.float32),
            "hmv": np.array([[1.0 if half == 1 else 0.0],
                             [1.0 if half == 0 else 0.0]], np.float32),
        })
        in_maps.append(im)
    return in_maps


_NC_CACHE = {}


def kernel(**inputs):
    if "nc" not in _NC_CACHE:
        _NC_CACHE["nc"] = build_nc()
    nc = _NC_CACHE["nc"]
    in_maps = _prep_inputs(inputs)
    res = run_bass_kernel_spmd(nc, in_maps, core_ids=list(range(8)))
    outv = np.empty((4, 8192, D), np.float32)
    for core in range(8):
        b, half = divmod(core, 2)
        outv[b, half * NMAIN:(half + 1) * NMAIN] = res.results[core]["out"]
    return outv


# version 24 (final)
# speedup vs baseline: 1.2102x; 1.2081x over previous
import contextlib

import numpy as np
import ml_dtypes

import concourse.bass as bass
import concourse.mybir as mybir
from concourse.bass_utils import run_bass_kernel_spmd

F32 = mybir.dt.float32
BF16 = mybir.dt.bfloat16
I32 = mybir.dt.int32
I8 = mybir.dt.int8
AF = mybir.ActivationFunctionType
ALU = mybir.AluOpType
AX = mybir.AxisListType

D = 1024
E = 2048
HD = 128
GRP = 256
DH = 4096
NMAIN = 4096
NPRE = 4096
FCH = 256
SL = FCH + 2
INV_SCALE = 1.0 / float(np.sqrt(2048.0))
EPS = 1e-5
WN = D * DH

ENGS = ("sync", "tensor", "scalar", "vector", "gpsimd")


def bcast0(t2d, col0, ncols, reps):
    v = t2d[:, col0:col0 + ncols].rearrange("p (c u) -> p c u", u=1)
    ap = [list(p) for p in v.ap]
    ap[-1] = [0, reps]
    return bass.AP(tensor=v.tensor, offset=v.offset, ap=ap)


class Sched:

    def __init__(self, nc, ctx):
        self.nc = nc
        self.ctx = ctx
        self.q = {e: [] for e in ENGS}
        self.sem = {}
        self.cnt = {}
        self.dma_streams = set()
        for name in ("pe", "ac", "dv", "bar"):
            self._sem(name)
        self.nbar = 0

    def _sem(self, name):
        if name not in self.sem:
            self.sem[name] = self.ctx.enter_context(self.nc.semaphore("sem_" + name))
            self.cnt[name] = 0
        return self.sem[name]

    ENG_SEM = {"tensor": "pe", "scalar": "ac", "vector": "dv"}

    def op(self, eng, fn, wait=(), inc=None, inc_by=1, sw=None):
        if sw is None and eng in ("vector", "scalar"):
            sw = True
        if sw:
            es = self.ENG_SEM[eng]
            wait = tuple(wait) + ((es, self.cnt[es]),)
        w = tuple((self._sem(s), val) for s, val in wait if val > 0)
        if inc is not None:
            self._sem(inc)
            self.cnt[inc] += inc_by
        self.q[eng].append((fn, w, (self.sem[inc], inc_by) if inc else None))
        return self.cnt[inc] if inc else None

    def dma(self, eng, out, in_, stream, wait=(), serial=False):
        self._sem(stream)
        self.dma_streams.add(stream)
        if serial:
            wait = tuple(wait) + ((stream, self.cnt[stream]),)
        return self.op(eng, lambda e, o=out, i=in_: e.dma_start(out=o, in_=i),
                       wait=wait, inc=stream, inc_by=16)

    def n(self, name):
        self._sem(name)
        return self.cnt[name]

    def dbg_fence(self):
        w = (("dbg", self.cnt.get("dbg", 0)),)
        for eng in ("tensor", "scalar", "vector"):
            self.op(eng, lambda e: None, wait=w)

    def barrier(self):
        waits = tuple((s, self.cnt[s]) for s in sorted(self.dma_streams))
        self.op("sync", lambda e: None, wait=waits)
        for eng in ENGS:
            self.op(eng, lambda e: None, inc="bar")
        self.nbar += 5
        for eng in ENGS:
            self.op(eng, lambda e: None, wait=(("bar", self.nbar),))

    def emit(self, block):
        for eng in ENGS:
            items = self.q[eng]

            def body(handle, items=items):
                for fn, waits, inc in items:
                    ws = list(waits)
                    while len(ws) > 2:
                        handle.wait_ge(*ws[0])
                        handle.wait_ge(*ws[1])
                        handle.nop()
                        ws = ws[2:]
                    for sem, val in ws:
                        handle.wait_ge(sem, val)
                    instr = fn(handle)
                    if inc is not None:
                        sem, by = inc
                        if instr is None:
                            handle.sem_inc(sem, by)
                        else:
                            instr.then_inc(sem, by)
                    elif instr is None and ws:
                        handle.nop()

            getattr(block, eng)(body)


def build_nc(n_pre_groups=16, n_main_groups=16, debug=False):
    NGP = n_pre_groups
    NG = n_main_groups
    NTOK = NG * GRP
    NT = NTOK * 0 + NG * 2
    NFC = NTOK // FCH
    nc = bass.Bass("TRN2", target_bir_lowering=False)

    dtn = lambda n, s, d, k: nc.dram_tensor(n, s, d, kind=k).ap()
    xm = dtn("xm", [NTOK, D], F32, "ExternalInput")
    xp = dtn("xp", [NGP * GRP, D], F32, "ExternalInput")
    xhT = dtn("xhT", [128, 16], F32, "ExternalInput")
    xh = dtn("xh", [2, D], F32, "ExternalInput")
    wv = dtn("wv", [D, E], F32, "ExternalInput")
    wg = dtn("wg", [D, E], F32, "ExternalInput")
    wi = dtn("wi", [D, HD], F32, "ExternalInput")
    wo = dtn("wo", [E, D], F32, "ExternalInput")
    bvr = dtn("bvr", [1, E], F32, "ExternalInput")
    bgr = dtn("bgr", [1, E], F32, "ExternalInput")
    bgc = dtn("bgc", [128, 16], F32, "ExternalInput")
    binc = dtn("binc", [128, 1], F32, "ExternalInput")
    boutr = dtn("boutr", [1, D], F32, "ExternalInput")
    affc = dtn("affc", [128, 8], F32, "ExternalInput")
    w1t = dtn("w1t", [D, DH], F32, "ExternalInput")
    w2t = dtn("w2t", [DH, D], F32, "ExternalInput")
    cwc = dtn("cwc", [128, 128], F32, "ExternalInput")
    identf = dtn("identf", [128, 128], F32, "ExternalInput")
    identb = dtn("identb", [128, 128], BF16, "ExternalInput")
    onesb = dtn("onesb", [1, 128], BF16, "ExternalInput")
    onesf = dtn("onesf", [1, 128], F32, "ExternalInput")
    onesm = dtn("onesm", [128, 256], F32, "ExternalInput")
    mask01 = dtn("mask01", [128, 2 * GRP], BF16, "ExternalInput")
    pmask = dtn("pmask", [128, 1], F32, "ExternalInput")
    hmv = dtn("hmv", [2, 1], F32, "ExternalInput")

    out = dtn("out", [NTOK, D], F32, "ExternalOutput")

    w1q_d = dtn("w1q_d", [8, 128, DH], BF16, "Internal")
    w2q_d = dtn("w2q_d", [32, 128, D], BF16, "Internal")
    x1_d = dtn("x1_d", [NT, 128, D], F32, "Internal")
    xqdT_d = dtn("xqdT_d", [128, 8, NTOK], BF16, "Internal")
    hal_d = dtn("hal_d", [2, DH], BF16, "Internal")

    dbg = {}
    if debug:
        dbg["S"] = dtn("dbg_S", [128, E], F32, "ExternalOutput")
        dbg["x1"] = dtn("dbg_x1", [NT, 128, D], F32, "ExternalOutput")
        dbg["xqdT"] = dtn("dbg_xqdT", [128, 8, NTOK], BF16, "ExternalOutput")
        dbg["halxq"] = dtn("dbg_halxq", [128, 16], BF16, "ExternalOutput")
        dbg["at"] = dtn("dbg_at", [128, 2 * GRP], BF16, "ExternalOutput")
        dbg["stat"] = dtn("dbg_stat", [128, 32], F32, "ExternalOutput")
        dbg["h"] = dtn("dbg_h", [128, 32 * SL], BF16, "ExternalOutput")
        dbg["hs"] = dtn("dbg_hs", [128, 32 * SL], BF16, "ExternalOutput")
        dbg["xq2"] = dtn("dbg_xq2", [128, 32 * FCH], BF16, "ExternalOutput")
        dbg["fst"] = dtn("dbg_fst", [128, 32], F32, "ExternalOutput")
        dbg["y2"] = dtn("dbg_y2", [128, D], F32, "ExternalOutput")
        dbg["fst2"] = dtn("dbg_fst2", [128, 32], F32, "ExternalOutput")
        dbg["wsc"] = dtn("dbg_wsc", [128, 4], F32, "ExternalOutput")
        dbg["pre"] = dtn("dbg_pre", [128, 32], F32, "ExternalOutput")
        dbg["pw0"] = dtn("dbg_pw0", [128, DH], F32, "ExternalOutput")
        dbg["y"] = dtn("dbg_y", [128, D], F32, "ExternalOutput")
        dbg["o"] = dtn("dbg_o", [128, 2 * GRP], BF16, "ExternalOutput")
        dbg["vt"] = dtn("dbg_vt", [128, 2 * E], BF16, "ExternalOutput")
        dbg["gate"] = dtn("dbg_gate", [128, 2 * GRP], BF16, "ExternalOutput")

    with contextlib.ExitStack() as root:
        S = Sched(nc, root)
        sb = lambda n, s, d: root.enter_context(nc.sbuf_tensor(n, s, d))

        c_idf = sb("c_idf", [128, 128], F32)
        c_idb = sb("c_idb", [128, 128], BF16)
        c_ones = sb("c_ones", [1, 128], BF16)
        c_onesf = sb("c_onesf", [1, 128], F32)
        c_onesm = sb("c_onesm", [128, 256], F32)
        c_hm = sb("c_hm", [2, 1], F32)
        stat = sb("stat", [128, 32], F32)
        hal_xqdT = sb("hal_xqdT", [128, 16], BF16)
        p_cw = sb("p_cw", [128, 128], F32)
        wsc = sb("wsc", [128, 4], F32)

        B = [root.enter_context(nc.psum_tensor(f"bk{i}", [128, 512], F32))
             for i in range(7)]
        B.append(root.enter_context(nc.psum_tensor("bk7", [128, 1024], BF16)))

        pe, ac, dv = "pe", "ac", "dv"

        env = {}

        with contextlib.ExitStack() as attn_scope:
            asb = lambda n, s, d: attn_scope.enter_context(nc.sbuf_tensor(n, s, d))
            c_m01 = asb("c_m01", [128, 2 * GRP], BF16)
            c_pm = asb("c_pm", [128, 1], F32)
            p_aff = asb("p_aff", [128, 8], F32)

            for dst, srcx in ((c_idf, identf), (c_idb, identb), (c_ones, onesb),
                              (c_onesf, onesf), (c_onesm, onesm), (c_hm, hmv),
                              (c_m01, mask01), (c_pm, pmask), (p_aff, affc),
                              (p_cw, cwc)):
                S.dma("sync", dst[:, :], srcx[:, :], "lc")
            n_const = S.n("lc")

            S.op("vector", lambda e: e.tensor_scalar(
                out=p_aff[:, 0:2], in0=p_aff[:, 0:2], scalar1=INV_SCALE,
                scalar2=None, op0=ALU.mult), wait=(("lc", n_const),), inc=dv)
            if debug:
                S.op("vector", lambda e: e.memset(stat[:, :], 0.0), inc=dv)

            p_bv = asb("p_bv", [1, E], BF16)
            p_bg = asb("p_bg", [1, E], BF16)
            p_bgc = asb("p_bgc", [128, 16], F32)
            p_bout = asb("p_bout", [1, D], BF16)
            w_v = asb("w_v", [128, 8 * E], BF16)
            w_i = asb("w_i", [128, 8 * HD], BF16)
            hal_xT = asb("hal_xT", [128, 16], BF16)
            xbuf = asb("xbuf", [128, 4 * D], F32)
            xT = asb("xT", [128, 2 * 8 * GRP], BF16)
            v_tm = asb("v_tm", [128, 4 * E], BF16)
            x_fm = asb("x_fm", [128, GRP], BF16)
            qq_f = asb("qq_f", [128, GRP], BF16)
            qk_f = asb("qk_f", [128, GRP], BF16)
            lq_f = asb("lq_f", [128, 2 * GRP], BF16)
            lk_f = asb("lk_f", [128, GRP], BF16)
            lk_tm = asb("lk_tm", [128, 2 * 128], BF16)
            S_m = asb("S_m", [128, E], F32)
            S_bf = asb("S_bf", [128, E], BF16)
            Sh_bf = asb("Sh_bf", [128, E], BF16)
            binc_col = asb("binc_col", [128, 1], F32)
            S.dma("sync", p_bgc[:, :], bgc[:, :], "lc", serial=True)
            S.dma("sync", binc_col[:, :], binc[:, :], "lc", serial=True)
            for k in range(8):
                S.dma("gpsimd", w_v[:, k * E:(k + 1) * E], wv[k * 128:(k + 1) * 128, :], "gw")
            for k in range(8):
                S.dma("gpsimd", w_i[:, k * HD:(k + 1) * HD], wi[k * 128:(k + 1) * 128, :], "gw")
            for dstt, srcx in ((p_bv, bvr), (p_bg, bgr), (p_bout, boutr)):
                S.dma("gpsimd", dstt[:, :], srcx[:, :], "gw")
            S.dma("gpsimd", hal_xT[:, :], xhT[:, :], "gw")
            n_wload_pre = S.n("gw")

            pre_scope = contextlib.ExitStack()
            qsb = lambda n, s, d: pre_scope.enter_context(nc.sbuf_tensor(n, s, d))
            pw = [qsb(f"pw{i}", [128, DH], F32) for i in range(4)]
            pi = qsb("pi", [128, DH // 2], I32)
            pb = qsb("pb", [128, DH // 2], BF16)
            pslices = []
            prest = {"dv": 0}

            def ps_load(wsrc, cs, dst_of, wait_dv):
                def f():
                    wl = ((dv, S.n(dv)),) if wait_dv else ()
                    for i, c in enumerate(cs):
                        t, sl = dst_of(c)
                        S.dma("gpsimd", pw[t][:, sl], wsrc[c * 128:(c + 1) * 128, :],
                              "lq", wait=wl if i == 0 else ())
                return f

            def ps_sums(scol0):
                def f():
                    nld = S.n("lq")
                    for t in range(4):
                        S.op("vector", lambda e, t=t, scol0=scol0: e.reduce_sum(
                            stat[:, scol0 + t:scol0 + t + 1], pw[t][:, :], axis=AX.X,
                            apply_absolute_value=True),
                            wait=(("lq", nld), (ac, S.n(ac))) if t == 0 else (),
                            inc=dv, sw=False)
                return f

            def ps_scale(col):
                def f():
                    ndv = S.n(dv)
                    S.op("tensor", lambda e, col=col: e.matmul(
                        B[1][:, 0:8], c_onesm[:, col * 64:col * 64 + 128],
                        stat[:, 16:24], start=True, stop=True),
                        wait=((dv, ndv), ("lc", n_const), (ac, S.n(ac))), inc=pe)
                    npe = S.n(pe)
                    S.op("scalar", lambda e: e.activation(
                        out=stat[:, 16:24], in_=B[1][:, 0:8], func=AF.Copy),
                        wait=((pe, npe),), inc=ac)
                    nac0 = S.n(ac)
                    S.op("vector", lambda e: e.reduce_sum(
                        stat[:, 25:26], stat[:, 16:24], axis=AX.X),
                        wait=((ac, nac0),), inc=dv, sw=True)
                    S.op("vector", lambda e: e.tensor_scalar(
                        out=stat[:, 26:27], in0=stat[:, 25:26],
                        scalar1=1.0 / WN, scalar2=EPS, op0=ALU.mult, op1=ALU.max),
                        inc=dv, sw=True)
                    S.op("vector", lambda e: e.reciprocal(
                        stat[:, 27:28], stat[:, 26:27]), inc=dv, sw=True)
                    S.op("vector", lambda e, col=col: e.tensor_copy(
                        wsc[:, col:col + 1], stat[:, 27:28]), inc=dv, sw=True)
                    S.op("vector", lambda e, col=col: e.tensor_copy(
                        wsc[:, col + 1:col + 2], stat[:, 26:27]), inc=dv, sw=True)
                return f

            def ps_quant(wdst, cs, src_of, col, qn):
                def f():
                    nlq = S.n("lq")
                    first = True
                    for c in cs:
                        t, off, dsl = src_of(c)
                        for h in range((off[1] - off[0]) // qn):
                            a = off[0] + h * qn
                            n_stp = S.n("sw")
                            wl = [("sw", n_stp)]
                            if first:
                                wl.append(("lq", nlq))
                                first = False
                            S.op("vector", lambda e, t=t, a=a, col=col: e.tensor_scalar(
                                out=pi[:, 0:qn], in0=pw[t][:, a:a + qn],
                                scalar1=wsc[:, col:col + 1], scalar2=None,
                                op0=ALU.mult),
                                wait=wl, inc=dv)
                            S.op("vector", lambda e: e.tensor_scalar(
                                out=pb[:, 0:qn], in0=pi[:, 0:qn],
                                scalar1=-1.0, scalar2=1.0, op0=ALU.max, op1=ALU.min),
                                inc=dv)
                            ndv = S.n(dv)
                            S.dma("sync", wdst[c, :, dsl[0] + h * qn: dsl[0] + (h + 1) * qn],
                                  pb[:, 0:qn], "sw", wait=((dv, ndv),))
                    prest["dv"] = S.n(dv)
                return f

            QN = DH // 2
            w1_dst = lambda c: (c % 4, slice(0, DH))
            w1_src = lambda c: (c % 4, (0, DH), (0, DH))
            w2_dst = lambda c: ((c % 16) // 4, slice((c % 4) * D, (c % 4 + 1) * D))
            w2_src = lambda c: ((c % 16) // 4, ((c % 4) * D, (c % 4 + 1) * D), (0, D))
            pslices.append(ps_load(w1t, range(0, 4), w1_dst, False))
            pslices.append(lambda f1=ps_sums(16), f2=ps_load(w1t, range(4, 8), w1_dst, True): (f1(), f2()))
            pslices.append(lambda f1=ps_sums(20), f2=ps_scale(0): (f1(), f2()))
            pslices.append(ps_quant(w1q_d, range(4, 8), w1_src, 0, QN))
            pslices.append(ps_load(w1t, range(0, 4), w1_dst, True))
            pslices.append(ps_quant(w1q_d, range(0, 4), w1_src, 0, QN))
            pslices.append(ps_load(w2t, range(0, 16), w2_dst, True))
            pslices.append(ps_sums(16))
            pslices.append(ps_load(w2t, range(16, 32), w2_dst, True))
            pslices.append(lambda f1=ps_sums(20), f2=ps_scale(2): (f1(), f2()))
            pslices.append(ps_quant(w2q_d, range(16, 24), w2_src, 2, D))
            pslices.append(ps_quant(w2q_d, range(24, 32), w2_src, 2, D))
            pslices.append(ps_load(w2t, range(0, 16), w2_dst, True))
            pslices.append(ps_quant(w2q_d, range(0, 8), w2_src, 2, D))
            pslices.append(lambda f1=ps_quant(w2q_d, range(8, 16), w2_src, 2, D),
                           f2=(lambda: S.op("vector", lambda e: e.tensor_scalar(
                               out=p_cw[:, 0:96], in0=p_cw[:, 0:96],
                               scalar1=wsc[:, 1:2], scalar2=None, op0=ALU.mult),
                               wait=((ac, S.n(ac)),), inc=dv)): (f1(), f2()))

            S.op("vector", lambda e: e.memset(S_m[:, :], 0.0), inc=dv)


            st8 = {}

            def attn_ln1_a(j):
                gi_, ndv_ = st8[f"lnp{j}"]
                sc = j * 8
                xqb = xqd_t if j == 0 else xqd_t2
                S.op("scalar", lambda e, sc=sc, j=j, xqb=xqb: e.activation(
                    out=xqb[:, :], in_=x1_t[:, j * D:(j + 1) * D], func=AF.Square,
                    accum_out=stat[:, sc + 2:sc + 3]),
                    wait=((dv, max(ndv_, st8.get(f"xqd_last{j}", 0))),
                          (pe, st8.get(f"xqdT_pe{j}", 0))), inc=ac)
                nac = S.n(ac)
                S.op("vector", lambda e, sc=sc: e.tensor_add(
                    stat[:, sc + 3:sc + 4], stat[:, sc:sc + 1],
                    stat[:, sc + 1:sc + 2]), wait=((ac, nac),), inc=dv)
                S.op("vector", lambda e, sc=sc: e.tensor_scalar(
                    out=stat[:, sc + 3:sc + 4], in0=stat[:, sc + 3:sc + 4],
                    scalar1=1.0 / D, scalar2=None, op0=ALU.mult), inc=dv)
                S.op("vector", lambda e, sc=sc: e.tensor_scalar(
                    out=stat[:, sc + 4:sc + 5], in0=stat[:, sc + 2:sc + 3],
                    scalar1=1.0 / D, scalar2=None, op0=ALU.mult), inc=dv)
                S.op("vector", lambda e, sc=sc: e.tensor_mul(
                    stat[:, sc + 5:sc + 6], stat[:, sc + 3:sc + 4],
                    stat[:, sc + 3:sc + 4]), inc=dv)
                S.op("vector", lambda e, sc=sc: e.tensor_sub(
                    stat[:, sc + 5:sc + 6], stat[:, sc + 4:sc + 5],
                    stat[:, sc + 5:sc + 6]), inc=dv)
                S.op("vector", lambda e, sc=sc: e.tensor_scalar(
                    out=stat[:, sc + 5:sc + 6], in0=stat[:, sc + 5:sc + 6],
                    scalar1=EPS, scalar2=None, op0=ALU.add), inc=dv)
                ndv = S.n(dv)
                S.op("scalar", lambda e, sc=sc: e.activation(
                    out=stat[:, sc + 6:sc + 7], in_=stat[:, sc + 5:sc + 6],
                    func=AF.Sqrt),
                    wait=((dv, ndv),), inc=ac)
                nac = S.n(ac)
                S.op("vector", lambda e, sc=sc: e.reciprocal(
                    stat[:, sc + 7:sc + 8], stat[:, sc + 6:sc + 7]),
                    wait=((ac, nac),), inc=dv)
                S.op("vector", lambda e, sc=sc, j=j: e.tensor_scalar(
                    out=x1_t[:, j * D:(j + 1) * D], in0=x1_t[:, j * D:(j + 1) * D],
                    scalar1=stat[:, sc + 3:sc + 4], scalar2=stat[:, sc + 7:sc + 8],
                    op0=ALU.subtract, op1=ALU.mult), inc=dv)
                S.op("vector", lambda e, sc=sc, j=j: e.reduce_max(
                    stat[:, sc + 4:sc + 5], x1_t[:, j * D:(j + 1) * D],
                    axis=AX.X, apply_absolute_value=True), inc=dv)
                S.op("vector", lambda e, sc=sc: e.tensor_scalar(
                    out=stat[:, sc + 4:sc + 5], in0=stat[:, sc + 4:sc + 5],
                    scalar1=EPS, scalar2=None, op0=ALU.max), inc=dv)
                S.op("vector", lambda e, sc=sc: e.reciprocal(
                    stat[:, sc + 5:sc + 6], stat[:, sc + 4:sc + 5]), inc=dv)
                S.op("vector", lambda e, sc=sc: e.tensor_scalar(
                    out=stat[:, sc + 5:sc + 6], in0=stat[:, sc + 5:sc + 6],
                    scalar1=127.0, scalar2=None, op0=ALU.mult), inc=dv)
                S.op("vector", lambda e, sc=sc: e.tensor_scalar(
                    out=stat[:, sc + 4:sc + 5], in0=stat[:, sc + 4:sc + 5],
                    scalar1=1.0 / 127.0, scalar2=None, op0=ALU.mult), inc=dv)
                for hf in range(2):
                    S.op("vector", lambda e, sc=sc, j=j, hf=hf: e.tensor_scalar(
                        out=xqi_t[:, :],
                        in0=x1_t[:, j * D + hf * 512: j * D + (hf + 1) * 512],
                        scalar1=stat[:, sc + 5:sc + 6], scalar2=None, op0=ALU.mult),
                        inc=dv)
                    S.op("vector", lambda e, sc=sc, hf=hf, xqb=xqb: e.tensor_scalar(
                        out=xqb[:, hf * 512:(hf + 1) * 512], in0=xqi_t[:, :],
                        scalar1=stat[:, sc + 4:sc + 5], scalar2=None, op0=ALU.mult),
                        inc=dv)
                st8[f"lnq{j}"] = S.n(dv)
                st8[f"xqd_last{j}"] = S.n(dv)

            def attn_ln1_b(j):
                gi_, _ = st8.pop(f"lnp{j}")
                ndv = st8[f"lnq{j}"]
                xqb = xqd_t if j == 0 else xqd_t2
                wl = [(dv, ndv), (ac, st8.get("b7_ac", 0))]
                for c in range(8):
                    S.op("tensor", lambda e, c=c, xqb=xqb: e.transpose(
                        B[7][:, c * 128:(c + 1) * 128],
                        xqb[:, c * 128:(c + 1) * 128], c_idb[:, :]),
                        wait=wl if c == 0 else (), inc=pe)
                npe = S.n(pe)
                st8[f"xqdT_pe{j}"] = npe
                for c in range(8):
                    S.op("scalar", lambda e, c=c, j=j: e.activation(
                        out=xqdT_t[:, c * 128:(c + 1) * 128],
                        in_=B[7][:, c * 128:(c + 1) * 128], func=AF.Copy),
                        wait=((pe, npe), ("sq0", st8.get("xqst0", 0)),
                              ("sq1", st8.get("xqst1", 0))) if c == 0 else (),
                        inc=ac)
                st8["b7_ac"] = S.n(ac)
                nac = S.n(ac)
                ndva = S.n(dv)
                tile = gi_ * 2 + j
                S.dma("sync", x1_d[tile, :, :], x1_t[:, j * D:(j + 1) * D],
                      f"s1{j}", wait=((dv, ndva),))
                if debug:
                    S.dma("sync", dbg["x1"][tile, :, :], x1_t[:, j * D:(j + 1) * D],
                          "dbg", serial=True)
                    S.dbg_fence()
                st8[f"x1st{j}"] = S.n(f"s1{j}")
                S.dma("sync",
                      xqdT_d[:, :, gi_ * GRP + j * 128: gi_ * GRP + j * 128 + 128],
                      xqdT_t[:, 0:D].rearrange("p (c t) -> p c t", c=8),
                      f"sq{j}", wait=((ac, nac),))
                if debug:
                    S.dma("sync",
                          dbg["xqdT"][:, :, gi_ * GRP + j * 128: gi_ * GRP + j * 128 + 128],
                          xqdT_t[:, 0:D].rearrange("p (c t) -> p c t", c=8),
                          "dbg", serial=True)
                    S.dbg_fence()
                st8[f"xqst{j}"] = S.n(f"sq{j}")

            def emit_group(gi, src_ap, full):
                gb = gi % 2
                xs = f"xm{gb}" if full else f"x{gb}"
                pre = st8.pop(f"xpre{gb}", None)
                if pre is None:
                    for j in range(2):
                        S.dma("gpsimd" if full else "sync",
                              xbuf[:, (gb * 2 + j) * D:(gb * 2 + j + 1) * D],
                              src_ap[gi * GRP + j * 128: gi * GRP + (j + 1) * 128, :],
                              xs,
                              wait=((dv, st8.get(f"xres{gb}", 0)),
                                    (pe, st8.get(f"xfree{gb}", 0)),))
                    nx = S.n(xs)
                else:
                    nx = pre
                for b4 in range(4):
                    bank = B[2 + b4 % 2]
                    wl = [(xs, nx), (ac, st8.get(f"vb{b4 % 2}_ac", 0))]
                    for s4 in range(4):
                        idx = b4 * 4 + s4
                        j, c = divmod(idx, 8)
                        S.op("tensor", lambda e, bank=bank, s4=s4, j=j, c=c, gb=gb: e.transpose(
                            bank[:, s4 * 128:(s4 + 1) * 128],
                            xbuf[:, (gb * 2 + j) * D + c * 128:(gb * 2 + j) * D + (c + 1) * 128],
                            c_idf[:, :]),
                            wait=wl if s4 == 0 else (), inc=pe)
                    npe = S.n(pe)
                    for s4 in range(4):
                        idx = b4 * 4 + s4
                        j, c = divmod(idx, 8)
                        S.op("scalar", lambda e, bank=bank, s4=s4, j=j, c=c, gb=gb: e.activation(
                            out=xT[:, gb * 8 * GRP + c * GRP + j * 128: gb * 8 * GRP + c * GRP + j * 128 + 128],
                            in_=bank[:, s4 * 128:(s4 + 1) * 128], func=AF.Copy),
                            wait=((pe, npe),) if s4 == 0 else (), inc=ac)
                    st8[f"vb{b4 % 2}_ac"] = S.n(ac)
                n_xT = S.n(ac)
                xTb = gb * 8 * GRP
                if full:
                    for j in range(2):
                        if st8.get(f"lnp{j}") is not None:
                            attn_ln1_a(j)

                for j in range(2):
                    for e4 in range(4):
                        bk = (j * 4 + e4) % 2
                        bank = B[2 + bk]
                        wl = [(ac, n_xT), (ac, st8.get(f"vb{bk}_ac", 0))]
                        for c in range(8):
                            S.op("tensor", lambda e, bank=bank, j=j, c=c, e4=e4, xTb=xTb: e.matmul(
                                bank[:, :],
                                xT[:, xTb + c * GRP + j * 128: xTb + c * GRP + j * 128 + 128],
                                w_v[:, c * E + e4 * 512: c * E + (e4 + 1) * 512],
                                start=(c == 0), stop=False),
                                wait=wl if c == 0 else (), inc=pe)
                        S.op("tensor", lambda e, bank=bank, e4=e4: e.matmul(
                            bank[:, :], c_ones[0:1, :],
                            p_bv[0:1, e4 * 512:(e4 + 1) * 512], start=False, stop=True), inc=pe)
                        npe = S.n(pe)
                        S.op("scalar", lambda e, bank=bank, j=j, e4=e4, gb=gb: e.activation(
                            out=v_tm[:, gb * 2 * E + j * E + e4 * 512: gb * 2 * E + j * E + (e4 + 1) * 512],
                            in_=bank[:, :], func=AF.Silu),
                            wait=((pe, npe), (pe, st8.get(f"vfree{gb}", 0))),
                            inc=ac)
                        st8[f"vb{bk}_ac"] = S.n(ac)
                n_v = S.n(ac)
                vb = gb * 2 * E
                if full:
                    for j in range(2):
                        if st8.get(f"lnp{j}") is not None:
                            attn_ln1_b(j)

                wl = [(ac, st8.get("vb0_ac", 0))]
                for c in range(8):
                    S.op("tensor", lambda e, c=c, xTb=xTb: e.matmul(
                        B[2][:, 0:GRP],
                        w_i[:, c * HD: c * HD + 128],
                        xT[:, xTb + c * GRP: xTb + (c + 1) * GRP],
                        start=(c == 0), stop=(c == 7)),
                        wait=wl if c == 0 else (), inc=pe)
                npe = S.n(pe)
                S.op("scalar", lambda e: e.activation(
                    out=x_fm[:, :], in_=B[2][:, 0:GRP], func=AF.Silu,
                    bias=binc_col[:, 0:1], scale=1.0),
                    wait=((pe, npe), (dv, st8.get("xfm_dv", 0)),
                          (ac, st8.get("xfm_ac", 0))), inc=ac)
                st8["vb0_ac"] = S.n(ac)
                n_xf = S.n(ac)

                S.op("scalar", lambda e: e.activation(
                    out=lk_f[:, :], in_=x_fm[:, :], func=AF.Identity,
                    bias=p_aff[:, 7:8], scale=p_aff[:, 6:7]),
                    wait=((ac, n_xf),), inc=ac)
                n_lkac = S.n(ac)
                st8["xfm_ac"] = n_lkac
                if full:
                    for ai, (dstt, gcol) in enumerate(((qq_f[:, 0:GRP], 0), (qk_f[:, 0:GRP], 2),
                                       (lq_f[:, gb * GRP:(gb + 1) * GRP], 4))):
                        S.op("vector", lambda e, dstt=dstt, gcol=gcol: e.tensor_scalar(
                            out=dstt, in0=x_fm[:, :],
                            scalar1=p_aff[:, gcol:gcol + 1],
                            scalar2=p_aff[:, gcol + 1:gcol + 2],
                            op0=ALU.mult, op1=ALU.add),
                            wait=((ac, n_xf),) if ai == 0 else (), inc=dv)
                n_aff = S.n(dv)
                st8["xfm_dv"] = n_aff

                wl = [(ac, max(n_lkac, st8.get("b7_ac", 0))),
                      (dv, st8.get("b7_dv", 0))]
                for cc in range(2):
                    S.op("tensor", lambda e, cc=cc: e.transpose(
                        B[7][:, cc * 128:(cc + 1) * 128], lk_f[:, cc * 128:(cc + 1) * 128],
                        c_idb[:, :]), wait=wl if cc == 0 else (), inc=pe)
                npe = S.n(pe)
                for cc in range(2):
                    S.op("scalar", lambda e, cc=cc: e.activation(
                        out=lk_tm[:, cc * 128:(cc + 1) * 128],
                        in_=B[7][:, cc * 128:(cc + 1) * 128], func=AF.Copy),
                        wait=((pe, npe),) if cc == 0 else (), inc=ac)
                st8["b7_ac"] = S.n(ac)
                n_lk = S.n(ac)

                if full:
                    S.op("vector", lambda e: e.tensor_copy(S_bf[:, :], S_m[:, :]),
                         wait=((pe, st8.get("sbf_pe", 0)),), inc=dv)
                n_scast = S.n(dv)

                for e4 in range(4):
                    wl = [(ac, n_v), (ac, n_lk), (dv, st8.get("kv_dv", 0)),
                          (dv, n_scast)]
                    for cc in range(2):
                        S.op("tensor", lambda e, cc=cc, e4=e4, vb=vb: e.matmul(
                            B[3][:, :],
                            lk_tm[:, cc * 128:(cc + 1) * 128],
                            v_tm[:, vb + cc * E + e4 * 512: vb + cc * E + (e4 + 1) * 512],
                            start=(cc == 0), stop=(cc == 1)),
                            wait=wl if cc == 0 else (), inc=pe)
                    npe = S.n(pe)
                    S.op("vector", lambda e, e4=e4: e.tensor_add(
                        S_m[:, e4 * 512:(e4 + 1) * 512],
                        S_m[:, e4 * 512:(e4 + 1) * 512], B[3][:, :]),
                        wait=((pe, npe),), inc=dv)
                    st8["kv_dv"] = S.n(dv)
                n_kv = S.n(dv)

                if not full:
                    st8[f"xfree{gb}"] = S.n(pe)
                    st8[f"xres{gb}"] = 0
                    st8[f"vfree{gb}"] = S.n(pe)
                    return

                wl = [(dv, n_aff), (ac, st8.get("b4_ac", 0)), (dv, st8.get("b4_dv", 0))]
                for mc in range(2):
                    S.op("tensor", lambda e, mc=mc: e.matmul(
                        B[4][:, mc * GRP:(mc + 1) * GRP],
                        qk_f[:, mc * 128:(mc + 1) * 128],
                        qq_f[:, 0:GRP], start=True, stop=True),
                        wait=wl if mc == 0 else (), inc=pe)
                npe = S.n(pe)
                for mc in range(2):
                    S.op("scalar", lambda e, mc=mc: e.activation(
                        out=at_bf[:, mc * GRP:(mc + 1) * GRP],
                        in_=B[4][:, mc * GRP:(mc + 1) * GRP], func=AF.Relu),
                        wait=((pe, npe),) if mc == 0 else (), inc=ac)
                st8["b4_ac"] = S.n(ac)
                n_rl = S.n(ac)
                for mc in range(2):
                    S.op("vector", lambda e, mc=mc: e.scalar_tensor_tensor(
                        out=at_bf[:, mc * GRP:(mc + 1) * GRP],
                        in0=at_bf[:, mc * GRP:(mc + 1) * GRP], scalar=1.0,
                        in1=at_bf[:, mc * GRP:(mc + 1) * GRP],
                        op0=ALU.mult, op1=ALU.mult),
                        wait=((ac, n_rl),) if mc == 0 else (), inc=dv)
                    S.op("vector", lambda e, mc=mc: e.tensor_mul(
                        at_bf[:, mc * GRP:(mc + 1) * GRP],
                        at_bf[:, mc * GRP:(mc + 1) * GRP],
                        c_m01[:, mc * GRP:(mc + 1) * GRP]), inc=dv)
                n_at = S.n(dv)

                for ec in range(16):
                    eb = ec % 2
                    gbk = B[5] if eb == 0 else B[0]
                    obk = B[4] if eb == 0 else B[6]
                    wl = [(ac, n_xT), (ac, st8.get(f"gb{eb}_ac", 0))]
                    if eb == 1 and ec == 1:
                        wl.append((dv, st8.get("resid0_dv", 0)))
                    if eb == 0 and ec == 0:
                        wl.append((dv, st8.get("resid1_dv", 0)))
                    for c in range(8):
                        S.op("tensor", lambda e, gbk=gbk, c=c, ec=ec, xTb=xTb: e.matmul(
                            gbk[:, 0:GRP],
                            w_g[:, c * E + ec * 128: c * E + (ec + 1) * 128],
                            xT[:, xTb + c * GRP: xTb + (c + 1) * GRP],
                            start=(c == 0), stop=(c == 7)),
                            wait=wl if c == 0 else (), inc=pe)
                    npe = S.n(pe)
                    S.op("scalar", lambda e, gbk=gbk, ec=ec, eb=eb: e.activation(
                        out=gate_p[:, eb * GRP:(eb + 1) * GRP],
                        in_=gbk[:, 0:GRP], func=AF.Silu,
                        bias=p_bgc[:, ec:ec + 1], scale=1.0),
                        wait=((pe, npe), (dv, st8.get(f"gate{eb}_dv", 0))), inc=ac)
                    st8[f"gb{eb}_ac"] = S.n(ac)
                    n_gate = S.n(ac)
                    wl = [(dv, n_at), (ac, n_v), (dv, max(st8.get(f"o{eb}_dv", 0),
                          st8.get("resid1_dv", 0) if eb == 1 else 0))]
                    for mc in range(2):
                        S.op("tensor", lambda e, obk=obk, mc=mc, ec=ec, vb=vb: e.matmul(
                            obk[:, 0:GRP],
                            v_tm[:, vb + mc * E + ec * 128: vb + mc * E + (ec + 1) * 128],
                            at_bf[:, mc * GRP:(mc + 1) * GRP],
                            start=(mc == 0), stop=False),
                            wait=wl if mc == 0 else (), inc=pe)
                    S.op("tensor", lambda e, obk=obk, ec=ec, gb=gb: e.matmul(
                        obk[:, 0:GRP], S_bf[:, ec * 128:(ec + 1) * 128],
                        lq_f[:, gb * GRP:(gb + 1) * GRP], start=False, stop=True), inc=pe)
                    npe = S.n(pe)
                    if ec == 15:
                        st8["sbf_pe"] = npe
                    S.op("vector", lambda e, obk=obk, ec=ec, eb=eb: e.scalar_tensor_tensor(
                        out=o_fm[:, ec * GRP:(ec + 1) * GRP],
                        in0=obk[:, 0:GRP], scalar=1.0,
                        in1=gate_p[:, eb * GRP:(eb + 1) * GRP],
                        op0=ALU.mult, op1=ALU.mult),
                        wait=((pe, npe), (ac, n_gate)), inc=dv)
                    st8[f"o{eb}_dv"] = S.n(dv)
                    st8[f"gate{eb}_dv"] = S.n(dv)
                st8["b4_dv"] = st8["o0_dv"]
                st8["b7_dv"] = 0
                n_omul = S.n(dv)

                wbanks = ((B[0], B[1]), (B[5], B[6]))
                for j in range(2):
                    wl0 = [(dv, n_omul), (dv, st8.get(f"resid{j}_dv", 0))]
                    for ec in range(16):
                        for hf in range(2):
                            bank = wbanks[j][hf]
                            S.op("tensor", lambda e, bank=bank, j=j, hf=hf, ec=ec: e.matmul(
                                bank[:, :],
                                o_fm[:, ec * GRP + j * 128: ec * GRP + j * 128 + 128],
                                w_o[:, ec * D + hf * 512: ec * D + (hf + 1) * 512],
                                start=(ec == 0), stop=False),
                                wait=wl0 if (ec == 0 and hf == 0) else (), inc=pe)
                    for hf in range(2):
                        bank = wbanks[j][hf]
                        S.op("tensor", lambda e, bank=bank, hf=hf: e.matmul(
                            bank[:, :], c_ones[0:1, :],
                            p_bout[0:1, hf * 512:(hf + 1) * 512], start=False, stop=True), inc=pe)
                    n_wout = S.n(pe)

                    sc = j * 8
                    for hf in range(2):
                        bank = wbanks[j][hf]
                        S.op("vector", lambda e, bank=bank, j=j, hf=hf, sc=sc, gb=gb: e.scalar_tensor_tensor(
                            out=x1_t[:, j * D + hf * 512: j * D + (hf + 1) * 512],
                            in0=bank[:, :], scalar=1.0,
                            in1=xbuf[:, (gb * 2 + j) * D + hf * 512:(gb * 2 + j) * D + (hf + 1) * 512],
                            op0=ALU.mult, op1=ALU.add,
                            accum_out=stat[:, sc + hf:sc + hf + 1]),
                            wait=((pe, n_wout), (f"s1{j}", st8.get(f"x1st{j}", 0))) if hf == 0 else (),
                            inc=dv)
                    st8[f"resid{j}_dv"] = S.n(dv)
                    if j == 1:
                        st8[f"xres{gb}"] = S.n(dv)
                    st8[f"lnp{j}"] = (gi, S.n(dv))
                    pass
                st8[f"xfree{gb}"] = S.n(pe)
                st8[f"vfree{gb}"] = S.n(pe)


            env.update(dict(
                c_ones=c_ones, c_onesf=c_onesf, c_idb=c_idb, c_idf=c_idf,
                stat=stat, hal_xqdT=hal_xqdT, hal_xT=hal_xT, p_aff=p_aff,
                p_bv=p_bv, p_bg=p_bg, p_bout=p_bout, binc_col=binc_col,
                w_v=w_v, w_i=w_i, x_fm=x_fm, qk_f=qk_f,
                v_tm=v_tm, Sh_bf=Sh_bf, S_bf=S_bf, xh=xh))

            S.op("tensor", lambda e: None, wait=(("gw", n_wload_pre), ("lc", S.n("lc"))))
            for gi in range(NGP):
                if gi == NGP - 1:
                    S.op("vector", lambda e: e.tensor_copy(Sh_bf[:, :], S_m[:, :]),
                         inc=dv)
                emit_group(gi, xp, full=False)
                if gi + 1 < NGP:
                    gb2 = (gi + 1) % 2
                    for j in range(2):
                        S.dma("sync", xbuf[:, (gb2 * 2 + j) * D:(gb2 * 2 + j + 1) * D],
                              xp[(gi + 1) * GRP + j * 128:(gi + 1) * GRP + (j + 1) * 128, :],
                              f"x{gb2}",
                              wait=((dv, st8.get(f"xres{gb2}", 0)),
                                    (pe, st8.get(f"xfree{gb2}", 0)),))
                    st8[f"xpre{gb2}"] = S.n(f"x{gb2}")
                if pslices:
                    pslices.pop(0)()
            S.op("vector", lambda e: e.tensor_scalar(
                out=S_m[:, :], in0=S_m[:, :], scalar1=c_pm[:, 0:1],
                scalar2=None, op0=ALU.mult), inc=dv)
            while pslices:
                pslices.pop(0)()
            pre_scope.close()
            w_g = asb("w_g", [128, 8 * E], BF16)
            w_o = asb("w_o", [128, 16 * D], BF16)
            at_bf = asb("at_bf", [128, 2 * GRP], BF16)
            gate_p = asb("gate_p", [128, 2 * GRP], BF16)
            o_fm = asb("o_fm", [128, 16 * GRP], BF16)
            x1_t = asb("x1_t", [128, 2 * D], F32)
            xqi_t = asb("xqi_t", [128, 512], I32)
            xqd_t = asb("xqd_t", [128, D], BF16)
            xqd_t2 = asb("xqd_t2", [128, D], BF16)
            xqdT_t = asb("xqdT_t", [128, D], BF16)
            for k in range(8):
                S.dma("gpsimd", w_g[:, k * E:(k + 1) * E], wg[k * 128:(k + 1) * 128, :],
                      "gw2", wait=((dv, prest["dv"]),) if k == 0 else ())
            for k in range(16):
                S.dma("gpsimd", w_o[:, k * D:(k + 1) * D], wo[k * 128:(k + 1) * 128, :], "gw2")
            n_wload = S.n("gw2")
            env.update(locals())
            S.op("tensor", lambda e: None, wait=(("gw2", n_wload),))
            env["pre_last_gb"] = (NGP - 1) % 2
            emit_halo(nc, S, B, env, which=0)
            st8["xqdT_pe0"] = S.n(pe)
            st8["xqdT_pe1"] = S.n(pe)
            S.barrier()

            for gi in range(NG):
                emit_group(gi, xm, full=True)
            for j in range(2):
                if st8.get(f"lnp{j}") is not None:
                    attn_ln1_a(j)
            for j in range(2):
                if st8.get(f"lnp{j}") is not None:
                    attn_ln1_b(j)
            S.op("vector", lambda e: e.tensor_copy(S_bf[:, :], S_m[:, :]), inc=dv)
            emit_halo(nc, S, B, env, which=1)

            if debug:
                ndv = S.n(dv)
                S.dma("sync", dbg["S"][:, :], S_m[:, :], "dbg", wait=((dv, ndv),), serial=True)
                S.dma("sync", dbg["halxq"][:, :], hal_xqdT[:, :], "dbg",
                      wait=((ac, S.n(ac)),), serial=True)
                S.dbg_fence()
            S.barrier()

        with contextlib.ExitStack() as ffn_scope:
            fsb = lambda n, s, d: ffn_scope.enter_context(nc.sbuf_tensor(n, s, d))
            w1q = fsb("w1q", [128, 8 * DH], BF16)
            w2q = fsb("w2q", [128, 32 * D], BF16)
            hsl = fsb("hsl", [128, 2 * 32 * SL], BF16)
            cv_h = fsb("cv_h", [128, 8 * FCH], BF16)
            q8_h = fsb("q8_h", [128, 16 * FCH], I8)
            xq2 = fsb("xq2", [128, 32 * FCH], BF16)
            x1f = fsb("x1f", [128, 2 * D], F32)
            xqTf = fsb("xqTf", [128, 8 * FCH], BF16)
            xqTf2 = fsb("xqTf2", [128, 8 * FCH], BF16)

            s2r = fsb("s2r", [1, 256], F32)
            s2bc = fsb("s2bc", [128, FCH], F32)
            hal_cols = fsb("hal_cols", [128, 64], BF16)
            hal_hrow = fsb("hal_hrow", [2, 512], BF16)
            scr4 = fsb("scr4", [128, 512], BF16)

            for c in range(8):
                S.dma("sync", w1q[:, c * DH:(c + 1) * DH], w1q_d[c, :, :], "lwq",
                      wait=(("sw", S.n("sw")),) if c == 0 else ())
            for c in range(32):
                S.dma("sync", w2q[:, c * D:(c + 1) * D], w2q_d[c, :, :], "lwq")
            n_wq = S.n("lwq")

            for nsl in range(8):
                bank = B[nsl % 2]
                wl = [("lwq", n_wq), (ac, 0)]
                for c in range(8):
                    S.op("tensor", lambda e, bank=bank, nsl=nsl, c=c: e.matmul(
                        bank[0:2, 0:512], hal_xqdT[:, c * 2:(c + 1) * 2],
                        w1q[:, c * DH + nsl * 512: c * DH + (nsl + 1) * 512],
                        start=(c == 0), stop=(c == 7)),
                        wait=wl if c == 0 else (), inc=pe)
                npe = S.n(pe)
                S.op("scalar", lambda e, bank=bank, nsl=nsl: e.activation(
                    out=hal_hrow[0:2, :],
                    in_=bank[0:2, 0:512], func=AF.Copy,
                    scale=c_hm[0:2, 0:1]),
                    wait=((pe, npe),), inc=ac)
                nac = S.n(ac)
                for j in range(4):
                    S.op("tensor", lambda e, j=j: e.transpose(
                        B[7][:, j * 2:(j + 1) * 2], hal_hrow[0:2, j * 128:(j + 1) * 128],
                        c_idb[0:2, 0:2]),
                        wait=((ac, nac),) if j == 0 else (), inc=pe)
                npe = S.n(pe)
                for j in range(4):
                    cc = nsl * 4 + j
                    S.op("scalar", lambda e, j=j, cc=cc: e.activation(
                        out=hal_cols[:, cc * 2:(cc + 1) * 2], in_=B[7][:, j * 2:(j + 1) * 2],
                        func=AF.Copy), wait=((pe, npe),) if j == 0 else (), inc=ac)
            n_halc_ac = S.n(ac)
            fst_init_halh = True
            n_halc = 0

            fst = {"halh_ac": n_halc_ac, "psil": [0] * 8, "tpend": [], "fpend": [], "fchain": [0, 0], "b2pend": []}
            if debug:
                S.op("vector", lambda e: e.memset(hsl[:, :], 0.0), inc=dv)
                fst["memset_dv"] = S.n(dv)
            hview = lambda fb: hsl[:, fb * 32 * SL:(fb + 1) * 32 * SL].rearrange(
                "p (c s) -> p c s", c=32)

            halv = hal_cols[:, :].rearrange("p (c u) -> p c u", c=32)
            mx_t = xq2[:, 0:2 * FCH]
            mn_t = xq2[:, 2 * FCH:4 * FCH]

            def slvt(tfc, hc, a, b):
                bs = (tfc % 2) * 32 * SL
                return hsl[:, bs + hc * SL + a: bs + hc * SL + b]

            def emit_sf(tfc, ent):
                _, g0, cntC = ent
                psil = fst["psil"]
                nsil = []
                for i, k2 in enumerate(range(g0, g0 + 4)):
                    S.op("scalar", lambda e, tfc=tfc, k2=k2: e.activation(
                        out=slvt(tfc, k2, 1, 1 + FCH),
                        in_=cv_h[:, (k2 % 8) * FCH:((k2 % 8) + 1) * FCH],
                        func=AF.Silu),
                        wait=((dv, cntC[i]),), inc=ac, sw=False)
                    psil[k2 % 8] = S.n(ac)
                    nsil.append(S.n(ac))
                fst["fpend"].append((max(g0 + 5, 18), g0, nsil))

            def emit_fold(tfc, ent):
                _, g0, nsil = ent
                ch = fst["fchain"]
                for i, k2 in enumerate(range(g0, g0 + 4)):
                    for ci, (buf, opx) in enumerate(((mx_t, ALU.max), (mn_t, ALU.min))):
                        if k2 == 0:
                            wl = [(ac, nsil[i])]
                            if ci == 0:
                                wl.append((pe, fst.get("w2done_pe", 0)))
                            S.op("vector", lambda e, tfc=tfc, buf=buf, k2=k2: e.tensor_copy(
                                buf[:, 0:FCH], slvt(tfc, k2, 1, 1 + FCH)),
                                wait=wl, inc=dv, sw=False)
                        else:
                            S.op("vector", lambda e, tfc=tfc, buf=buf, opx=opx, k2=k2: e.tensor_tensor(
                                buf[:, 0:FCH], slvt(tfc, k2, 1, 1 + FCH),
                                buf[:, 0:FCH], op=opx),
                                wait=((ac, nsil[i]), (dv, ch[ci])), inc=dv, sw=False)
                        ch[ci] = S.n(dv)

            def tail_piece(tfc, hc, evacs):
                while fst["tpend"] and fst["tpend"][0][0] <= hc:
                    emit_sf(tfc, fst["tpend"].pop(0))
                while fst["fpend"] and fst["fpend"][0][0] <= hc:
                    emit_fold(tfc, fst["fpend"].pop(0))
                for _ in range(2):
                    if fst["b2pend"]:
                        fst["b2pend"].pop(0)()
                final = (tfc == NFC - 1)
                fb = tfc % 2
                fbn = (tfc + 1) % 2
                k = hc % 8
                hs = hc // 16
                psil = fst["psil"]
                if hc == 0 and tfc == 0:
                    S.op("vector", lambda e: e.tensor_copy(
                        hview(0)[:, :, 0:1], halv[:, :, 0:1]),
                        wait=((ac, n_halc_ac),), inc=dv, sw=False)
                if final:
                    if hc == 0:
                        S.op("vector", lambda e, fb=fb: e.tensor_copy(
                            hview(fb)[:, :, SL - 1:SL], halv[:, :, 1:2]),
                            wait=((ac, n_halc_ac),), inc=dv, sw=False)
                else:
                    if hc == 0:
                        S.op("vector", lambda e, fb=fb, fbn=fbn: e.tensor_copy(
                            hview(fbn)[:, :, 0:1], hview(fb)[:, :, FCH:FCH + 1]),
                            wait=((ac, evacs[0]),), inc=dv, sw=False)
                    S.op("vector", lambda e, hc=hc, fb=fb, fbn=fbn: e.tensor_copy(
                        hview(fb)[:, hc:hc + 1, SL - 1:SL],
                        hview(fbn)[:, hc:hc + 1, 1:2]),
                        wait=((ac, evacs[hc]),), inc=dv, sw=False)
                S.op("vector", lambda e, tfc=tfc, hc=hc, k=k: e.tensor_scalar(
                    out=cv_h[:, k * FCH:(k + 1) * FCH],
                    in0=slvt(tfc, hc, 1, 1 + FCH),
                    scalar1=p_cw[:, 32 + hc:33 + hc],
                    scalar2=p_cw[:, 96 + hc:97 + hc],
                    op0=ALU.mult, op1=ALU.add),
                    wait=((ac, max(psil[k], evacs[hc] if final else 0)),),
                    inc=dv, sw=False)
                if hc % 4 != 3:
                    return
                g0 = hc - 3
                nA = S.n(dv)
                for k2 in range(g0, hc + 1):
                    S.op("vector", lambda e, tfc=tfc, k2=k2: e.scalar_tensor_tensor(
                        out=cv_h[:, (k2 % 8) * FCH:((k2 % 8) + 1) * FCH],
                        in0=slvt(tfc, k2, 0, FCH),
                        scalar=p_cw[:, k2:k2 + 1],
                        in1=cv_h[:, (k2 % 8) * FCH:((k2 % 8) + 1) * FCH],
                        op0=ALU.mult, op1=ALU.add),
                        wait=((dv, nA),) if k2 == g0 else (), inc=dv, sw=False)
                nB = S.n(dv)
                cntC = []
                for k2 in range(g0, hc + 1):
                    S.op("vector", lambda e, tfc=tfc, k2=k2: e.scalar_tensor_tensor(
                        out=cv_h[:, (k2 % 8) * FCH:((k2 % 8) + 1) * FCH],
                        in0=slvt(tfc, k2, 2, 2 + FCH),
                        scalar=p_cw[:, 64 + k2:65 + k2],
                        in1=cv_h[:, (k2 % 8) * FCH:((k2 % 8) + 1) * FCH],
                        op0=ALU.mult, op1=ALU.add),
                        wait=((dv, nB),) if k2 == g0 else (), inc=dv, sw=False)
                    cntC.append(S.n(dv))
                fst["tpend"].append((g0 + 5, g0, cntC))
                if hc == 31:
                    while fst["tpend"]:
                        emit_sf(tfc, fst["tpend"].pop(0))
                    while fst["fpend"]:
                        emit_fold(tfc, fst["fpend"].pop(0))
                    nw0 = S.n(dv)
                    S.op("vector", lambda e: e.tensor_scalar(
                        out=mn_t[:, 0:FCH], in0=mn_t[:, 0:FCH], scalar1=-1.0,
                        scalar2=None, op0=ALU.mult),
                        wait=((dv, nw0),), inc=dv, sw=False)
                    S.op("vector", lambda e: e.tensor_max(
                        mx_t[:, 0:FCH], mx_t[:, 0:FCH], mn_t[:, 0:FCH]),
                        inc=dv, sw=True)
                    fst["cvfold_dv"] = S.n(dv)

            def tail_rest(fc):
                fb = fc % 2
                base = fb * 32 * SL
                ndv = fst["cvfold_dv"]
                for tc in range(2):
                    S.op("tensor", lambda e, tc=tc: e.transpose(
                        B[7][:, tc * 128:(tc + 1) * 128],
                        mx_t[:, tc * 128:(tc + 1) * 128], c_idb[:, :]),
                        wait=((dv, ndv), (dv, fst.get("s2_dv", 0)),
                              (ac, fst.get("ln2sq_ac", 0))) if tc == 0 else (),
                        inc=pe)
                npe = S.n(pe)
                for tc in range(2):
                    S.op("vector", lambda e, tc=tc: e.reduce_max(
                        stat[:, 20 + tc:21 + tc], B[7][:, tc * 128:(tc + 1) * 128],
                        axis=AX.X), wait=((pe, npe),) if tc == 0 else (), inc=dv)
                S.op("vector", lambda e: e.tensor_scalar(
                    out=stat[:, 20:22], in0=stat[:, 20:22], scalar1=EPS,
                    scalar2=None, op0=ALU.max), inc=dv, sw=True)
                S.op("vector", lambda e: e.reciprocal(stat[:, 22:23], stat[:, 20:21]), inc=dv, sw=True)
                S.op("vector", lambda e: e.reciprocal(stat[:, 23:24], stat[:, 21:22]), inc=dv, sw=True)
                S.op("vector", lambda e: e.tensor_scalar(
                    out=stat[:, 22:24], in0=stat[:, 22:24], scalar1=127.0,
                    scalar2=None, op0=ALU.mult), inc=dv, sw=True)
                S.op("vector", lambda e: e.tensor_scalar(
                    out=stat[:, 24:26], in0=stat[:, 20:22], scalar1=1.0 / 127.0,
                    scalar2=wsc[:, 3:4], op0=ALU.mult, op1=ALU.mult), inc=dv, sw=True)
                ndv = S.n(dv)
                for tc in range(2):
                    S.op("tensor", lambda e, tc=tc: e.transpose(
                        B[4][0:1, 256 + tc * 128:384 + tc * 128],
                        stat[:, 22 + tc:23 + tc], c_idf[:, :]),
                        wait=((dv, ndv), (ac, fst.get("ln2sq_ac", 0))) if tc == 0 else (),
                        inc=pe)
                npe = S.n(pe)
                for tc in range(2):
                    S.op("scalar", lambda e, tc=tc: e.activation(
                        out=s2r[0:1, tc * 128:(tc + 1) * 128],
                        in_=B[4][0:1, 256 + tc * 128:384 + tc * 128], func=AF.Copy),
                        wait=((pe, npe),) if tc == 0 else (), inc=ac)
                nac = S.n(ac)
                for tc in range(2):
                    S.op("tensor", lambda e, tc=tc: e.matmul(
                        B[4][:, tc * 128:(tc + 1) * 128], c_onesf[0:1, :],
                        s2r[0:1, tc * 128:(tc + 1) * 128], start=True, stop=True),
                        wait=((ac, nac),) if tc == 0 else (), inc=pe)
                npe = S.n(pe)
                S.op("scalar", lambda e: e.activation(
                    out=s2bc[:, :], in_=B[4][:, 0:256], func=AF.Copy),
                    wait=((pe, npe),), inc=ac)
                n_s2 = S.n(ac)
                fst["s2bc_ac"] = n_s2

                s2v = s2bc[:, :]
                s2ap = [list(p) for p in s2v.ap]
                s2ap = [s2ap[0], [0, 16], s2ap[1]]
                s2b3 = bass.AP(tensor=s2v.tensor, offset=s2v.offset, ap=s2ap)
                n_q2h = [0, 0]
                for hs in range(2):
                    hv = hsl[:, base + hs * 16 * SL: base + (hs + 1) * 16 * SL].rearrange(
                        "p (c s) -> p c s", c=16)
                    S.op("vector", lambda e, hv=hv: e.tensor_mul(
                        q8_h[:, :].rearrange("p (c t) -> p c t", c=16),
                        hv[:, :, 1:1 + FCH], s2b3),
                        wait=((ac, n_s2), (dv, fst.get("q8_dv", 0))), inc=dv, sw=False)
                    S.op("vector", lambda e, hs=hs: e.tensor_copy(
                        xq2[:, hs * 16 * FCH:(hs + 1) * 16 * FCH], q8_h[:, :]),
                        inc=dv, sw=True)
                    fst["q8_dv"] = S.n(dv)
                    n_q2h[hs] = S.n(dv)
                n_q2 = S.n(dv)
                fst["s2_dv"] = n_q2
                fst[f"slabfree{fb}0"] = n_q2h[0]
                fst[f"slabfree{fb}1"] = n_q2h[1]
                fst["n_q2h"] = n_q2h

            def tail_rest_b(fc):
                fb = fc % 2
                n_q2h = fst.pop("n_q2h")
                banksets = ((B[5], B[6]), (B[0], B[1]))
                npe_t = [0, 0]
                for tt in range(2):
                    tile = fc * 2 + tt
                    xb = tt
                    S.dma("sync", x1f[:, xb * D:(xb + 1) * D], x1_d[tile, :, :],
                          f"lx1{xb}",
                          wait=((dv, fst.get(f"x1used{xb}", 0)),
                                (f"so{xb}", fst.get(f"o2st{xb}", 0))))
                    fst[f"nld_t{tt}"] = S.n(f"lx1{xb}")
                    for hf in range(2):
                        bank = banksets[tt][hf]
                        for hc in range(32):
                            wl = []
                            if hc == 0:
                                wl = [(dv, max(n_q2h[0], fst.get(f"w2o{tt}{hf}_dv", 0)))]
                                if tt == 1:
                                    wl.append((ac, fst.get("halh_ac", 0)))
                            elif hc == 16:
                                wl = [(dv, n_q2h[1])]
                            S.op("tensor", lambda e, bank=bank, tt=tt, hf=hf, hc=hc: e.matmul(
                                bank[:, :],
                                xq2[:, hc * FCH + tt * 128: hc * FCH + tt * 128 + 128],
                                w2q[:, hc * D + hf * 512: hc * D + (hf + 1) * 512],
                                start=(hc == 0), stop=(hc == 31)),
                                wait=wl, inc=pe)
                    npe_t[tt] = S.n(pe)
                fst["npe_t"] = npe_t
                fst["w2done_pe"] = npe_t[1]

            def tail_rest_b2(fc):
                fb = fc % 2
                npe_t = fst.pop("npe_t")
                banksets = ((B[5], B[6]), (B[0], B[1]))
                th = fst["b2pend"]

                def ep_stt(tt, hf):
                    xb = tt
                    scc = 24 + tt
                    bank = banksets[tt][hf]
                    wl = ((pe, npe_t[tt]), (f"lx1{xb}", fst[f"nld_t{tt}"])) if hf == 0 else ()
                    S.op("vector", lambda e, bank=bank, xb=xb, hf=hf, scc=scc, tt=tt: e.scalar_tensor_tensor(
                        out=x1f[:, xb * D + hf * 512: xb * D + (hf + 1) * 512],
                        in0=bank[:, :], scalar=stat[:, scc:scc + 1],
                        in1=x1f[:, xb * D + hf * 512: xb * D + (hf + 1) * 512],
                        op0=ALU.mult, op1=ALU.add,
                        accum_out=stat[:, 28 + hf * 2 + tt:29 + hf * 2 + tt]),
                        wait=wl, inc=dv, sw=False)
                    fst[f"w2o{tt}{hf}_dv"] = S.n(dv)
                    fst[f"b2stt{tt}"] = S.n(dv)

                def ep_sq(tt, hf):
                    xb = tt
                    wl = [(ac, S.n(ac))]
                    if hf == 0:
                        wl.append((dv, fst[f"b2stt{tt}"]))
                    S.op("scalar", lambda e, xb=xb, hf=hf, tt=tt: e.activation(
                        out=scr4[:, :],
                        in_=x1f[:, xb * D + hf * 512: xb * D + (hf + 1) * 512],
                        func=AF.Square,
                        accum_out=stat[:, 20 + hf * 2 + tt:21 + hf * 2 + tt]),
                        wait=wl, inc=ac, sw=False)
                    fst["ln2sq_ac"] = S.n(ac)

                def chain(i):
                    if i == 0:
                        S.op("vector", lambda e: e.tensor_add(
                            stat[:, 28:30], stat[:, 28:30], stat[:, 30:32]),
                            wait=((ac, fst["ln2sq_ac"]),), inc=dv, sw=False)
                    elif i == 1:
                        S.op("vector", lambda e: e.tensor_scalar(
                            out=stat[:, 28:30], in0=stat[:, 28:30],
                            scalar1=1.0 / D, scalar2=None, op0=ALU.mult),
                            inc=dv, sw=True)
                    elif i == 2:
                        S.op("vector", lambda e: e.tensor_add(
                            stat[:, 20:22], stat[:, 20:22], stat[:, 22:24]),
                            inc=dv, sw=True)
                    elif i == 3:
                        S.op("vector", lambda e: e.tensor_scalar(
                            out=stat[:, 20:22], in0=stat[:, 20:22],
                            scalar1=1.0 / D, scalar2=None, op0=ALU.mult),
                            inc=dv, sw=True)
                    elif i == 4:
                        S.op("vector", lambda e: e.tensor_mul(
                            stat[:, 22:24], stat[:, 28:30], stat[:, 28:30]),
                            inc=dv, sw=True)
                    elif i == 5:
                        S.op("vector", lambda e: e.tensor_sub(
                            stat[:, 22:24], stat[:, 20:22], stat[:, 22:24]),
                            inc=dv, sw=True)
                    elif i == 6:
                        S.op("vector", lambda e: e.tensor_scalar(
                            out=stat[:, 22:24], in0=stat[:, 22:24], scalar1=EPS,
                            scalar2=None, op0=ALU.add), inc=dv, sw=True)
                    elif i == 7:
                        ndv = S.n(dv)
                        S.op("scalar", lambda e: e.activation(
                            out=stat[:, 20:22], in_=stat[:, 22:24], func=AF.Sqrt),
                            wait=((dv, ndv),), inc=ac, sw=False)
                    else:
                        nac = S.n(ac)
                        S.op("vector", lambda e: e.reciprocal(
                            stat[:, 22:24], stat[:, 20:22]),
                            wait=((ac, nac),), inc=dv, sw=False)

                def fin(tt):
                    xb = tt
                    S.op("vector", lambda e, xb=xb, tt=tt: e.tensor_scalar(
                        out=x1f[:, xb * D:(xb + 1) * D], in0=x1f[:, xb * D:(xb + 1) * D],
                        scalar1=stat[:, 28 + tt:29 + tt], scalar2=stat[:, 22 + tt:23 + tt],
                        op0=ALU.subtract, op1=ALU.mult),
                        inc=dv, sw=True)
                    fst[f"x1used{xb}"] = S.n(dv)
                    ndv = S.n(dv)
                    tile = fc * 2 + tt
                    S.dma("sync", out[tile * 128:(tile + 1) * 128, :],
                          x1f[:, xb * D:(xb + 1) * D], f"so{xb}", wait=((dv, ndv),))
                    fst[f"o2st{xb}"] = S.n(f"so{xb}")

                for tt in range(2):
                    for hf in range(2):
                        th.append(lambda tt=tt, hf=hf: ep_stt(tt, hf))
                for tt in range(2):
                    for hf in range(2):
                        th.append(lambda tt=tt, hf=hf: ep_sq(tt, hf))
                for i in range(9):
                    th.append(lambda i=i: chain(i))
                th.append(lambda: fin(0))
                th.append(lambda: fin(1))

            def ffn_chunk(fc):
                fb = fc % 2
                xqT = xqTf if fb == 0 else xqTf2
                S.dma("gpsimd", xqT[:, :].rearrange("p (c t) -> p c t", c=8),
                      xqdT_d[:, :, fc * FCH:(fc + 1) * FCH], f"lxq{fb}",
                      wait=((pe, fst.get(f"xqTfree{fb}", 0)),))
                nld = S.n(f"lxq{fb}")
                tp = fst.pop("tailpend", None)
                evac_ac = []
                for hc in range(32):
                    bank = (B[2], B[3], B[4])[hc % 3]
                    wl = [(f"lxq{fb}", nld), ("lwq", n_wq),
                          (ac, fst.get(f"h{hc % 3}_ac", 0))]
                    if hc == 2:
                        wl.append((ac, fst.get("s2bc_ac", 0)))
                    for c in range(8):
                        S.op("tensor", lambda e, bank=bank, xqT=xqT, hc=hc, c=c: e.matmul(
                            bank[:, 0:FCH],
                            w1q[:, c * DH + hc * 128: c * DH + (hc + 1) * 128],
                            xqT[:, c * FCH:(c + 1) * FCH],
                            start=(c == 0), stop=(c == 7)),
                            wait=wl if c == 0 else (), inc=pe)
                    npe = S.n(pe)
                    S.op("scalar", lambda e, bank=bank, hc=hc, fb=fb: e.activation(
                        out=hsl[:, fb * 32 * SL + hc * SL + 1: fb * 32 * SL + hc * SL + 1 + FCH],
                        in_=bank[:, 0:FCH], func=AF.Copy),
                        wait=((pe, npe), (dv, max(fst.get(f"slabfree{fb}{hc // 16}", 0),
                                                  fst.get("memset_dv", 0)))),
                        inc=ac)
                    fst[f"h{hc % 3}_ac"] = S.n(ac)
                    evac_ac.append(S.n(ac))
                    if tp is not None:
                        tail_piece(tp, hc, evac_ac)
                    if hc == 8 and fst.get("restpend") is not None:
                        tail_rest_b(fst["restpend"])
                    if hc == 15 and fst.get("restpend") is not None:
                        tail_rest_b2(fst.pop("restpend"))
                n_h = S.n(ac)
                if debug and fc == 0:
                    S.dma("sync", dbg["h"][:, :], hsl[:, 0:32 * SL], "dbg",
                          wait=((ac, n_h),), serial=True)
                    S.dbg_fence()
                fst[f"xqTfree{fb}"] = S.n(pe)
                while fst["b2pend"]:
                    fst["b2pend"].pop(0)()
                if tp is not None:
                    tail_rest(tp)
                    fst["restpend"] = tp
                fst["tailpend"] = fc
                if fc == NFC - 1:
                    fst.pop("tailpend")
                    rp = fst.pop("restpend", None)
                    if rp is not None:
                        tail_rest_b(rp)
                        tail_rest_b2(rp)
                    for hc in range(32):
                        tail_piece(fc, hc, evac_ac)
                    tail_rest(fc)
                    tail_rest_b(fc)
                    tail_rest_b2(fc)
                    while fst["b2pend"]:
                        fst["b2pend"].pop(0)()

            for fc in range(NFC):
                ffn_chunk(fc)

            S.op("sync", lambda e: None,
                 wait=(("so0", S.n("so0")), ("so1", S.n("so1"))))

        with nc.Block() as block:
            S.emit(block)
    return nc


def emit_halo(nc, S, B, env, which):
    pe, ac, dv = "pe", "ac", "dv"
    c_ones, c_onesf = env["c_ones"], env["c_onesf"]
    c_idb = env["c_idb"]
    stat = env["stat"]
    aff = env["p_aff"]
    x1_t, xqi_t, xqd_t = env["x1_t"], env["xqi_t"], env["xqd_t"]
    h_y = x1_t[0:1, 0:D]
    hx = x1_t[0:1, D:2 * D]
    h_xq = xqd_t[0:1, 0:D]
    h_q = xqi_t[0:1, 0:512]
    vb15 = env.get("pre_last_gb", 1) * 2 * E

    with contextlib.ExitStack() as hs:
        hsb = lambda n, s, d: hs.enter_context(nc.sbuf_tensor(n, s, d))
        h_ve = hsb(f"h_ve{which}", [1, 512], BF16)
        h_ga = hsb(f"h_ga{which}", [1, 512], BF16)
        h_ob = hsb(f"h_ob{which}", [1, 512], BF16)
        h_xf = hsb(f"h_xf{which}", [128, 4], F32)
        hq_b = hsb(f"hq_b{which}", [128, 4], BF16)
        h_at = hsb(f"h_at{which}", [128, 2], BF16)
        h_oT = hsb(f"h_oT{which}", [128, 16], BF16)

        xcol = lambda c: env["hal_xT"][:, c * 2 + which: c * 2 + which + 1]
        n0dv, n0ac = S.n(dv), S.n(ac)
        w_drain = (("s10", S.n("s10")), ("s11", S.n("s11")),
                   ("sq0", S.n("sq0")), ("sq1", S.n("sq1")))
        for engx in ("vector", "scalar"):
            S.op(engx, lambda e: None, wait=w_drain[:2])
            S.op(engx, lambda e: None, wait=w_drain[2:])

        for c in range(8):
            S.op("tensor", lambda e, c=c: e.matmul(
                B[5][:, 0:1], env["w_i"][:, c * HD: c * HD + 128], xcol(c),
                start=(c == 0), stop=(c == 7)),
                wait=((dv, n0dv), (ac, n0ac)) if c == 0 else (), inc=pe)
        npe = S.n(pe)
        S.op("scalar", lambda e: e.activation(
            out=h_xf[:, 0:1], in_=B[5][:, 0:1], func=AF.Silu,
            bias=env["binc_col"][:, 0:1], scale=1.0), wait=((pe, npe),), inc=ac)
        nac = S.n(ac)
        for col, gcol in ((1, 0), (2, 2), (3, 4)):
            S.op("vector", lambda e, col=col, gcol=gcol: e.tensor_scalar(
                out=h_xf[:, col:col + 1], in0=h_xf[:, 0:1],
                scalar1=aff[:, gcol:gcol + 1], scalar2=aff[:, gcol + 1:gcol + 2],
                op0=ALU.mult, op1=ALU.add),
                wait=((ac, nac),) if col == 1 else (), inc=dv)
        S.op("vector", lambda e: e.tensor_copy(hq_b[:, 0:4], h_xf[:, 0:4]), inc=dv)
        ndv = S.n(dv)

        if which == 0:
            S.op("vector", lambda e: e.tensor_scalar(
                out=env["qk_f"][:, 0:GRP], in0=env["x_fm"][:, :],
                scalar1=aff[:, 2:3], scalar2=aff[:, 3:4],
                op0=ALU.mult, op1=ALU.add), inc=dv)
            ndv = S.n(dv)
            for mc in range(2):
                S.op("tensor", lambda e, mc=mc: e.matmul(
                    B[6][:, mc: mc + 1],
                    env["qk_f"][:, mc * 128:(mc + 1) * 128], hq_b[:, 1:2],
                    start=True, stop=True),
                    wait=((dv, ndv),) if mc == 0 else (), inc=pe)
            npe = S.n(pe)
            S.op("scalar", lambda e: e.activation(
                out=h_at[:, 0:2], in_=B[6][:, 0:2], func=AF.Relu),
                wait=((pe, npe),), inc=ac)
            nac = S.n(ac)
            S.op("vector", lambda e: e.scalar_tensor_tensor(
                out=h_at[:, 0:2], in0=h_at[:, 0:2], scalar=1.0,
                in1=h_at[:, 0:2], op0=ALU.mult, op1=ALU.mult),
                wait=((ac, nac),), inc=dv)
        else:
            S.op("tensor", lambda e: e.matmul(
                B[6][0:1, 0:1], hq_b[:, 1:2], hq_b[:, 2:3], start=True, stop=True),
                wait=((dv, ndv),), inc=pe)
            npe = S.n(pe)
            S.op("scalar", lambda e: e.activation(
                out=stat[0:1, 19:20], in_=B[6][0:1, 0:1], func=AF.Relu),
                wait=((pe, npe),), inc=ac)
            nac = S.n(ac)
            S.op("vector", lambda e: e.scalar_tensor_tensor(
                out=stat[0:1, 19:20], in0=stat[0:1, 19:20], scalar=1.0,
                in1=stat[0:1, 19:20], op0=ALU.mult, op1=ALU.mult),
                wait=((ac, nac),), inc=dv)
        n_att = S.n(dv)

        nprev_ac = S.n(ac)
        for e4 in range(4):
            wl = [(dv, n_att), (ac, nprev_ac)]
            for c in range(8):
                S.op("tensor", lambda e, c=c, e4=e4: e.matmul(
                    B[5][0:1, :], xcol(c),
                    env["w_g"][:, c * E + e4 * 512: c * E + (e4 + 1) * 512],
                    start=(c == 0), stop=False),
                    wait=wl if c == 0 else (), inc=pe)
            S.op("tensor", lambda e, e4=e4: e.matmul(
                B[5][0:1, :], c_ones[0:1, 0:1],
                env["p_bg"][0:1, e4 * 512:(e4 + 1) * 512],
                start=False, stop=True), inc=pe)
            npe = S.n(pe)
            S.op("scalar", lambda e: e.activation(
                out=h_ga[0:1, :], in_=B[5][0:1, :], func=AF.Silu),
                wait=((pe, npe),), inc=ac)
            nga = S.n(ac)
            if which == 1:
                for c in range(8):
                    S.op("tensor", lambda e, c=c, e4=e4: e.matmul(
                        B[5][0:1, :], xcol(c),
                        env["w_v"][:, c * E + e4 * 512: c * E + (e4 + 1) * 512],
                        start=(c == 0), stop=False),
                        wait=((ac, nga),) if c == 0 else (), inc=pe)
                S.op("tensor", lambda e, e4=e4: e.matmul(
                    B[5][0:1, :], c_ones[0:1, 0:1],
                    env["p_bv"][0:1, e4 * 512:(e4 + 1) * 512],
                    start=False, stop=True), inc=pe)
                npe = S.n(pe)
                S.op("scalar", lambda e: e.activation(
                    out=h_ve[0:1, :], in_=B[5][0:1, :], func=AF.Silu),
                    wait=((pe, npe),), inc=ac)
            n_ve = S.n(ac)
            if which == 0:
                for mc in range(2):
                    S.op("tensor", lambda e, mc=mc, e4=e4: e.matmul(
                        B[6][0:1, :], h_at[:, mc:mc + 1],
                        env["v_tm"][:, vb15 + mc * E + e4 * 512: vb15 + mc * E + (e4 + 1) * 512],
                        start=(mc == 0), stop=False),
                        wait=((dv, n_att),) if mc == 0 else (), inc=pe)
                S.op("tensor", lambda e, e4=e4: e.matmul(
                    B[6][0:1, :], hq_b[:, 3:4],
                    env["Sh_bf"][:, e4 * 512:(e4 + 1) * 512],
                    start=False, stop=True), inc=pe)
            else:
                S.op("tensor", lambda e, e4=e4: e.matmul(
                    B[6][0:1, :], hq_b[:, 3:4],
                    env["S_bf"][:, e4 * 512:(e4 + 1) * 512], start=True, stop=True),
                    wait=((dv, n_att),), inc=pe)
            npe = S.n(pe)
            S.op("scalar", lambda e: e.activation(
                out=h_ob[0:1, :], in_=B[6][0:1, :], func=AF.Copy),
                wait=((pe, npe),), inc=ac)
            nac = S.n(ac)
            if which == 1:
                S.op("vector", lambda e: e.scalar_tensor_tensor(
                    out=h_ob[0:1, :], in0=h_ve[0:1, :], scalar=stat[0:1, 19:20],
                    in1=h_ob[0:1, :], op0=ALU.mult, op1=ALU.add),
                    wait=((ac, max(nac, n_ve)),), inc=dv)
            S.op("vector", lambda e: e.tensor_mul(
                h_ob[0:1, :], h_ob[0:1, :], h_ga[0:1, :]),
                wait=((ac, max(nac, nga)),), inc=dv)
            ndv = S.n(dv)
            for s4 in range(4):
                S.op("tensor", lambda e, s4=s4: e.transpose(
                    B[7][:, s4 * 128:s4 * 128 + 1],
                    h_ob[0:1, s4 * 128:(s4 + 1) * 128], c_idb[0:1, 0:1]),
                    wait=((dv, ndv),) if s4 == 0 else (), inc=pe)
            npe = S.n(pe)
            for s4 in range(4):
                cix = e4 * 4 + s4
                S.op("scalar", lambda e, s4=s4, cix=cix: e.activation(
                    out=h_oT[:, cix:cix + 1], in_=B[7][:, s4 * 128:s4 * 128 + 1],
                    func=AF.Copy), wait=((pe, npe),) if s4 == 0 else (), inc=ac)
            nprev_ac = S.n(ac)
        nac = S.n(ac)

        nprev = nac
        for hf in range(2):
            for c in range(16):
                S.op("tensor", lambda e, hf=hf, c=c: e.matmul(
                    B[5][0:1, :], h_oT[:, c:c + 1],
                    env["w_o"][:, c * D + hf * 512: c * D + (hf + 1) * 512],
                    start=(c == 0), stop=False),
                    wait=((ac, nprev),) if c == 0 else (), inc=pe)
            S.op("tensor", lambda e, hf=hf: e.matmul(
                B[5][0:1, :], c_ones[0:1, 0:1], env["p_bout"][0:1, hf * 512:(hf + 1) * 512],
                start=False, stop=True), inc=pe)
            npe = S.n(pe)
            S.op("scalar", lambda e, hf=hf: e.activation(
                out=h_y[0:1, hf * 512:(hf + 1) * 512], in_=B[5][0:1, :],
                func=AF.Copy), wait=((pe, npe),), inc=ac)
            nprev = S.n(ac)
        nac = S.n(ac)
        S.dma("sync", hx[0:1, :], env["xh"][which:which + 1, :], "lhx", serial=True,
              wait=(("s10", S.n("s10")), ("s11", S.n("s11"))))
        nld = S.n("lhx")
        S.op("vector", lambda e: e.tensor_add(h_y[0:1, :], h_y[0:1, :], hx[0:1, :]),
             wait=((ac, nac), ("lhx", nld)), inc=dv)
        S.op("vector", lambda e: e.reduce_sum(stat[0:1, 16:17], h_y[0:1, :], axis=AX.X),
             inc=dv)
        ndv = S.n(dv)
        S.op("scalar", lambda e: e.activation(
            out=h_xq[0:1, :], in_=h_y[0:1, :], func=AF.Square,
            accum_out=stat[0:1, 17:18]), wait=((dv, ndv),), inc=ac)
        nac = S.n(ac)
        S.op("vector", lambda e: e.tensor_scalar(
            out=stat[0:1, 16:18], in0=stat[0:1, 16:18], scalar1=1.0 / D,
            scalar2=None, op0=ALU.mult), wait=((ac, nac),), inc=dv)
        S.op("vector", lambda e: e.tensor_mul(
            stat[0:1, 18:19], stat[0:1, 16:17], stat[0:1, 16:17]), inc=dv)
        S.op("vector", lambda e: e.tensor_sub(
            stat[0:1, 18:19], stat[0:1, 17:18], stat[0:1, 18:19]), inc=dv)
        S.op("vector", lambda e: e.tensor_scalar(
            out=stat[0:1, 18:19], in0=stat[0:1, 18:19], scalar1=EPS,
            scalar2=None, op0=ALU.add), inc=dv)
        ndv = S.n(dv)
        S.op("scalar", lambda e: e.activation(
            out=stat[0:1, 17:18], in_=stat[0:1, 18:19], func=AF.Sqrt),
            wait=((dv, ndv),), inc=ac)
        nac = S.n(ac)
        S.op("vector", lambda e: e.reciprocal(stat[0:1, 18:19], stat[0:1, 17:18]),
             wait=((ac, nac),), inc=dv)
        S.op("vector", lambda e: e.tensor_scalar(
            out=h_y[0:1, :], in0=h_y[0:1, :], scalar1=stat[0:1, 16:17],
            scalar2=stat[0:1, 18:19], op0=ALU.subtract, op1=ALU.mult), inc=dv)
        S.op("vector", lambda e: e.reduce_max(
            stat[0:1, 16:17], h_y[0:1, :], axis=AX.X, apply_absolute_value=True),
            inc=dv)
        S.op("vector", lambda e: e.tensor_scalar(
            out=stat[0:1, 16:17], in0=stat[0:1, 16:17], scalar1=EPS, scalar2=None,
            op0=ALU.max), inc=dv)
        S.op("vector", lambda e: e.reciprocal(stat[0:1, 17:18], stat[0:1, 16:17]),
             inc=dv)
        S.op("vector", lambda e: e.tensor_scalar(
            out=stat[0:1, 17:18], in0=stat[0:1, 17:18], scalar1=127.0, scalar2=None,
            op0=ALU.mult), inc=dv)
        S.op("vector", lambda e: e.tensor_scalar(
            out=stat[0:1, 16:17], in0=stat[0:1, 16:17], scalar1=1.0 / 127.0,
            scalar2=None, op0=ALU.mult), inc=dv)
        for hf in range(2):
            S.op("vector", lambda e, hf=hf: e.tensor_scalar(
                out=h_q[0:1, :], in0=h_y[0:1, hf * 512:(hf + 1) * 512],
                scalar1=stat[0:1, 17:18], scalar2=None, op0=ALU.mult), inc=dv)
            S.op("vector", lambda e, hf=hf: e.tensor_scalar(
                out=h_xq[0:1, hf * 512:(hf + 1) * 512], in0=h_q[0:1, :],
                scalar1=stat[0:1, 16:17], scalar2=None, op0=ALU.mult), inc=dv)
        ndv = S.n(dv)
        nprev_ac = S.n(ac)
        for b4 in range(2):
            wl = [(dv, ndv), (ac, nprev_ac)]
            for s4 in range(4):
                c = b4 * 4 + s4
                S.op("tensor", lambda e, s4=s4, c=c: e.transpose(
                    B[7][:, s4 * 128:s4 * 128 + 1],
                    h_xq[0:1, c * 128:(c + 1) * 128], c_idb[0:1, 0:1]),
                    wait=wl if s4 == 0 else (), inc=pe)
            npe = S.n(pe)
            for s4 in range(4):
                c = b4 * 4 + s4
                S.op("scalar", lambda e, s4=s4, c=c: e.activation(
                    out=env["hal_xqdT"][:, c * 2 + which: c * 2 + which + 1],
                    in_=B[7][:, s4 * 128:s4 * 128 + 1], func=AF.Copy),
                    wait=((pe, npe),) if s4 == 0 else (), inc=ac)
            nprev_ac = S.n(ac)


def _prep_inputs(inp):
    bf = ml_dtypes.bfloat16
    x = np.ascontiguousarray(np.asarray(inp["L0_attn_x"], np.float32))
    f32 = lambda a: np.ascontiguousarray(np.asarray(a, np.float32))
    col = lambda a: np.ascontiguousarray(f32(a).reshape(128, 1))
    aff = np.concatenate([f32(inp[k]).reshape(128, 1) for k in (
        "L0_attn_g_qq", "L0_attn_b_qq", "L0_attn_g_qk", "L0_attn_b_qk",
        "L0_attn_g_lq", "L0_attn_b_lq", "L0_attn_g_lk", "L0_attn_b_lk")], axis=1)
    cw = f32(inp["L0_ffn_conv_w"])
    cwc = np.concatenate(
        [np.ascontiguousarray(cw[:, 0, k].reshape(32, 128).T) for k in range(3)]
        + [np.ascontiguousarray(f32(inp["L0_ffn_conv_b"]).reshape(32, 128).T)], axis=1)
    m = np.arange(128)[:, None]
    n = np.arange(GRP)[None, :]
    mask01 = np.concatenate([(n >= m).astype(bf), (n >= m + 128).astype(bf)], axis=1)
    shared = {
        "wv": f32(inp["L0_attn_Wv"]), "wg": f32(inp["L0_attn_Wg"]),
        "wi": f32(inp["L0_attn_Win"]), "wo": f32(inp["L0_attn_Wout"]),
        "bvr": f32(inp["L0_attn_bv"]).reshape(1, E),
        "bgr": f32(inp["L0_attn_bg"]).reshape(1, E),
        "bgc": np.ascontiguousarray(f32(inp["L0_attn_bg"]).reshape(16, 128).T),
        "binc": col(inp["L0_attn_bin_"]),
        "boutr": f32(inp["L0_attn_bout"]).reshape(1, D),
        "affc": np.ascontiguousarray(aff),
        "w1t": np.ascontiguousarray(f32(inp["L0_ffn_w1"]).T),
        "w2t": np.ascontiguousarray(f32(inp["L0_ffn_w2"]).T),
        "cwc": np.ascontiguousarray(cwc),
        "identf": np.eye(128, dtype=np.float32),
        "identb": np.eye(128, dtype=bf),
        "onesb": np.ones((1, 128), dtype=bf),
        "onesf": np.ones((1, 128), dtype=np.float32),
        "onesm": np.ones((128, 256), dtype=np.float32),
        "mask01": np.ascontiguousarray(mask01),
    }
    in_maps = []
    for core in range(8):
        b, half = divmod(core, 2)
        t0 = half * NMAIN
        hL = x[b, t0 - 1] if t0 > 0 else np.zeros(D, np.float32)
        hR = x[b, t0 + NMAIN] if t0 + NMAIN < 8192 else np.zeros(D, np.float32)
        xhv = np.stack([hL, hR]).astype(np.float32)
        xhT = np.ascontiguousarray(
            xhv.T.reshape(8, 128, 2).transpose(1, 0, 2).reshape(128, 16))
        im = dict(shared)
        im.update({
            "xm": np.ascontiguousarray(x[b, t0:t0 + NMAIN]),
            "xp": np.ascontiguousarray(x[b, 0:NPRE]),
            "xh": np.ascontiguousarray(xhv), "xhT": xhT,
            "pmask": np.full((128, 1), 1.0 if half == 1 else 0.0, np.float32),
            "hmv": np.array([[1.0 if half == 1 else 0.0],
                             [1.0 if half == 0 else 0.0]], np.float32),
        })
        in_maps.append(im)
    return in_maps


_NC_CACHE = {}


def kernel(**inputs):
    if "nc" not in _NC_CACHE:
        _NC_CACHE["nc"] = build_nc()
    nc = _NC_CACHE["nc"]
    in_maps = _prep_inputs(inputs)
    res = run_bass_kernel_spmd(nc, in_maps, core_ids=list(range(8)))
    outv = np.empty((4, 8192, D), np.float32)
    for core in range(8):
        b, half = divmod(core, 2)
        outv[b, half * NMAIN:(half + 1) * NMAIN] = res.results[core]["out"]
    return outv
